# version 03
# baseline (speedup 1.0000x reference)
import numpy as np
from contextlib import ExitStack

import concourse.bass as bass
import concourse.mybir as mybir
from concourse.bass_utils import run_bass_kernel_spmd

F32 = mybir.dt.float32
BF16 = mybir.dt.bfloat16
I32 = mybir.dt.int32
AF = mybir.ActivationFunctionType
ALU = mybir.AluOpType
AX = mybir.AxisListType

T = 4096
D = 1024
NT = T // 128
ND = D // 128

LN_EPS = 1e-5


class Ctr:
    def __init__(self):
        self.v = 0

    def add(self, n):
        self.v += n
        return self.v


class Prog:

    ENGS = ("tensor", "vector", "scalar", "sync", "gpsimd")

    def __init__(self, nc, ctx):
        self.nc = nc
        self.ctx = ctx
        self.streams = {e: [] for e in self.ENGS}
        self.sems = {}
        self.ctrs = {}

    def sem(self, name):
        if name not in self.sems:
            self.sems[name] = self.ctx.enter_context(self.nc.semaphore(name))
            self.ctrs[name] = Ctr()
        return self.sems[name]

    def emit(self, eng, fn):
        self.streams[eng].append(fn)

    def wait(self, eng, sem_name, val=None):
        s = self.sem(sem_name)
        v = self.ctrs[sem_name].v if val is None else val
        if v > 0:
            self.emit(eng, lambda e, s=s, v=v: e.wait_ge(s, v))
        return v

    def inc_after(self, eng, sem_name, n, fn):
        s = self.sem(sem_name)
        self.ctrs[sem_name].add(n)
        self.emit(eng, lambda e, s=s, n=n, fn=fn: fn(e).then_inc(s, n))

    def now(self, sem_name):
        self.sem(sem_name)
        return self.ctrs[sem_name].v

    def run(self):
        block = self.ctx.enter_context(self.nc.Block())
        for eng in self.ENGS:
            fns = self.streams[eng]
            getattr(block, eng)(lambda e, fns=fns: [f(e) for f in fns])

    def t(self, fn):
        self.emit("tensor", fn)

    def v(self, fn):
        self.emit("vector", fn)

    def s(self, fn):
        self.emit("scalar", fn)


def emit_ln_tile(p, s32_ap, out_ap, stats, sq_scr=None):
    ssq, ssum, mean, var, rstd = (
        stats["ssq"], stats["ssum"], stats["mean"], stats["var"], stats["rstd"])
    if sq_scr is None:
        sq_scr = stats["sq_scr"][:, :]
    p.wait("scalar", "ve")
    p.inc_after("scalar", "se", 1,
                lambda e: e.activation(sq_scr, s32_ap, AF.Square, accum_out=ssq[:, 0:1]))
    p.wait("vector", "ve")
    p.inc_after("vector", "ve", 1,
                lambda e: e.reduce_sum(ssum[:, 0:1], s32_ap, axis=AX.X))
    p.wait("vector", "ve")
    p.wait("vector", "se")
    p.inc_after("vector", "ve", 1,
                lambda e: e.tensor_scalar_mul(mean[:, 0:1], ssum[:, 0:1], 1.0 / D))
    p.wait("vector", "ve")
    p.inc_after("vector", "ve", 1,
                lambda e: e.tensor_scalar_mul(var[:, 0:1], ssq[:, 0:1], 1.0 / D))
    p.wait("vector", "ve")
    p.inc_after("vector", "ve", 1,
                lambda e: e.tensor_mul(ssum[:, 0:1], mean[:, 0:1], mean[:, 0:1]))
    p.wait("vector", "ve")
    p.inc_after("vector", "ve", 1,
                lambda e: e.tensor_sub(var[:, 0:1], var[:, 0:1], ssum[:, 0:1]))
    p.wait("scalar", "ve")
    p.inc_after("scalar", "se", 1,
                lambda e: e.activation(rstd[:, 0:1], var[:, 0:1], AF.Sqrt,
                                       bias=stats["eps"][:, 0:1]))
    p.wait("vector", "se")
    p.inc_after("vector", "ve", 1,
                lambda e: e.reciprocal(rstd[:, 0:1], rstd[:, 0:1]))
    p.wait("vector", "ve")
    p.inc_after("vector", "ve", 1,
                lambda e: e.tensor_scalar(out_ap, s32_ap, mean[:, 0:1], rstd[:, 0:1],
                                          ALU.subtract, ALU.mult))


def alloc_stats(sbt, tag=""):
    return dict(
        ssq=sbt("ssq" + tag, [128, 1], F32),
        ssum=sbt("ssum" + tag, [128, 1], F32),
        mean=sbt("mean" + tag, [128, 1], F32),
        var=sbt("var" + tag, [128, 1], F32),
        rstd=sbt("rstd" + tag, [128, 1], F32),
        eps=sbt("eps" + tag, [128, 1], F32),
    )



def build_ffn_int8(debug=False, H=3072, T_=None):
    T = T_ or globals()["T"]
    NH = H // 128
    TS = 256
    NS = T // TS
    TPS = TS // 128

    nc = bass.Bass("TRN2", target_bir_lowering=False)
    x = nc.dram_tensor("x", [T, D], F32, kind="ExternalInput").ap()
    wgT = nc.dram_tensor("wgT", [D, H], I32, kind="ExternalInput").ap()
    wuT = nc.dram_tensor("wuT", [D, H], I32, kind="ExternalInput").ap()
    wdT = nc.dram_tensor("wdT", [H, D], I32, kind="ExternalInput").ap()
    ident = nc.dram_tensor("c_ident", [128, 128], F32, kind="ExternalInput").ap()
    yo = nc.dram_tensor("y", [T, D], F32, kind="ExternalOutput").ap()
    dbg = nc.dram_tensor("dbg", [128, 3584], F32, kind="ExternalOutput").ap() if debug else None

    with ExitStack() as ctx:
        sbt = lambda *a: ctx.enter_context(nc.sbuf_tensor(*a))
        pst = lambda *a: ctx.enter_context(nc.psum_tensor(*a))
        p = Prog(nc, ctx)

        IDN = sbt("IDN", [128, 128], BF16)
        WG = sbt("WG", [128, NH * ND * 128], BF16)
        WU = sbt("WU", [128, NH * ND * 128], BF16)
        WD = sbt("WD", [128, NH * D], BF16)
        WSTG = [sbt("WSTG0", [128, 3072], I32)]
        XSB = sbt("XSB", [128, TPS * D], BF16)
        XS32 = sbt("XS32", [128, TPS * D], F32)
        XF = [sbt(f"XF{i}", [128, ND * TS], BF16) for i in range(2)]
        GS = sbt("GS", [128, TS], BF16)
        HS = sbt("HS", [128, TS], BF16)
        S32 = sbt("S32", [128, TPS * D], F32)
        OUT = sbt("OUT", [128, TPS * D], F32)
        stats = alloc_stats(sbt)

        DBG = sbt("DBG", [128, 3584], F32) if debug else None
        ps_tr = pst("ps_tr", [128, ND * 128], BF16)
        ps_g = pst("ps_g", [128, TS], F32)
        ps_u = pst("ps_u", [128, TS], F32)
        ps_d = [pst(f"ps_d{i}", [128, D], F32) for i in range(TPS)]

        p.inc_after("vector", "ve", 1, lambda e: e.memset(stats["eps"][:, 0:1], LN_EPS))
        p.inc_after("gpsimd", "gld", 16, lambda e: e.dma_start(out=IDN[:, :], in_=ident))

        def load_dequant(wT, dest, nrow_chunks, ncols, kind):
            src = wT.rearrange("(rc p) f -> rc p f", p=128)
            for rc in range(nrow_chunks):
                stg = WSTG[0]
                p.wait("sync", "se")
                p.inc_after("sync", "wld", 16,
                            lambda e, rc=rc, stg=stg, ncols=ncols: e.dma_start(
                                out=stg[:, 0:ncols], in_=src[rc]))
                p.wait("scalar", "wld")
                if kind == "d":
                    out_ap = dest[:, rc * ncols:(rc + 1) * ncols]
                else:
                    out_ap = dest[:, :].rearrange(
                        "p (hc nd f) -> p hc nd f", nd=ND, f=128)[:, :, rc, :]
                p.inc_after("scalar", "se", 1,
                            lambda e, stg=stg, out_ap=out_ap, ncols=ncols:
                            e.activation(out_ap, stg[:, 0:ncols], AF.Copy, scale=1.0 / 127.0))

        load_dequant(wgT, WG, ND, H, "g")
        load_dequant(wuT, WU, ND, H, "u")
        load_dequant(wdT, WD, NH, D, "d")
        se_w = p.now("se")

        x_t = x.rearrange("(nt p) d -> nt p d", p=128)
        yo_t = yo.rearrange("(nt p) d -> nt p d", p=128)

        p.wait("tensor", "se", se_w)

        for s in range(NS):
            xf = XF[s % 2]
            if s > 0:
                p.wait("gpsimd", "te")
                p.wait("sync", "ve")
            for tt in range(TPS):
                nt = s * TPS + tt
                p.inc_after("gpsimd", "gld", 16,
                            lambda e, nt=nt, tt=tt: e.dma_start(
                                out=XSB[:, tt * D:(tt + 1) * D], in_=x_t[nt]))
                p.inc_after("sync", "ld", 16,
                            lambda e, nt=nt, tt=tt: e.dma_start(
                                out=XS32[:, tt * D:(tt + 1) * D], in_=x_t[nt]))
            ld_slab = p.now("ld")
            gld_slab = p.now("gld")

            p.wait("tensor", "ld", ld_slab)
            p.wait("tensor", "gld", gld_slab)
            xf_view = xf[:, :].rearrange("p (nd ts) -> p nd ts", ts=TS)
            for tt in range(TPS):
                p.wait("tensor", "ve")
                for dc in range(ND):
                    fn = lambda e, tt=tt, dc=dc: e.transpose(
                        ps_tr[:, dc * 128:(dc + 1) * 128],
                        XSB[:, tt * D + dc * 128: tt * D + (dc + 1) * 128], IDN[:, :])
                    if dc == ND - 1:
                        p.inc_after("tensor", "te", 1, fn)
                    else:
                        p.t(fn)
                p.wait("vector", "te")
                p.inc_after("vector", "ve", 1,
                            lambda e, xf_view=xf_view, tt=tt: e.tensor_copy(
                                xf_view[:, :, tt * 128:(tt + 1) * 128],
                                ps_tr[:, :].rearrange("p (nd f) -> p nd f", f=128)))
            ve_xf = p.now("ve")

            p.wait("tensor", "ve", ve_xf)
            for hc in range(NH):
                p.wait("tensor", "se")
                p.wait("tensor", "ve")
                for dc in range(ND):
                    fn = lambda e, hc=hc, dc=dc, xf=xf: e.matmul(
                        ps_g[:, :], WG[:, (hc * ND + dc) * 128:(hc * ND + dc + 1) * 128],
                        xf[:, dc * TS:(dc + 1) * TS],
                        start=(dc == 0), stop=(dc == ND - 1))
                    if dc == ND - 1:
                        p.inc_after("tensor", "te", 1, fn)
                    else:
                        p.t(fn)
                for dc in range(ND):
                    fn = lambda e, hc=hc, dc=dc, xf=xf: e.matmul(
                        ps_u[:, :], WU[:, (hc * ND + dc) * 128:(hc * ND + dc + 1) * 128],
                        xf[:, dc * TS:(dc + 1) * TS],
                        start=(dc == 0), stop=(dc == ND - 1))
                    if dc == ND - 1:
                        p.inc_after("tensor", "te", 1, fn)
                    else:
                        p.t(fn)
                te_gu = p.now("te")
                p.wait("scalar", "te", te_gu - 1)
                p.wait("scalar", "ve")
                p.inc_after("scalar", "se", 1,
                            lambda e: e.activation(GS[:, :], ps_g[:, :], AF.Silu))
                p.wait("vector", "te", te_gu)
                p.wait("vector", "se")
                p.inc_after("vector", "ve", 1,
                            lambda e: e.tensor_tensor(HS[:, :], GS[:, :], ps_u[:, :], ALU.mult))
                p.wait("tensor", "ve")
                for i, (tt, half) in enumerate([(a, b) for a in range(TPS) for b in range(2)]):
                    fn = lambda e, hc=hc, tt=tt, half=half: e.matmul(
                        ps_d[tt][:, half * 512:(half + 1) * 512],
                        HS[:, tt * 128:(tt + 1) * 128],
                        WD[:, hc * D + half * 512: hc * D + half * 512 + 512],
                        start=(hc == 0), stop=(hc == NH - 1))
                    if i == 2 * TPS - 1:
                        p.inc_after("tensor", "te", 1, fn)
                    else:
                        p.t(fn)
            te_down = p.now("te")

            if debug and s == 0:
                p.wait("vector", "te")
                p.inc_after("vector", "ve", 1,
                            lambda e: e.tensor_copy(DBG[:, 0:2048], XF[0][:, 0:2048]))
                p.inc_after("vector", "ve", 1,
                            lambda e: e.tensor_copy(DBG[:, 2048:2304], GS[:, :]))
                p.inc_after("vector", "ve", 1,
                            lambda e: e.tensor_copy(DBG[:, 2304:2560], HS[:, :]))
                p.inc_after("vector", "ve", 1,
                            lambda e: e.tensor_copy(DBG[:, 2560:2816], ps_g[:, :]))
                p.inc_after("vector", "ve", 1,
                            lambda e: e.tensor_copy(DBG[:, 2816:3072], ps_u[:, :]))
                p.wait("sync", "ve")
                p.inc_after("sync", "st", 16, lambda e: e.dma_start(out=dbg, in_=DBG[:, 0:3584]))
            for tt in range(TPS):
                nt = s * TPS + tt
                p.wait("vector", "te", te_down)
                p.wait("vector", "ld", ld_slab)
                p.inc_after("vector", "ve", 1,
                            lambda e, tt=tt: e.tensor_tensor(
                                S32[:, tt * D:(tt + 1) * D], XS32[:, tt * D:(tt + 1) * D],
                                ps_d[tt][:, :], ALU.add))
                p.wait("scalar", "st")
                p.wait("vector", "st")
                emit_ln_tile(p, S32[:, tt * D:(tt + 1) * D], OUT[:, tt * D:(tt + 1) * D], stats,
                             sq_scr=OUT[:, tt * D:(tt + 1) * D])
                ve_out = p.now("ve")
                p.wait("sync", "ve", ve_out)
                p.inc_after("sync", "st", 16,
                            lambda e, nt=nt, tt=tt: e.dma_start(
                                out=yo_t[nt], in_=OUT[:, tt * D:(tt + 1) * D]))

        p.wait("sync", "st")
        p.run()
    return nc




def build_attn(kind, T_=None, NHEADS=16, HG=4):
    Tl = T_ or T
    NTl = Tl // 128
    HD = 64
    NG = NHEADS // HG
    GR = HG * HD
    GC = GR // 128
    NQS = Tl // 512
    F = 3 * D
    lora = kind == "l1"

    nc = bass.Bass("TRN2", target_bir_lowering=False)
    x = nc.dram_tensor("x", [Tl, D], F32, kind="ExternalInput").ap()
    wqkv = nc.dram_tensor("wqkv", [D, F], F32, kind="ExternalInput").ap()
    bqkv = nc.dram_tensor("bqkv", [1, F], F32, kind="ExternalInput").ap()
    bqkvc = nc.dram_tensor("bqkvc", [128, F // 128], F32, kind="ExternalInput").ap()
    wproj = nc.dram_tensor("wproj", [D, D], F32, kind="ExternalInput").ap()
    bproj = nc.dram_tensor("bproj", [1, D], F32, kind="ExternalInput").ap()
    if lora:
        aqkv = nc.dram_tensor("aqkv", [D, 8], F32, kind="ExternalInput").ap()
        bl_qkv = nc.dram_tensor("bl_qkv", [8, F], F32, kind="ExternalInput").ap()
        aproj = nc.dram_tensor("aproj", [D, 8], F32, kind="ExternalInput").ap()
        bl_proj = nc.dram_tensor("bl_proj", [8, D], F32, kind="ExternalInput").ap()
    else:
        ccos = nc.dram_tensor("c_cosc", [128, 8], F32, kind="ExternalInput").ap()
        csin = nc.dram_tensor("c_sinc", [128, 8], F32, kind="ExternalInput").ap()
        cperm = nc.dram_tensor("c_perm", [128, 128], F32, kind="ExternalInput").ap()
        cmask = nc.dram_tensor("c_mask", [128, 128], F32, kind="ExternalInput").ap()
    ident = nc.dram_tensor("c_ident", [128, 128], F32, kind="ExternalInput").ap()
    yo = nc.dram_tensor("y", [Tl, D], F32, kind="ExternalOutput").ap()

    x_t = x.rearrange("(nt p) d -> nt p d", p=128)
    yo_t = yo.rearrange("(nt p) d -> nt p d", p=128)

    with ExitStack() as ctx:
        sbt = lambda *a: ctx.enter_context(nc.sbuf_tensor(*a))
        pst = lambda *a: ctx.enter_context(nc.psum_tensor(*a))
        p = Prog(nc, ctx)

        IDN = sbt("IDN", [128, 128], BF16)
        XF = sbt("XFa", [128, ND * Tl], BF16)
        OF = sbt("OFa", [128, ND * Tl], BF16)
        WQG = sbt("WQG", [128, ND * 2 * GR], BF16)
        KGW = sbt("KGW", [128, max(GC * Tl, ND * D)], BF16)
        KG = KGW
        WPROJ = KGW
        VT = sbt("VT", [128, NTl * HG * 65], BF16)
        QF = sbt("QF", [128, GC * 512], BF16)
        BQC = sbt("BQC", [128, F // 128], F32)
        BVB = sbt("BVB", [128, GR], F32)
        BVR = sbt("BVR", [1, GR], F32)
        PBJ = sbt("PBJ", [128, 1024], BF16)
        PB = [PBJ[:, 0:512], PBJ[:, 512:1024]]
        OS = sbt("OS", [128, 512], BF16)
        OUT = sbt("OUTa", [128, D], F32)
        ONE1 = sbt("ONE1", [1, 128], F32)
        SHARE = sbt("SHARE", [128, 4 * D], BF16)
        XSB = SHARE[:, 0:2 * D]
        S32 = SHARE[:, :].bitcast(F32)
        BPB = sbt("BPB", [128, D], BF16)
        BPR = sbt("BPR", [1, D], F32)
        DEN = BPR[0:1, 0:512]
        RECIP = BPR[0:1, 512:1024]
        stats = alloc_stats(sbt)
        if lora:
            AQ = sbt("AQ", [128, ND * 8], BF16)
            BLG = sbt("BLG", [8, 3 * GR], BF16)
            APJ = sbt("APJ", [128, ND * 8], BF16)
            BLP = sbt("BLP", [8, D], BF16)
            MIDB = sbt("MIDB", [8, Tl], BF16)
        else:
            COS = sbt("COSC", [128, 8], F32)
            SIN = sbt("SINC", [128, 8], F32)
            PERM = sbt("PERM", [128, 128], BF16)
            MASK = sbt("MASKT", [128, 128], BF16)
            RSC = PBJ[:, :].bitcast(F32)

        ps_tr = pst("ps_tr", [128, ND * 128], BF16)
        ps_a = pst("ps_a", [128, 512], F32)
        ps_st = [pst(f"ps_st{i}", [128, 512], F32) for i in range(2)]
        ps_o = pst("ps_o", [128, 512], F32)
        ps_rb = pst("ps_rb", [128, 512], F32)
        ps_po = pst("ps_po", [128, D], F32)

        def mm_chain(fns, extra=None):
            all_fns = fns + ([extra] if extra is not None else [])
            for i, fn in enumerate(all_fns):
                if i == len(all_fns) - 1:
                    p.inc_after("tensor", "te", 1, fn)
                else:
                    p.t(fn)
            return p.now("te")

        p.inc_after("vector", "ve", 1, lambda e: e.memset(stats["eps"][:, 0:1], LN_EPS))
        p.inc_after("vector", "ve", 1, lambda e: e.memset(ONE1[:, :], 1.0))
        p.inc_after("vector", "ve", 1, lambda e: e.memset(
            VT[:, :].rearrange("q (nt h c) -> q nt h c", h=HG, c=65)[:, :, :, 64:65], 1.0))

        p.inc_after("gpsimd", "gld", 16, lambda e: e.dma_start(out=IDN[:, :], in_=ident))
        p.inc_after("sync", "ld", 16, lambda e: e.dma_start(out=BQC[:, :], in_=bqkvc))
        if lora:
            p.inc_after("gpsimd", "gld", 16, lambda e: e.dma_start(
                out=AQ[:, :].rearrange("p (dc r) -> p dc r", r=8),
                in_=aqkv.rearrange("(dc p) r -> p dc r", p=128)))
            p.inc_after("gpsimd", "gld", 16, lambda e: e.dma_start(
                out=APJ[:, :].rearrange("p (dc r) -> p dc r", r=8),
                in_=aproj.rearrange("(dc p) r -> p dc r", p=128)))
            p.inc_after("gpsimd", "gld", 16, lambda e: e.dma_start(out=BLP[:, :], in_=bl_proj))
        else:
            p.inc_after("sync", "ld", 16, lambda e: e.dma_start(out=COS[:, :], in_=ccos))
            p.inc_after("sync", "ld", 16, lambda e: e.dma_start(out=SIN[:, :], in_=csin))
            p.inc_after("gpsimd", "gld", 16, lambda e: e.dma_start(out=PERM[:, :], in_=cperm))
            p.inc_after("gpsimd", "gld", 16, lambda e: e.dma_start(out=MASK[:, :], in_=cmask))

        te_nt = []
        for nt in range(NTl):
            buf = (nt % 2) * D
            par = f"gx{nt % 2}"
            if nt >= 2:
                p.wait("gpsimd", "te", te_nt[nt - 2])
            p.inc_after("gpsimd", par, 16,
                        lambda e, nt=nt, buf=buf: e.dma_start(out=XSB[:, buf:buf + D], in_=x_t[nt]))
            p.wait("tensor", par)
            if nt == 0:
                p.wait("tensor", "gld")
            p.wait("tensor", "ve")
            fns = [
                (lambda e, nt=nt, dc=dc, buf=buf: e.transpose(
                    ps_tr[:, dc * 128:(dc + 1) * 128],
                    XSB[:, buf + dc * 128: buf + (dc + 1) * 128], IDN[:, :]))
                for dc in range(ND)]
            te_nt.append(mm_chain(fns))
            p.wait("vector", "te")
            xf_view = XF[:, :].rearrange("q (nd t) -> q nd t", t=Tl)
            p.inc_after("vector", "ve", 1,
                        lambda e, nt=nt, xf_view=xf_view: e.tensor_copy(
                            xf_view[:, :, nt * 128:(nt + 1) * 128],
                            ps_tr[:, :].rearrange("q (nd f) -> q nd f", f=128)))
        ve_xf = p.now("ve")
        p.wait("tensor", "ve", ve_xf)

        if lora:
            for tc in range(Tl // 512):
                if tc > 0:
                    p.wait("tensor", "se")
                mm_chain([
                    (lambda e, tc=tc, dc=dc: e.matmul(
                        ps_a[0:8, :], AQ[:, dc * 8:(dc + 1) * 8],
                        XF[:, dc * Tl + tc * 512: dc * Tl + tc * 512 + 512],
                        start=(dc == 0), stop=(dc == ND - 1)))
                    for dc in range(ND)])
                p.wait("scalar", "te")
                p.inc_after("scalar", "se", 1,
                            lambda e, tc=tc: e.activation(
                                MIDB[0:8, tc * 512:(tc + 1) * 512], ps_a[0:8, :],
                                AF.Copy, scale=2.0))
        se_mid = p.now("se")
        p.wait("tensor", "se", se_mid)

        def emit_rope(dst_sl, col):
            tgt, sl = dst_sl
            p.wait("vector", "te")
            p.wait("tensor", "ve")
            p.inc_after("tensor", "te", 1,
                        lambda e, tgt=tgt, sl=sl: e.matmul(ps_rb[:, :], PERM[:, :], tgt[:, sl],
                                                           start=True, stop=True))
            p.wait("vector", "te")
            p.wait("vector", "ld")
            p.inc_after("vector", "ve", 1,
                        lambda e, tgt=tgt, sl=sl, col=col: e.tensor_scalar(
                            RSC[:, :], tgt[:, sl], COS[:, col:col + 1], None, ALU.mult))
            p.wait("vector", "ve")
            p.inc_after("vector", "ve", 1,
                        lambda e, tgt=tgt, sl=sl, col=col: e.scalar_tensor_tensor(
                            tgt[:, sl], ps_rb[:, :], SIN[:, col:col + 1], RSC[:, :],
                            ALU.mult, ALU.add))

        for g in range(NG):
            p.wait("gpsimd", "te")
            for si, base in ((0, D + g * GR), (1, 2 * D + g * GR)):
                p.inc_after("gpsimd", "gw", 16,
                            lambda e, si=si, base=base: e.dma_start(
                                out=WQG[:, :].rearrange("q (nd s c) -> q nd s c", s=2, c=GR)[:, :, si, :],
                                in_=wqkv.rearrange("(dc q) f -> q dc f", q=128)[:, :, base:base + GR]))
            if lora:
                for si, base in enumerate((g * GR, D + g * GR, 2 * D + g * GR)):
                    p.inc_after("gpsimd", "gw", 16,
                                lambda e, si=si, base=base: e.dma_start(
                                    out=BLG[0:8, si * GR:(si + 1) * GR],
                                    in_=bl_qkv[0:8, base:base + GR]))
            p.wait("sync", "ve")
            p.inc_after("sync", "ld", 16,
                        lambda e, g=g: e.dma_start(out=BVR[0:1, :],
                                                   in_=bqkv[0:1, 2 * D + g * GR: 2 * D + g * GR + GR]))
            ld_bv = p.now("ld")
            gld_w = p.now("gw")
            p.wait("tensor", "ld", ld_bv)
            p.wait("tensor", "gw", gld_w)
            p.wait("tensor", "ve")
            mm_chain([lambda e: e.matmul(ps_rb[:, 0:GR], ONE1[0:1, :], BVR[0:1, :],
                                         start=True, stop=True)])
            p.wait("vector", "te")
            p.inc_after("vector", "ve", 1,
                        lambda e: e.tensor_copy(BVB[:, :], ps_rb[:, 0:GR]))
            ve_bvb = p.now("ve")

            for cc in range(GC):
                for tc in range(Tl // 512):
                    p.wait("tensor", "ve")
                    extra = None
                    if lora:
                        extra = (lambda e, cc=cc, tc=tc: e.matmul(
                            ps_a[:, :], BLG[0:8, GR + cc * 128: GR + cc * 128 + 128],
                            MIDB[0:8, tc * 512:(tc + 1) * 512], start=False, stop=True))
                    mm_chain([
                        (lambda e, cc=cc, tc=tc, dc=dc: e.matmul(
                            ps_a[:, :], WQG[:, dc * 2 * GR + cc * 128: dc * 2 * GR + cc * 128 + 128],
                            XF[:, dc * Tl + tc * 512: dc * Tl + tc * 512 + 512],
                            start=(dc == 0), stop=(dc == ND - 1 and not lora)))
                        for dc in range(ND)], extra)
                    p.wait("vector", "te")
                    fglob = D + g * GR + cc * 128
                    p.inc_after("vector", "ve", 1,
                                lambda e, cc=cc, tc=tc, fglob=fglob: e.tensor_scalar_add(
                                    KG[:, cc * Tl + tc * 512: cc * Tl + tc * 512 + 512],
                                    ps_a[:, :], BQC[:, fglob // 128: fglob // 128 + 1]))
            se_k = p.now("ve")
            if not lora:
                p.wait("vector", "ve", se_k)
                for cc in range(GC):
                    for tc in range(Tl // 512):
                        emit_rope((KG, slice(cc * Tl + tc * 512, cc * Tl + tc * 512 + 512)),
                                  g * GC + cc)
                p.wait("tensor", "ve")

            p.wait("vector", "ve", ve_bvb)
            for kt in range(NTl):
                p.wait("tensor", "ve")
                extra = None
                if lora:
                    extra = (lambda e, kt=kt: e.matmul(
                        ps_st[0][:, 0:GR], MIDB[0:8, kt * 128:(kt + 1) * 128],
                        BLG[0:8, 2 * GR: 3 * GR], start=False, stop=True))
                mm_chain([
                    (lambda e, kt=kt, dc=dc: e.matmul(
                        ps_st[0][:, 0:GR], XF[:, dc * Tl + kt * 128: dc * Tl + kt * 128 + 128],
                        WQG[:, dc * 2 * GR + GR: dc * 2 * GR + 2 * GR],
                        start=(dc == 0), stop=(dc == ND - 1 and not lora)))
                    for dc in range(ND)], extra)
                p.wait("vector", "te")
                vt_view = VT[:, :].rearrange("q (nt h c) -> q nt h c", h=HG, c=65)
                p.inc_after("vector", "ve", 1,
                            lambda e, kt=kt, vt_view=vt_view: e.tensor_tensor(
                                vt_view[:, kt, :, 0:64],
                                ps_st[0][:, 0:GR].rearrange("q (h c) -> q h c", c=64),
                                BVB[:, :].rearrange("q (h c) -> q h c", c=64), ALU.add))
            ve_v = p.now("ve")
            p.wait("tensor", "ve", ve_v)

            p.wait("gpsimd", "te")
            p.inc_after("gpsimd", "gw", 16,
                        lambda e, g=g: e.dma_start(
                            out=WQG[:, :].rearrange("q (nd s c) -> q nd s c", s=2, c=GR)[:, :, 0, :],
                            in_=wqkv.rearrange("(dc q) f -> q dc f", q=128)[:, :, g * GR: g * GR + GR]))
            gw_q = p.now("gw")
            p.wait("tensor", "gw", gw_q)

            for qs in range(NQS):
                for cc in range(GC):
                    p.wait("tensor", "ve")
                    extra = None
                    if lora:
                        extra = (lambda e, cc=cc, qs=qs: e.matmul(
                            ps_a[:, :], BLG[0:8, cc * 128: cc * 128 + 128],
                            MIDB[0:8, qs * 512:(qs + 1) * 512], start=False, stop=True))
                    mm_chain([
                        (lambda e, cc=cc, dc=dc, qs=qs: e.matmul(
                            ps_a[:, :], WQG[:, dc * 2 * GR + cc * 128: dc * 2 * GR + cc * 128 + 128],
                            XF[:, dc * Tl + qs * 512: dc * Tl + qs * 512 + 512],
                            start=(dc == 0), stop=(dc == ND - 1 and not lora)))
                        for dc in range(ND)], extra)
                    p.wait("vector", "te")
                    fglob = g * GR + cc * 128
                    p.inc_after("vector", "ve", 1,
                                lambda e, cc=cc, fglob=fglob: e.tensor_scalar_add(
                                    QF[:, cc * 512:(cc + 1) * 512], ps_a[:, :],
                                    BQC[:, fglob // 128: fglob // 128 + 1]))
                ve_q = p.now("ve")
                if not lora:
                    p.wait("vector", "ve", ve_q)
                    for cc in range(GC):
                        emit_rope((QF, slice(cc * 512, cc * 512 + 512)), g * GC + cc)
                p.wait("tensor", "ve")

                for hh in range(HG):
                    cc = hh // 2
                    rb = (hh % 2) * 64
                    kts = list(range(NTl)) if lora else list(range(min(NTl, qs * 4 + 4)))
                    for i, kt in enumerate(kts):
                        if i >= 2:
                            p.wait("tensor", "se")
                        p.inc_after("tensor", "te", 1,
                                    lambda e, kt=kt, cc=cc, rb=rb, i=i: e.matmul(
                                        ps_st[i % 2][:, :],
                                        KG[rb:rb + 64, cc * Tl + kt * 128: cc * Tl + kt * 128 + 128],
                                        QF[rb:rb + 64, cc * 512:(cc + 1) * 512],
                                        start=True, stop=True))
                        te_s = p.now("te")
                        p.wait("scalar", "te", te_s)
                        if (not lora) and kt >= qs * 4:
                            p.wait("scalar", "ve")
                        p.inc_after("scalar", "se", 1,
                                    lambda e, i=i: e.activation(
                                        PB[i % 2][:, :], ps_st[i % 2][:, :], AF.Exp, scale=0.125))
                        se_e = p.now("se")
                        if (not lora) and kt >= qs * 4:
                            mt = kt - qs * 4
                            p.wait("vector", "se", se_e)
                            if mt > 0:
                                p.inc_after("vector", "ve", 1,
                                            lambda e, i=i, mt=mt: e.memset(
                                                PB[i % 2][:, 0:128 * mt], 0.0))
                            p.inc_after("vector", "ve", 1,
                                        lambda e, i=i, mt=mt: e.tensor_tensor(
                                            PB[i % 2][:, 128 * mt:128 * mt + 128],
                                            PB[i % 2][:, 128 * mt:128 * mt + 128],
                                            MASK[:, :], ALU.mult))
                            p.wait("tensor", "ve")
                        else:
                            p.wait("tensor", "se", se_e)
                        fn = (lambda e, kt=kt, hh=hh, i=i, last=(i == len(kts) - 1): e.matmul(
                            ps_o[0:65, :], VT[:, kt * HG * 65 + hh * 65: kt * HG * 65 + hh * 65 + 65],
                            PB[i % 2][:, :], start=(i == 0), stop=last))
                        if i == len(kts) - 1:
                            p.inc_after("tensor", "te", 1, fn)
                        else:
                            p.t(fn)
                    te_pv = p.now("te")
                    p.wait("vector", "te", te_pv)
                    p.inc_after("vector", "ve", 1,
                                lambda e: e.tensor_copy(DEN, ps_o[64:65, :]))
                    p.wait("vector", "ve")
                    p.inc_after("vector", "ve", 1,
                                lambda e: e.reciprocal(RECIP, DEN))
                    ve_r = p.now("ve")
                    p.wait("tensor", "ve", ve_r)
                    p.inc_after("tensor", "te", 1,
                                lambda e: e.matmul(ps_rb[0:64, :], ONE1[0:1, 0:64], RECIP,
                                                   start=True, stop=True))
                    te_rb = p.now("te")
                    p.wait("scalar", "te", te_rb)
                    p.inc_after("scalar", "se", 1,
                                lambda e: e.activation(OS[0:64, :], ps_o[0:64, :], AF.Copy))
                    p.wait("vector", "se")
                    p.wait("vector", "te", te_rb)
                    grow = g * GR + hh * 64
                    ocn, orb = grow // 128, grow % 128
                    p.inc_after("vector", "ve", 1,
                                lambda e, ocn=ocn, orb=orb, qs=qs: e.tensor_tensor(
                                    OF[orb:orb + 64, ocn * Tl + qs * 512: ocn * Tl + qs * 512 + 512],
                                    OS[0:64, :], ps_rb[0:64, :], ALU.mult))
                    p.wait("tensor", "se")
                    p.wait("tensor", "ve")
        ve_of = p.now("ve")
        p.wait("tensor", "ve", ve_of)

        if lora:
            for tc in range(Tl // 512):
                p.wait("tensor", "se")
                mm_chain([
                    (lambda e, tc=tc, dc=dc: e.matmul(
                        ps_a[0:8, :], APJ[:, dc * 8:(dc + 1) * 8],
                        OF[:, dc * Tl + tc * 512: dc * Tl + tc * 512 + 512],
                        start=(dc == 0), stop=(dc == ND - 1)))
                    for dc in range(ND)])
                p.wait("scalar", "te")
                p.inc_after("scalar", "se", 1,
                            lambda e, tc=tc: e.activation(
                                MIDB[0:8, tc * 512:(tc + 1) * 512], ps_a[0:8, :],
                                AF.Copy, scale=2.0))
        se_mid2 = p.now("se")

        p.wait("sync", "ve")
        p.inc_after("sync", "ld", 16, lambda e: e.dma_start(out=BPR[0:1, :], in_=bproj))
        p.wait("gpsimd", "te")
        p.inc_after("gpsimd", "gwp", 16,
                    lambda e: e.dma_start(out=WPROJ[:, 0:ND * D].rearrange("q (dc f) -> q dc f", f=D),
                                          in_=wproj.rearrange("(dc q) f -> q dc f", q=128)))
        gld_wp = p.now("gwp")
        p.wait("tensor", "gwp", gld_wp)
        p.wait("tensor", "ld")
        for half in range(2):
            p.wait("tensor", "ve")
            mm_chain([lambda e, half=half: e.matmul(
                ps_rb[:, :], ONE1[0:1, :], BPR[0:1, half * 512:(half + 1) * 512],
                start=True, stop=True)])
            p.wait("vector", "te")
            p.inc_after("vector", "ve", 1,
                        lambda e, half=half: e.tensor_copy(BPB[:, half * 512:(half + 1) * 512],
                                                           ps_rb[:, :]))
        ve_bpb = p.now("ve")

        p.wait("tensor", "se", se_mid2)
        p.wait("tensor", "ve", ve_bpb)
        for nt in range(NTl):
            buf = (nt % 2) * D
            p.wait("sync", "ve")
            p.inc_after("sync", "ld", 16,
                        lambda e, nt=nt, buf=buf: e.dma_start(out=S32[:, buf:buf + D], in_=x_t[nt]))
            ld_nt = p.now("ld")
            if nt >= 1:
                p.wait("tensor", "ve")
            for half in range(2):
                extra = None
                if lora:
                    extra = (lambda e, nt=nt, half=half: e.matmul(
                        ps_po[:, half * 512:(half + 1) * 512],
                        MIDB[0:8, nt * 128:(nt + 1) * 128],
                        BLP[0:8, half * 512: half * 512 + 512], start=False, stop=True))
                mm_chain([
                    (lambda e, nt=nt, half=half, oc=oc: e.matmul(
                        ps_po[:, half * 512:(half + 1) * 512],
                        OF[:, oc * Tl + nt * 128: oc * Tl + nt * 128 + 128],
                        WPROJ[:, oc * D + half * 512: oc * D + half * 512 + 512],
                        start=(oc == 0), stop=(oc == ND - 1 and not lora)))
                    for oc in range(ND)], extra)
            te_po = p.now("te")
            p.wait("vector", "te", te_po)
            p.wait("vector", "ld", ld_nt)
            p.inc_after("vector", "ve", 1,
                        lambda e, buf=buf: e.tensor_tensor(S32[:, buf:buf + D], S32[:, buf:buf + D],
                                                           ps_po[:, :], ALU.add))
            p.wait("vector", "ve")
            p.inc_after("vector", "ve", 1,
                        lambda e, buf=buf: e.tensor_tensor(S32[:, buf:buf + D], S32[:, buf:buf + D],
                                                           BPB[:, :], ALU.add))
            p.wait("scalar", "st")
            p.wait("vector", "st")
            emit_ln_tile(p, S32[:, buf:buf + D], OUT[:, 0:D], stats,
                         sq_scr=OUT[:, 0:D])
            ve_out = p.now("ve")
            p.wait("sync", "ve", ve_out)
            p.inc_after("sync", "st", 16,
                        lambda e, nt=nt: e.dma_start(out=yo_t[nt], in_=OUT[:, 0:D]))

        p.wait("sync", "st")
        p.run()
    return nc



def build_mamba(T_=None):
    Tl = T_ or T
    DI = 2048
    NDI = DI // 128
    NST = 16
    TS = 256
    NS = Tl // TS
    TPS = TS // 128

    nc = bass.Bass("TRN2", target_bir_lowering=False)
    x = nc.dram_tensor("x", [Tl, D], F32, kind="ExternalInput").ap()
    lnw = nc.dram_tensor("lnw", [1, D], F32, kind="ExternalInput").ap()
    lnb = nc.dram_tensor("lnb", [1, D], F32, kind="ExternalInput").ap()
    wipT = nc.dram_tensor("wipT", [D, 2 * DI], F32, kind="ExternalInput").ap()
    convw = nc.dram_tensor("convw", [128, NDI * 4], F32, kind="ExternalInput").ap()
    convb = nc.dram_tensor("convb", [128, NDI], F32, kind="ExternalInput").ap()
    wxpT = nc.dram_tensor("wxpT", [DI, 96], F32, kind="ExternalInput").ap()
    wdtT = nc.dram_tensor("wdtT", [64, DI], F32, kind="ExternalInput").ap()
    dtb = nc.dram_tensor("dtb", [128, NDI], F32, kind="ExternalInput").ap()
    alog = nc.dram_tensor("alog", [128, NDI * NST], F32, kind="ExternalInput").ap()
    dmat = nc.dram_tensor("dmat", [128, NDI], F32, kind="ExternalInput").ap()
    wopT = nc.dram_tensor("wopT", [DI, D], F32, kind="ExternalInput").ap()
    ehb = nc.dram_tensor("c_ehb", [32, NST * 128], F32, kind="ExternalInput").ap()
    ehc = nc.dram_tensor("c_ehc", [32, NST * 128], F32, kind="ExternalInput").ap()
    ident = nc.dram_tensor("c_ident", [128, 128], F32, kind="ExternalInput").ap()
    yo = nc.dram_tensor("y", [Tl, D], F32, kind="ExternalOutput").ap()

    x_t = x.rearrange("(nt p) d -> nt p d", p=128)
    yo_t = yo.rearrange("(nt p) d -> nt p d", p=128)

    with ExitStack() as ctx:
        sbt = lambda *a: ctx.enter_context(nc.sbuf_tensor(*a))
        pst = lambda *a: ctx.enter_context(nc.psum_tensor(*a))
        p = Prog(nc, ctx)

        IDN = sbt("IDN", [128, 128], BF16)
        WIP = sbt("WIP", [128, ND * 2 * DI], BF16)
        WOP = sbt("WOP", [128, NDI * D], BF16)
        WXP = sbt("WXP", [128, NDI * 96], BF16)
        WDT = sbt("WDT", [64, NDI * 128], BF16)
        CW = sbt("CW", [128, NDI * 4], F32)
        CB = sbt("CB", [128, NDI], F32)
        DTB = sbt("DTB", [128, NDI], F32)
        AC = sbt("AC", [128, NDI * NST], F32)
        DM = sbt("DM", [128, NDI], F32)
        EHB = sbt("EHB", [32, NST * 128], BF16)
        EHC = sbt("EHC", [32, NST * 128], BF16)
        XS32 = sbt("XS32m", [128, TPS * D], F32)
        XLNF = sbt("XLNF", [128, ND * TS], BF16)
        XCB = sbt("XCB", [128, NDI * (TS + 3)], BF16)
        ZS = sbt("ZSm", [128, NDI * TS], BF16)
        XCONV = sbt("XCONV", [128, NDI * TS], BF16)
        DELTA = sbt("DELTA", [128, NDI * TS], BF16)
        DU = sbt("DU", [128, NDI * TS], BF16)
        XLNT = DU[:, 0:TPS * D]
        BB = sbt("BBm", [128, NST * TS], BF16)
        CCB = sbt("CCm", [128, NST * TS], BF16)
        DT_ = sbt("DTm", [64, TS], BF16)
        BCB = sbt("BCB", [32, TS], BF16)
        DA = [sbt(f"DAm{i}", [128, TS], F32) for i in range(2)]
        DBu = [sbt(f"DBm{i}", [128, TS], BF16) for i in range(2)]
        HS = sbt("HSm", [128, TS], F32)
        YA = sbt("YAm", [128, TS], F32)
        SCR1 = sbt("SCR1m", [128, TS], F32)
        TS3 = SCR1
        CV = SCR1
        AB = SCR1
        YF = sbt("YFm", [128, NDI * TS], BF16)
        HC = sbt("HCm", [128, NDI * NST], F32)
        EL = sbt("ELm", [128, TS], F32)
        D0 = sbt("D0m", [128, TS], F32)
        WLNB = sbt("WLNB", [128, D], BF16)
        BLNB = sbt("BLNB", [128, D], BF16)
        ONE1 = sbt("ONE1m", [1, 128], F32)
        OUT = sbt("OUTm", [128, D], F32)
        LNROW = OUT[:, 0:512].bitcast(F32)
        stats = alloc_stats(sbt)

        ps_tr = pst("ps_tr", [128, ND * 128], BF16)
        ps_ip = pst("ps_ip", [128, TS], F32)
        ps_xp = pst("ps_xp", [128, TS], F32)
        ps_dt = pst("ps_dt", [128, TS], F32)
        ps_bc = pst("ps_bc", [128, TS], F32)
        ps_o = pst("ps_o", [128, D], F32)

        p.inc_after("vector", "ve", 1, lambda e: e.memset(stats["eps"][:, 0:1], LN_EPS))
        p.inc_after("vector", "ve", 1, lambda e: e.memset(ONE1[:, :], 1.0))
        p.inc_after("vector", "ve", 1, lambda e: e.memset(HC[:, :], 0.0))
        p.inc_after("vector", "ve", 1, lambda e: e.memset(
            XCB[:, :].rearrange("q (dc c) -> q dc c", c=TS + 3)[:, :, 0:3], 0.0))

        p.inc_after("gpsimd", "gld", 16, lambda e: e.dma_start(out=IDN[:, :], in_=ident))
        p.inc_after("gpsimd", "gld", 16, lambda e: e.dma_start(out=EHB[:, :], in_=ehb))
        p.inc_after("gpsimd", "gld", 16, lambda e: e.dma_start(out=EHC[:, :], in_=ehc))
        for piece in range(4):
            p.inc_after("gpsimd", "gld", 16,
                        lambda e, piece=piece: e.dma_start(
                            out=WIP[:, :].rearrange("q (dc f) -> q dc f", f=2 * DI)[:, 2 * piece:2 * piece + 2, :],
                            in_=wipT.rearrange("(dc q) f -> q dc f", q=128)[:, 2 * piece:2 * piece + 2, :]))
        for piece in range(4):
            p.inc_after("gpsimd", "gld", 16,
                        lambda e, piece=piece: e.dma_start(
                            out=WOP[:, :].rearrange("q (fc f) -> q fc f", f=D)[:, 4 * piece:4 * piece + 4, :],
                            in_=wopT.rearrange("(fc q) f -> q fc f", q=128)[:, 4 * piece:4 * piece + 4, :]))
        p.inc_after("gpsimd", "gld", 16,
                    lambda e: e.dma_start(out=WXP[:, :].rearrange("q (dc f) -> q dc f", f=96),
                                          in_=wxpT.rearrange("(dc q) f -> q dc f", q=128)))
        p.inc_after("gpsimd", "gld", 16, lambda e: e.dma_start(out=WDT[0:64, :], in_=wdtT))
        p.inc_after("sync", "cld", 16, lambda e: e.dma_start(out=CW[:, :], in_=convw))
        p.inc_after("sync", "cld", 16, lambda e: e.dma_start(out=CB[:, :], in_=convb))
        p.inc_after("sync", "cld", 16, lambda e: e.dma_start(out=DTB[:, :], in_=dtb))
        p.inc_after("sync", "cld", 16, lambda e: e.dma_start(out=AC[:, :], in_=alog))
        p.inc_after("sync", "cld", 16, lambda e: e.dma_start(out=DM[:, :], in_=dmat))
        ld_c = p.now("cld")
        p.wait("scalar", "cld", ld_c)
        p.inc_after("scalar", "se", 1, lambda e: e.activation(AC[:, :], AC[:, :], AF.Exp))
        p.wait("vector", "se")
        p.inc_after("vector", "ve", 1, lambda e: e.tensor_scalar_mul(AC[:, :], AC[:, :], -1.0))
        ve_a = p.now("ve")
        LNW = XS32[:, :].bitcast(F32)[0:1, 0:D]
        LNB2 = XS32[:, :].bitcast(F32)[0:1, D:2 * D]
        p.inc_after("sync", "lnld", 16, lambda e: e.dma_start(out=LNW, in_=lnw))
        p.inc_after("sync", "lnld", 16, lambda e: e.dma_start(out=LNB2, in_=lnb))
        ld_ln = p.now("lnld")
        p.wait("tensor", "lnld", ld_ln)
        p.wait("tensor", "ve", ve_a)
        for half in range(2):
            p.inc_after("tensor", "te", 1,
                        lambda e, half=half: e.matmul(ps_o[:, 0:512], ONE1[0:1, :],
                                                      LNW[0:1, half * 512:(half + 1) * 512],
                                                      start=True, stop=True))
            p.wait("vector", "te")
            p.inc_after("vector", "ve", 1,
                        lambda e, half=half: e.tensor_copy(WLNB[:, half * 512:(half + 1) * 512],
                                                           ps_o[:, 0:512]))
            p.wait("tensor", "ve")
            p.inc_after("tensor", "te", 1,
                        lambda e, half=half: e.matmul(ps_o[:, 0:512], ONE1[0:1, :],
                                                      LNB2[0:1, half * 512:(half + 1) * 512],
                                                      start=True, stop=True))
            p.wait("vector", "te")
            p.inc_after("vector", "ve", 1,
                        lambda e, half=half: e.tensor_copy(BLNB[:, half * 512:(half + 1) * 512],
                                                           ps_o[:, 0:512]))
            p.wait("tensor", "ve")
        ve_pre = p.now("ve")
        p.wait("sync", "ve", ve_pre)

        for s in range(NS):
            p.wait("sync", "ve")
            for tt in range(TPS):
                p.inc_after("sync", "ld", 16,
                            lambda e, s=s, tt=tt: e.dma_start(out=XS32[:, tt * D:(tt + 1) * D],
                                                              in_=x_t[s * TPS + tt]))
            ld_s = p.now("ld")
            for tt in range(TPS):
                p.wait("vector", "ld", ld_s)
                p.wait("scalar", "ld", ld_s)
                p.wait("scalar", "st")
                p.wait("vector", "st")
                emit_ln_tile(p, XS32[:, tt * D:(tt + 1) * D], OUT[:, 0:D], stats,
                             sq_scr=OUT[:, 0:D])
                p.wait("vector", "ve")
                p.inc_after("vector", "ve", 1,
                            lambda e, tt=tt: e.tensor_tensor(OUT[:, 0:D], OUT[:, 0:D],
                                                             WLNB[:, :], ALU.mult))
                p.wait("vector", "ve")
                p.inc_after("vector", "ve", 1,
                            lambda e, tt=tt: e.tensor_tensor(XLNT[:, tt * D:(tt + 1) * D],
                                                             OUT[:, 0:D], BLNB[:, :], ALU.add))
            ve_ln = p.now("ve")
            p.wait("tensor", "gld")
            p.wait("tensor", "ve", ve_ln)
            xlnf_view = XLNF[:, :].rearrange("q (nd t) -> q nd t", t=TS)
            for tt in range(TPS):
                p.wait("tensor", "ve")
                fns = [
                    (lambda e, tt=tt, dc=dc: e.transpose(
                        ps_tr[:, dc * 128:(dc + 1) * 128],
                        XLNT[:, tt * D + dc * 128: tt * D + (dc + 1) * 128], IDN[:, :]))
                    for dc in range(ND)]
                for i, fn in enumerate(fns):
                    if i == ND - 1:
                        p.inc_after("tensor", "te", 1, fn)
                    else:
                        p.t(fn)
                p.wait("vector", "te")
                p.inc_after("vector", "ve", 1,
                            lambda e, tt=tt, xlnf_view=xlnf_view: e.tensor_copy(
                                xlnf_view[:, :, tt * 128:(tt + 1) * 128],
                                ps_tr[:, :].rearrange("q (nd f) -> q nd f", f=128)))
            ve_xf = p.now("ve")
            p.wait("tensor", "ve", ve_xf)
            p.wait("tensor", "gld")
            xcb_view = XCB[:, :].rearrange("q (dc c) -> q dc c", c=TS + 3)
            for fc in range(32):
                if fc > 0 or s > 0:
                    p.wait("tensor", "se")
                for dc in range(ND):
                    fn = lambda e, fc=fc, dc=dc: e.matmul(
                        ps_ip[:, :], WIP[:, (dc * 32 + fc) * 128:(dc * 32 + fc + 1) * 128],
                        XLNF[:, dc * TS:(dc + 1) * TS],
                        start=(dc == 0), stop=(dc == ND - 1))
                    if dc == ND - 1:
                        p.inc_after("tensor", "te", 1, fn)
                    else:
                        p.t(fn)
                p.wait("scalar", "te")
                if fc < NDI:
                    p.wait("scalar", "ve")
                    p.inc_after("scalar", "se", 1,
                                lambda e, fc=fc, xcb_view=xcb_view: e.activation(
                                    xcb_view[:, fc, 3:TS + 3], ps_ip[:, :], AF.Copy))
                else:
                    p.inc_after("scalar", "se", 1,
                                lambda e, fc=fc: e.activation(
                                    ZS[:, (fc - NDI) * TS:(fc - NDI + 1) * TS], ps_ip[:, :], AF.Copy))
            se_ip = p.now("se")
            for dc in range(NDI):
                p.wait("vector", "se")
                p.inc_after("vector", "ve", 1,
                            lambda e, dc=dc: e.tensor_scalar(
                                CV[:, :], XCB[:, dc * (TS + 3): dc * (TS + 3) + TS],
                                CW[:, dc * 4: dc * 4 + 1], None, ALU.mult))
                for k in range(1, 4):
                    p.wait("vector", "ve")
                    p.inc_after("vector", "ve", 1,
                                lambda e, dc=dc, k=k: e.scalar_tensor_tensor(
                                    CV[:, :], XCB[:, dc * (TS + 3) + k: dc * (TS + 3) + k + TS],
                                    CW[:, dc * 4 + k: dc * 4 + k + 1], CV[:, :],
                                    ALU.mult, ALU.add))
                p.wait("scalar", "ve")
                p.wait("scalar", "te")
                p.inc_after("scalar", "se", 1,
                            lambda e, dc=dc: e.activation(XCONV[:, dc * TS:(dc + 1) * TS],
                                                          CV[:, :], AF.Silu,
                                                          bias=CB[:, dc:dc + 1]))
            se_cv = p.now("se")
            p.wait("vector", "se", se_cv)
            p.inc_after("vector", "ve", 1,
                        lambda e: e.tensor_copy(xcb_view[:, :, 0:3], xcb_view[:, :, TS:TS + 3]))
            p.wait("tensor", "se", se_cv)
            for dc in range(NDI):
                fn = lambda e, dc=dc: e.matmul(
                    ps_xp[0:96, :], WXP[:, dc * 96:(dc + 1) * 96],
                    XCONV[:, dc * TS:(dc + 1) * TS],
                    start=(dc == 0), stop=(dc == NDI - 1))
                if dc == NDI - 1:
                    p.inc_after("tensor", "te", 1, fn)
                else:
                    p.t(fn)
            te_xp = p.now("te")
            p.wait("scalar", "te", te_xp)
            p.inc_after("scalar", "se", 1,
                        lambda e: e.activation(DT_[0:64, :], ps_xp[0:64, :], AF.Copy))
            p.inc_after("scalar", "se", 1,
                        lambda e: e.activation(BCB[0:32, :], ps_xp[64:96, :], AF.Copy))
            se_bc = p.now("se")
            p.wait("tensor", "se", se_bc)
            p.wait("vector", "se")
            for fc in range(NDI):
                p.inc_after("tensor", "te", 1,
                            lambda e, fc=fc: e.matmul(ps_dt[:, :],
                                                      WDT[0:64, fc * 128:(fc + 1) * 128],
                                                      DT_[0:64, :], start=True, stop=True))
                te_d = p.now("te")
                p.wait("vector", "te", te_d)
                p.inc_after("vector", "ve", 1,
                            lambda e, fc=fc: e.tensor_scalar_add(D0[:, :], ps_dt[:, :],
                                                                 DTB[:, fc:fc + 1]))
                p.wait("vector", "ve")
                p.inc_after("vector", "ve", 1,
                            lambda e: e.scalar_tensor_tensor(AB[:, :], D0[:, :], -1.0, D0[:, :],
                                                             ALU.mult, ALU.max))
                p.wait("scalar", "ve")
                p.inc_after("scalar", "se", 1,
                            lambda e: e.activation(EL[:, :], AB[:, :], AF.Exp, scale=-1.0))
                p.wait("scalar", "se")
                p.inc_after("scalar", "se", 1,
                            lambda e: e.activation(EL[:, :], EL[:, :], AF.Ln, bias=1.0))
                p.wait("vector", "ve")
                p.inc_after("vector", "ve", 1,
                            lambda e: e.tensor_scalar_max(D0[:, :], D0[:, :], 0.0))
                p.wait("vector", "se")
                p.wait("vector", "ve")
                p.inc_after("vector", "ve", 1,
                            lambda e, fc=fc: e.tensor_tensor(DELTA[:, fc * TS:(fc + 1) * TS],
                                                             EL[:, :], D0[:, :], ALU.add))
                p.wait("tensor", "ve")
            p.wait("vector", "ve")
            for dc in range(NDI):
                p.inc_after("vector", "ve", 1,
                            lambda e, dc=dc: e.tensor_tensor(DU[:, dc * TS:(dc + 1) * TS],
                                                             DELTA[:, dc * TS:(dc + 1) * TS],
                                                             XCONV[:, dc * TS:(dc + 1) * TS],
                                                             ALU.mult))
            ve_du = p.now("ve")
            p.wait("tensor", "se", se_bc)
            p.wait("scalar", "ve")
            for n in range(NST):
                p.wait("tensor", "se")
                p.inc_after("tensor", "te", 1,
                            lambda e, n=n: e.matmul(ps_bc[:, :], EHB[:, n * 128:(n + 1) * 128],
                                                    BCB[0:32, :], start=True, stop=True))
                p.wait("scalar", "te")
                p.inc_after("scalar", "se", 1,
                            lambda e, n=n: e.activation(BB[:, n * TS:(n + 1) * TS],
                                                        ps_bc[:, :], AF.Copy))
                p.wait("tensor", "se")
                p.inc_after("tensor", "te", 1,
                            lambda e, n=n: e.matmul(ps_bc[:, :], EHC[:, n * 128:(n + 1) * 128],
                                                    BCB[0:32, :], start=True, stop=True))
                p.wait("scalar", "te")
                p.inc_after("scalar", "se", 1,
                            lambda e, n=n: e.activation(CCB[:, n * TS:(n + 1) * TS],
                                                        ps_bc[:, :], AF.Copy))
            se_bb = p.now("se")
            p.wait("vector", "se", se_bb)
            p.wait("scalar", "ve", ve_du)
            for dc in range(NDI):
                for n in range(NST):
                    par = (dc * NST + n) % 2
                    p.wait("scalar", "ve")
                    p.inc_after("scalar", "se", 1,
                                lambda e, dc=dc, n=n, par=par: e.activation(
                                    DA[par][:, :], DELTA[:, dc * TS:(dc + 1) * TS], AF.Exp,
                                    scale=AC[:, dc * NST + n: dc * NST + n + 1]))
                    p.inc_after("vector", "ve", 1,
                                lambda e, dc=dc, n=n, par=par: e.tensor_tensor(
                                    DBu[par][:, :], DU[:, dc * TS:(dc + 1) * TS],
                                    BB[:, n * TS:(n + 1) * TS], ALU.mult))
                    p.wait("vector", "se")
                    p.wait("vector", "ve")
                    p.inc_after("vector", "ve", 1,
                                lambda e, dc=dc, n=n, par=par: e.tensor_tensor_scan(
                                    HS[:, :], DA[par][:, :], DBu[par][:, :],
                                    HC[:, dc * NST + n: dc * NST + n + 1],
                                    ALU.mult, ALU.add))
                    p.wait("vector", "ve")
                    p.inc_after("vector", "ve", 1,
                                lambda e, dc=dc, n=n: e.tensor_copy(
                                    HC[:, dc * NST + n: dc * NST + n + 1], HS[:, TS - 1:TS]))
                    if n == 0:
                        p.inc_after("vector", "ve", 1,
                                    lambda e, n=n: e.tensor_tensor(YA[:, :], HS[:, :],
                                                                   CCB[:, n * TS:(n + 1) * TS],
                                                                   ALU.mult))
                    else:
                        p.inc_after("vector", "ve", 1,
                                    lambda e, n=n: e.tensor_tensor(TS3[:, :], HS[:, :],
                                                                   CCB[:, n * TS:(n + 1) * TS],
                                                                   ALU.mult))
                        p.wait("vector", "ve")
                        p.inc_after("vector", "ve", 1,
                                    lambda e: e.tensor_tensor(YA[:, :], YA[:, :], TS3[:, :],
                                                              ALU.add))
                p.wait("vector", "ve")
                p.inc_after("vector", "ve", 1,
                            lambda e, dc=dc: e.scalar_tensor_tensor(
                                YA[:, :], XCONV[:, dc * TS:(dc + 1) * TS], DM[:, dc:dc + 1],
                                YA[:, :], ALU.mult, ALU.add))
                p.wait("scalar", "ve")
                p.inc_after("scalar", "se", 1,
                            lambda e, dc=dc: e.activation(TS3[:, :].bitcast(BF16)[:, 0:TS],
                                                          ZS[:, dc * TS:(dc + 1) * TS], AF.Silu))
                p.wait("vector", "se")
                p.inc_after("vector", "ve", 1,
                            lambda e, dc=dc: e.tensor_tensor(YF[:, dc * TS:(dc + 1) * TS],
                                                             YA[:, :],
                                                             TS3[:, :].bitcast(BF16)[:, 0:TS],
                                                             ALU.mult))
            ve_y = p.now("ve")
            p.wait("tensor", "ve", ve_y)
            for tt in range(TPS):
                if s > 0 or tt > 0:
                    p.wait("tensor", "ve")
                for half in range(2):
                    for fc in range(NDI):
                        fn = lambda e, tt=tt, half=half, fc=fc: e.matmul(
                            ps_o[:, half * 512:(half + 1) * 512],
                            YF[:, fc * TS + tt * 128: fc * TS + tt * 128 + 128],
                            WOP[:, fc * D + half * 512: fc * D + half * 512 + 512],
                            start=(fc == 0), stop=(fc == NDI - 1))
                        if fc == NDI - 1:
                            p.inc_after("tensor", "te", 1, fn)
                        else:
                            p.t(fn)
                te_o = p.now("te")
                p.wait("vector", "te", te_o)
                p.inc_after("vector", "ve", 1,
                            lambda e, tt=tt: e.scalar_tensor_tensor(
                                XS32[:, tt * D:(tt + 1) * D], XS32[:, tt * D:(tt + 1) * D],
                                2.0, ps_o[:, :], ALU.mult, ALU.add))
                p.wait("scalar", "st")
                p.wait("vector", "st")
                emit_ln_tile(p, XS32[:, tt * D:(tt + 1) * D], OUT[:, 0:D], stats,
                             sq_scr=OUT[:, 0:D])
                ve_out = p.now("ve")
                p.wait("sync", "ve", ve_out)
                p.inc_after("sync", "st", 16,
                            lambda e, s=s, tt=tt: e.dma_start(out=yo_t[s * TPS + tt],
                                                              in_=OUT[:, 0:D]))

        p.wait("sync", "st")
        p.run()
    return nc



def build_l2_attn(T_=None):
    Tl = T_ or T
    NTl = Tl // 128
    NH = 8
    NQS = Tl // 512
    GEPS = 1e-5

    nc = bass.Bass("TRN2", target_bir_lowering=False)
    x = nc.dram_tensor("x", [Tl, D], F32, kind="ExternalInput").ap()
    wq = nc.dram_tensor("wq", [128, NH * 128], F32, kind="ExternalInput").ap()
    wk = nc.dram_tensor("wk", [128, NH * 128], F32, kind="ExternalInput").ap()
    wv = nc.dram_tensor("wv", [128, NH * 128], F32, kind="ExternalInput").ap()
    w1 = nc.dram_tensor("w1", [D, D], F32, kind="ExternalInput").ap()
    w2 = nc.dram_tensor("w2", [D, D], F32, kind="ExternalInput").ap()
    gnw = nc.dram_tensor("gnw", [128, NH], F32, kind="ExternalInput").ap()
    gnb = nc.dram_tensor("gnb", [128, NH], F32, kind="ExternalInput").ap()
    gp = nc.dram_tensor("c_gp", [128, NH * 36], F32, kind="ExternalInput").ap()
    ramp = nc.dram_tensor("c_ramp", [128, NH * 512], F32, kind="ExternalInput").ap()
    cmask = nc.dram_tensor("c_mask", [128, 128], F32, kind="ExternalInput").ap()
    ident = nc.dram_tensor("c_ident", [128, 128], F32, kind="ExternalInput").ap()
    yo = nc.dram_tensor("y", [Tl, D], F32, kind="ExternalOutput").ap()

    gammas = 1.0 - np.exp(np.linspace(np.log(1 / 512), np.log(1 / 32), NH))
    mm_max = [min(NTl, int(np.ceil(37.0 / (128 * -np.log(g)))) + 1) for g in gammas]

    x_t = x.rearrange("(nt p) d -> nt p d", p=128)
    yo_t = yo.rearrange("(nt p) d -> nt p d", p=128)

    with ExitStack() as ctx:
        sbt = lambda *a: ctx.enter_context(nc.sbuf_tensor(*a))
        pst = lambda *a: ctx.enter_context(nc.psum_tensor(*a))
        p = Prog(nc, ctx)

        IDN = sbt("IDN", [128, 128], BF16)
        XF = sbt("XF2", [128, ND * Tl], BF16)
        RETF = sbt("RETF", [128, ND * Tl], BF16)
        WQ = sbt("WQ2", [128, NH * 128], BF16)
        WK = sbt("WK2", [128, NH * 128], BF16)
        WV = sbt("WV2", [128, NH * 128], BF16)
        KVW = sbt("KVW", [128, max(2 * Tl, ND * D)], BF16)
        W2S = sbt("W2S", [128, ND * D], BF16)
        W1S = KVW[:, 0:ND * D]
        GNW = sbt("GNW", [128, NH], F32)
        GNB = sbt("GNB", [128, NH], F32)
        GP = sbt("GP2", [128, NH * 36], F32)
        RAMP = sbt("RAMP", [128, NH * 512], BF16)
        MASK = sbt("MASK2", [128, 128], BF16)
        KF = KVW[:, 0:Tl]
        VT = KVW[:, Tl:2 * Tl]
        QS_ = sbt("QS2", [128, 512], BF16)
        PBJ = sbt("PBJ2", [128, 512], BF16)
        RET_s = sbt("RET_s", [128, 512], BF16)
        SQ = sbt("SQ2", [128, 512], BF16)
        RSC = sbt("RSC2f", [128, 512], F32)
        SG = sbt("SG2", [128, 512], BF16)
        S2F = sbt("S2F", [128, ND * 512], BF16)
        MU = S2F[:, :].bitcast(F32)[0:1, 0:512]
        RSTD = S2F[:, :].bitcast(F32)[0:1, 512:1024]
        ONE1 = sbt("ONE1f", [1, 128], F32)
        ONEC = sbt("ONECb", [128, 1], BF16)
        SHARE = sbt("SHARE2", [128, 4 * D], BF16)
        XSB = SHARE[:, 0:2 * D]
        S32 = SHARE[:, :].bitcast(F32)
        OUT = sbt("OUT2", [128, D], F32)
        stats = alloc_stats(sbt)

        ps_tr = pst("ps_tr", [128, ND * 128], BF16)
        ps_a = pst("ps_a", [128, 512], F32)
        ps_sc = pst("ps_sc", [128, 512], F32)
        ps_ret = pst("ps_ret", [128, 512], F32)
        ps_rb = pst("ps_rb", [128, 512], F32)
        ps_po = pst("ps_po", [128, D], F32)

        p.inc_after("vector", "ve", 1, lambda e: e.memset(stats["eps"][:, 0:1], LN_EPS))
        p.inc_after("vector", "ve", 1, lambda e: e.memset(ONE1[:, :], 1.0))
        p.inc_after("vector", "ve", 1, lambda e: e.memset(ONEC[:, :], 1.0))

        p.inc_after("gpsimd", "gld", 16, lambda e: e.dma_start(out=IDN[:, :], in_=ident))
        p.inc_after("gpsimd", "gld", 16, lambda e: e.dma_start(out=WQ[:, :], in_=wq))
        p.inc_after("gpsimd", "gld", 16, lambda e: e.dma_start(out=WK[:, :], in_=wk))
        p.inc_after("gpsimd", "gld", 16, lambda e: e.dma_start(out=WV[:, :], in_=wv))
        p.inc_after("gpsimd", "gld", 16,
                    lambda e: e.dma_start(out=W2S[:, :].rearrange("q (dc f) -> q dc f", f=D),
                                          in_=w2.rearrange("(dc q) f -> q dc f", q=128)))
        p.inc_after("gpsimd", "gld", 16, lambda e: e.dma_start(out=RAMP[:, :], in_=ramp))
        p.inc_after("gpsimd", "gld", 16, lambda e: e.dma_start(out=MASK[:, :], in_=cmask))
        p.inc_after("sync", "cld", 16, lambda e: e.dma_start(out=GNW[:, :], in_=gnw))
        p.inc_after("sync", "cld", 16, lambda e: e.dma_start(out=GNB[:, :], in_=gnb))
        p.inc_after("sync", "cld", 16, lambda e: e.dma_start(out=GP[:, :], in_=gp))
        cld_c = p.now("cld")

        te_nt = []
        for nt in range(NTl):
            buf = (nt % 2) * D
            par = f"gx{nt % 2}"
            if nt >= 2:
                p.wait("gpsimd", "te", te_nt[nt - 2])
            p.inc_after("gpsimd", par, 16,
                        lambda e, nt=nt, buf=buf: e.dma_start(out=XSB[:, buf:buf + D], in_=x_t[nt]))
            p.wait("tensor", par)
            if nt == 0:
                p.wait("tensor", "gld")
            p.wait("tensor", "ve")
            fns = [
                (lambda e, nt=nt, dc=dc, buf=buf: e.transpose(
                    ps_tr[:, dc * 128:(dc + 1) * 128],
                    XSB[:, buf + dc * 128: buf + (dc + 1) * 128], IDN[:, :]))
                for dc in range(ND)]
            for i, fn in enumerate(fns):
                if i == ND - 1:
                    p.inc_after("tensor", "te", 1, fn)
                else:
                    p.t(fn)
            te_nt.append(p.now("te"))
            p.wait("vector", "te")
            xf_view = XF[:, :].rearrange("q (nd t) -> q nd t", t=Tl)
            p.inc_after("vector", "ve", 1,
                        lambda e, nt=nt, xf_view=xf_view: e.tensor_copy(
                            xf_view[:, :, nt * 128:(nt + 1) * 128],
                            ps_tr[:, :].rearrange("q (nd f) -> q nd f", f=128)))
        ve_xf = p.now("ve")
        p.wait("tensor", "ve", ve_xf)
        p.wait("scalar", "cld", cld_c)

        for h in range(NH):
            for tc in range(Tl // 512):
                if h > 0 or tc > 0:
                    p.wait("tensor", "se")
                p.inc_after("tensor", "te", 1,
                            lambda e, h=h, tc=tc: e.matmul(
                                ps_a[:, :], WK[:, h * 128:(h + 1) * 128],
                                XF[:, h * Tl + tc * 512: h * Tl + tc * 512 + 512],
                                start=True, stop=True))
                p.wait("scalar", "te")
                p.inc_after("scalar", "se", 1,
                            lambda e, tc=tc: e.activation(KF[:, tc * 512:(tc + 1) * 512],
                                                          ps_a[:, :], AF.Copy))
            for kt in range(NTl):
                p.wait("tensor", "se")
                p.inc_after("tensor", "te", 1,
                            lambda e, h=h, kt=kt: e.matmul(
                                ps_a[:, 0:128], XF[:, h * Tl + kt * 128: h * Tl + kt * 128 + 128],
                                WV[:, h * 128:(h + 1) * 128], start=True, stop=True))
                p.wait("scalar", "te")
                p.inc_after("scalar", "se", 1,
                            lambda e, kt=kt: e.activation(VT[:, kt * 128:(kt + 1) * 128],
                                                          ps_a[:, 0:128], AF.Copy))
            se_kv = p.now("se")
            p.wait("tensor", "se", se_kv)

            for qs in range(NQS):
                p.wait("tensor", "ve")
                p.inc_after("tensor", "te", 1,
                            lambda e, h=h, qs=qs: e.matmul(
                                ps_a[:, :], WQ[:, h * 128:(h + 1) * 128],
                                XF[:, h * Tl + qs * 512: h * Tl + qs * 512 + 512],
                                start=True, stop=True))
                p.wait("vector", "te")
                p.inc_after("vector", "ve", 1,
                            lambda e, h=h: e.tensor_tensor(QS_[:, :], ps_a[:, :],
                                                           RAMP[:, h * 512:(h + 1) * 512],
                                                           ALU.mult))
                ve_q = p.now("ve")
                p.wait("tensor", "ve", ve_q)
                kt_lo = max(0, qs * 4 - mm_max[h])
                kts = list(range(kt_lo, min(NTl, qs * 4 + 4)))
                for i, kt in enumerate(kts):
                    p.wait("tensor", "se")
                    p.inc_after("tensor", "te", 1,
                                lambda e, kt=kt: e.matmul(
                                    ps_sc[:, :], KF[:, kt * 128:(kt + 1) * 128], QS_[:, :],
                                    start=True, stop=True))
                    te_s = p.now("te")
                    p.wait("scalar", "te", te_s)
                    mm = qs * 4 - kt
                    p.wait("scalar", "ve")
                    p.inc_after("scalar", "se", 1,
                                lambda e, h=h, mm=mm: e.activation(
                                    PBJ[:, :], ps_sc[:, :], AF.Copy,
                                    scale=GP[:, h * 36 + mm + 3: h * 36 + mm + 4]))
                    se_e = p.now("se")
                    if mm <= 0:
                        mt = -mm
                        p.wait("vector", "se", se_e)
                        if mt > 0:
                            p.inc_after("vector", "ve", 1,
                                        lambda e, mt=mt: e.memset(PBJ[:, 0:128 * mt], 0.0))
                        p.inc_after("vector", "ve", 1,
                                    lambda e, mt=mt: e.tensor_tensor(
                                        PBJ[:, 128 * mt:128 * mt + 128],
                                        PBJ[:, 128 * mt:128 * mt + 128], MASK[:, :], ALU.mult))
                        p.wait("tensor", "ve")
                    else:
                        p.wait("tensor", "se", se_e)
                    fn = (lambda e, kt=kt, i=i, last=(i == len(kts) - 1): e.matmul(
                        ps_ret[:, :], VT[:, kt * 128:(kt + 1) * 128], PBJ[:, :],
                        start=(i == 0), stop=last))
                    if i == len(kts) - 1:
                        p.inc_after("tensor", "te", 1, fn)
                    else:
                        p.t(fn)
                te_pv = p.now("te")
                p.wait("scalar", "te", te_pv)
                p.inc_after("scalar", "se", 1,
                            lambda e: e.activation(RET_s[:, :], ps_ret[:, :], AF.Copy))
                p.wait("scalar", "se")
                p.inc_after("scalar", "se", 1,
                            lambda e: e.activation(SQ[:, :], RET_s[:, :], AF.Square))
                se_r = p.now("se")
                p.wait("tensor", "se", se_r)
                p.inc_after("tensor", "te", 1,
                            lambda e: e.matmul(ps_sc[0:1, :], ONEC[:, 0:1], RET_s[:, :],
                                               start=True, stop=True))
                p.inc_after("tensor", "te", 1,
                            lambda e: e.matmul(ps_rb[0:1, :], ONEC[:, 0:1], SQ[:, :],
                                               start=True, stop=True))
                te_st = p.now("te")
                p.wait("vector", "te", te_st)
                p.inc_after("vector", "ve", 1,
                            lambda e: e.tensor_scalar_mul(MU, ps_sc[0:1, :], 1.0 / 128))
                p.wait("vector", "ve")
                p.inc_after("vector", "ve", 1,
                            lambda e: e.tensor_scalar_mul(RSTD, ps_rb[0:1, :], 1.0 / 128))
                p.wait("vector", "ve")
                p.inc_after("vector", "ve", 1,
                            lambda e: e.tensor_tensor(ps_sc[0:1, :], MU[0:1, :], MU[0:1, :],
                                                      ALU.mult) if False else
                            e.tensor_mul(RSC[0:1, 0:512], MU, MU))
                p.wait("vector", "ve")
                p.inc_after("vector", "ve", 1,
                            lambda e: e.tensor_sub(RSTD, RSTD, RSC[0:1, 0:512]))
                p.wait("scalar", "ve")
                p.inc_after("scalar", "se", 1,
                            lambda e: e.activation(RSTD, RSTD, AF.Sqrt,
                                                   bias=stats["eps"][0:1, 0:1]))
                p.wait("vector", "se")
                p.inc_after("vector", "ve", 1,
                            lambda e: e.reciprocal(RSTD, RSTD))
                ve_st = p.now("ve")
                p.wait("tensor", "ve", ve_st)
                p.inc_after("tensor", "te", 1,
                            lambda e: e.matmul(ps_sc[:, :], ONE1[0:1, :], MU,
                                               start=True, stop=True))
                p.inc_after("tensor", "te", 1,
                            lambda e: e.matmul(ps_rb[:, :], ONE1[0:1, :], RSTD,
                                               start=True, stop=True))
                te_bc = p.now("te")
                p.wait("vector", "te", te_bc)
                p.inc_after("vector", "ve", 1,
                            lambda e: e.tensor_tensor(RSC[:, :], RET_s[:, :], ps_sc[:, :],
                                                      ALU.subtract))
                p.wait("vector", "ve")
                p.inc_after("vector", "ve", 1,
                            lambda e: e.tensor_tensor(RSC[:, :], RSC[:, :], ps_rb[:, :],
                                                      ALU.mult))
                p.wait("vector", "ve")
                p.inc_after("vector", "ve", 1,
                            lambda e, h=h, qs=qs: e.tensor_scalar(
                                RETF[:, h * Tl + qs * 512: h * Tl + qs * 512 + 512],
                                RSC[:, :], GNW[:, h:h + 1], GNB[:, h:h + 1],
                                ALU.mult, ALU.add))
                p.wait("tensor", "ve")
        ve_ret = p.now("ve")

        p.wait("gpsimd", "te")
        p.inc_after("gpsimd", "gw1", 16,
                    lambda e: e.dma_start(out=W1S[0:128, 0:ND * D].rearrange("q (dc f) -> q dc f", f=D),
                                          in_=w1.rearrange("(dc q) f -> q dc f", q=128)))
        gw1 = p.now("gw1")
        p.wait("tensor", "gw1", gw1)

        p.wait("tensor", "ve", ve_ret)
        for ts_i in range(NQS):
            for fc in range(ND):
                if ts_i > 0 or fc > 0:
                    p.wait("tensor", "ve")
                    p.wait("tensor", "se")
                fns = [
                    (lambda e, ts_i=ts_i, fc=fc, dc=dc: e.matmul(
                        ps_a[:, :], W1S[:, dc * D + fc * 128: dc * D + fc * 128 + 128],
                        XF[:, dc * Tl + ts_i * 512: dc * Tl + ts_i * 512 + 512],
                        start=(dc == 0), stop=(dc == ND - 1)))
                    for dc in range(ND)]
                for i, fn in enumerate(fns):
                    if i == ND - 1:
                        p.inc_after("tensor", "te", 1, fn)
                    else:
                        p.t(fn)
                p.wait("scalar", "te")
                p.wait("scalar", "ve")
                p.inc_after("scalar", "se", 1,
                            lambda e: e.activation(SG[:, :], ps_a[:, :], AF.Silu))
                p.wait("vector", "se")
                p.inc_after("vector", "ve", 1,
                            lambda e, ts_i=ts_i, fc=fc: e.tensor_tensor(
                                S2F[:, fc * 512:(fc + 1) * 512], SG[:, :],
                                RETF[:, fc * Tl + ts_i * 512: fc * Tl + ts_i * 512 + 512],
                                ALU.add))
            ve_s2 = p.now("ve")
            p.wait("tensor", "ve", ve_s2)
            for tt in range(4):
                nt = ts_i * 4 + tt
                p.wait("sync", "ve")
                p.inc_after("sync", "ld", 16,
                            lambda e, nt=nt: e.dma_start(out=S32[:, 0:D], in_=x_t[nt]))
                ld_nt = p.now("ld")
                if nt > 0:
                    p.wait("tensor", "ve")
                for half in range(2):
                    fns = [
                        (lambda e, tt=tt, half=half, fc=fc: e.matmul(
                            ps_po[:, half * 512:(half + 1) * 512],
                            S2F[:, fc * 512 + tt * 128: fc * 512 + tt * 128 + 128],
                            W2S[:, fc * D + half * 512: fc * D + half * 512 + 512],
                            start=(fc == 0), stop=(fc == ND - 1)))
                        for fc in range(ND)]
                    for i, fn in enumerate(fns):
                        if i == ND - 1:
                            p.inc_after("tensor", "te", 1, fn)
                        else:
                            p.t(fn)
                te_po = p.now("te")
                p.wait("vector", "te", te_po)
                p.wait("vector", "ld", ld_nt)
                p.inc_after("vector", "ve", 1,
                            lambda e: e.tensor_tensor(S32[:, 0:D], S32[:, 0:D],
                                                      ps_po[:, :], ALU.add))
                p.wait("scalar", "st")
                p.wait("vector", "st")
                emit_ln_tile(p, S32[:, 0:D], OUT[:, 0:D], stats, sq_scr=OUT[:, 0:D])
                ve_out = p.now("ve")
                p.wait("sync", "ve", ve_out)
                p.inc_after("sync", "st", 16,
                            lambda e, nt=nt: e.dma_start(out=yo_t[nt], in_=OUT[:, 0:D]))

        p.wait("sync", "st")
        p.run()
    return nc



def build_l2_ffn(T_=None):
    Tl = T_ or T
    NTl = Tl // 128
    H = 4096
    NHC = H // 128
    TS = 256
    NS = Tl // TS
    TPS = TS // 128
    REPS = 1e-8

    nc = bass.Bass("TRN2", target_bir_lowering=False)
    x = nc.dram_tensor("x", [Tl, D], F32, kind="ExternalInput").ap()
    wupT = nc.dram_tensor("wupT", [D, H], F32, kind="ExternalInput").ap()
    wdnT = nc.dram_tensor("wdnT", [H, D], F32, kind="ExternalInput").ap()
    ident = nc.dram_tensor("c_ident", [128, 128], F32, kind="ExternalInput").ap()
    yo = nc.dram_tensor("y", [Tl, D], F32, kind="ExternalOutput").ap()

    x_t = x.rearrange("(nt p) d -> nt p d", p=128)
    yo_t = yo.rearrange("(nt p) d -> nt p d", p=128)

    with ExitStack() as ctx:
        sbt = lambda *a: ctx.enter_context(nc.sbuf_tensor(*a))
        pst = lambda *a: ctx.enter_context(nc.psum_tensor(*a))
        p = Prog(nc, ctx)

        IDN = sbt("IDN", [128, 128], BF16)
        WUQ = sbt("WUQ", [128, ND * H], BF16)
        WDQ = sbt("WDQ", [128, NHC * D], BF16)
        WSTG = sbt("WSTG2", [128, 4096], F32)
        ABSS = sbt("ABSS", [128, 2], F32)
        TH = sbt("TH2", [128, 2], F32)
        SGN = sbt("SGN2", [128, 4096], BF16)
        AB2 = sbt("AB2f", [128, 4096], F32)
        RED = sbt("RED2", [128, 1], F32)
        XNT = sbt("XNT2", [128, TPS * D], BF16)
        XS32 = sbt("XS32f2", [128, TPS * D], F32)
        XNF = [sbt(f"XNF{i}", [128, ND * TS], BF16) for i in range(2)]
        HS = sbt("HS2", [128, TS], BF16)
        SQH = sbt("SQH2", [128, TS], BF16)
        RSTDC = sbt("RSTDC", [128, TPS], F32)
        ONEC = sbt("ONECf2", [128, 1], BF16)
        ONE1 = sbt("ONE1f2", [1, 128], F32)
        OUT = sbt("OUT2f", [128, D], F32)
        RROW = sbt("RROW", [1, TS], F32)
        stats = alloc_stats(sbt)

        ps_tr = pst("ps_tr", [128, ND * 128], BF16)
        ps_u = pst("ps_u", [128, TS], F32)
        ps_sq = pst("ps_sq", [1, TS], F32)
        ps_d = [pst(f"ps_d{i}", [128, D], F32) for i in range(TPS)]
        ps_al = pst("ps_al", [128, 8], F32)

        p.inc_after("vector", "ve", 1, lambda e: e.memset(stats["eps"][:, 0:1], LN_EPS))
        p.inc_after("vector", "ve", 1, lambda e: e.memset(ONEC[:, :], 1.0))
        p.inc_after("vector", "ve", 1, lambda e: e.memset(ONE1[:, :], 1.0))
        p.inc_after("vector", "ve", 1, lambda e: e.memset(ABSS[:, :], 0.0))
        p.inc_after("gpsimd", "gld", 16, lambda e: e.dma_start(out=IDN[:, :], in_=ident))

        wup_r = wupT.rearrange("(rc q) f -> rc q f", q=128)
        wdn_r = wdnT.rearrange("(rc q) f -> rc q f", q=128)
        for wi_, (wr, nrc, ncol) in enumerate(((wup_r, ND, H), (wdn_r, NHC, D))):
            for rc in range(nrc):
                p.wait("sync", "ve")
                p.inc_after("sync", "wld", 16,
                            lambda e, wr=wr, rc=rc, ncol=ncol: e.dma_start(
                                out=WSTG[:, 0:ncol], in_=wr[rc]))
                p.wait("vector", "wld")
                p.inc_after("vector", "ve", 1,
                            lambda e, ncol=ncol: e.reduce_sum(RED[:, 0:1], WSTG[:, 0:ncol],
                                                              axis=AX.X,
                                                              apply_absolute_value=True))
                p.wait("vector", "ve")
                p.inc_after("vector", "ve", 1,
                            lambda e, wi_=wi_: e.tensor_add(ABSS[:, wi_:wi_ + 1],
                                                            ABSS[:, wi_:wi_ + 1],
                                                            RED[:, 0:1]))
        ve_abs = p.now("ve")
        ONECF = sbt("ONECF", [128, 1], F32)
        THR = sbt("THR2", [1, 2], F32)
        p.inc_after("vector", "ve", 1, lambda e: e.memset(ONECF[:, :], 1.0))
        p.wait("tensor", "ve", p.now("ve"))
        for wi_ in range(2):
            p.inc_after("tensor", "te", 1,
                        lambda e, wi_=wi_: e.matmul(ps_al[0:1, wi_:wi_ + 1], ONECF[:, 0:1],
                                                    ABSS[:, wi_:wi_ + 1],
                                                    start=True, stop=True))
        te_al = p.now("te")
        p.wait("vector", "te", te_al)
        p.inc_after("vector", "ve", 1,
                    lambda e: e.tensor_scalar_mul(THR[0:1, 0:2], ps_al[0:1, 0:2],
                                                  0.5 / (D * H)))
        p.wait("tensor", "ve")
        for wi_ in range(2):
            p.inc_after("tensor", "te", 1,
                        lambda e, wi_=wi_: e.matmul(ps_al[:, 4 + wi_:5 + wi_], ONE1[0:1, :],
                                                    THR[0:1, wi_:wi_ + 1],
                                                    start=True, stop=True))
        te_th = p.now("te")
        p.wait("vector", "te", te_th)
        p.inc_after("vector", "ve", 1,
                    lambda e: e.tensor_copy(TH[:, 0:2], ps_al[:, 4:6]))
        ve_th = p.now("ve")

        p.wait("sync", "ve", ve_th)
        for wi_, (wr, nrc, ncol, dest) in enumerate(((wup_r, ND, H, WUQ), (wdn_r, NHC, D, WDQ))):
            for rc in range(nrc):
                p.wait("sync", "se")
                p.wait("sync", "ve")
                p.inc_after("sync", "wld", 16,
                            lambda e, wr=wr, rc=rc, ncol=ncol: e.dma_start(
                                out=WSTG[:, 0:ncol], in_=wr[rc]))
                p.wait("scalar", "wld")
                p.inc_after("scalar", "se", 1,
                            lambda e, ncol=ncol: e.activation(SGN[:, 0:ncol], WSTG[:, 0:ncol],
                                                              AF.Sign))
                p.wait("vector", "wld")
                p.inc_after("vector", "ve", 1,
                            lambda e, ncol=ncol: e.scalar_tensor_tensor(
                                AB2[:, 0:ncol], WSTG[:, 0:ncol], -1.0, WSTG[:, 0:ncol],
                                ALU.mult, ALU.max))
                p.wait("vector", "ve")
                p.inc_after("vector", "ve", 1,
                            lambda e, wi_=wi_, ncol=ncol: e.tensor_scalar(
                                AB2[:, 0:ncol], AB2[:, 0:ncol], TH[:, wi_:wi_ + 1], None,
                                ALU.is_gt))
                p.wait("vector", "se")
                p.wait("vector", "ve")
                if wi_ == 0:
                    out_ap = dest[:, :].rearrange("q (dc f) -> q dc f", f=H)[:, rc, :]
                else:
                    out_ap = dest[:, rc * D:(rc + 1) * D]
                p.inc_after("vector", "ve", 1,
                            lambda e, out_ap=out_ap, ncol=ncol: e.tensor_tensor(
                                out_ap, SGN[:, 0:ncol], AB2[:, 0:ncol], ALU.mult))
        ve_w = p.now("ve")
        p.wait("tensor", "ve", ve_w)

        wuq_view = WUQ[:, :].rearrange("q (dc hc f) -> q dc hc f", hc=NHC, f=128)
        for s in range(NS):
            xnf = XNF[s % 2]
            if s > 0:
                p.wait("sync", "ve")
            for tt in range(TPS):
                p.inc_after("sync", "ld", 16,
                            lambda e, s=s, tt=tt: e.dma_start(out=XS32[:, tt * D:(tt + 1) * D],
                                                              in_=x_t[s * TPS + tt]))
            ld_s = p.now("ld")
            for tt in range(TPS):
                ssq, rstd = stats["ssq"], stats["rstd"]
                p.wait("scalar", "ld", ld_s)
                p.wait("scalar", "st")
                p.wait("scalar", "ve")
                p.inc_after("scalar", "se", 1,
                            lambda e, tt=tt: e.activation(OUT[:, 0:D],
                                                          XS32[:, tt * D:(tt + 1) * D],
                                                          AF.Square, accum_out=ssq[:, 0:1]))
                p.wait("scalar", "se")
                p.inc_after("scalar", "se", 1,
                            lambda e: e.activation(rstd[:, 0:1], ssq[:, 0:1], AF.Sqrt,
                                                   scale=1.0 / D))
                p.wait("vector", "se")
                p.inc_after("vector", "ve", 1,
                            lambda e: e.tensor_scalar_add(rstd[:, 0:1], rstd[:, 0:1], REPS))
                p.wait("vector", "ve")
                p.inc_after("vector", "ve", 1,
                            lambda e: e.reciprocal(rstd[:, 0:1], rstd[:, 0:1]))
                p.wait("vector", "ve")
                p.inc_after("vector", "ve", 1,
                            lambda e, tt=tt: e.tensor_scalar_mul(
                                XNT[:, tt * D:(tt + 1) * D], XS32[:, tt * D:(tt + 1) * D],
                                rstd[:, 0:1]))
            ve_xn = p.now("ve")
            p.wait("tensor", "ve", ve_xn)
            p.wait("tensor", "gld")
            xnf_view = xnf[:, :].rearrange("q (nd t) -> q nd t", t=TS)
            for tt in range(TPS):
                p.wait("tensor", "ve")
                fns = [
                    (lambda e, tt=tt, dc=dc: e.transpose(
                        ps_tr[:, dc * 128:(dc + 1) * 128],
                        XNT[:, tt * D + dc * 128: tt * D + (dc + 1) * 128], IDN[:, :]))
                    for dc in range(ND)]
                for i, fn in enumerate(fns):
                    if i == ND - 1:
                        p.inc_after("tensor", "te", 1, fn)
                    else:
                        p.t(fn)
                p.wait("vector", "te")
                p.inc_after("vector", "ve", 1,
                            lambda e, tt=tt, xnf_view=xnf_view: e.tensor_copy(
                                xnf_view[:, :, tt * 128:(tt + 1) * 128],
                                ps_tr[:, :].rearrange("q (nd f) -> q nd f", f=128)))
            ve_xf = p.now("ve")
            p.wait("tensor", "ve", ve_xf)

            for hc in range(NHC):
                p.wait("tensor", "se")
                fns = [
                    (lambda e, hc=hc, dc=dc, xnf=xnf: e.matmul(
                        ps_u[:, :], wuq_view[:, dc, hc, :], xnf[:, dc * TS:(dc + 1) * TS],
                        start=(dc == 0), stop=(dc == ND - 1)))
                    for dc in range(ND)]
                for i, fn in enumerate(fns):
                    if i == ND - 1:
                        p.inc_after("tensor", "te", 1, fn)
                    else:
                        p.t(fn)
                te_u = p.now("te")
                p.wait("scalar", "te", te_u)
                p.wait("scalar", "te")
                p.inc_after("scalar", "se", 1,
                            lambda e: e.activation(HS[:, :], ps_u[:, :], AF.Gelu))
                p.wait("scalar", "se")
                p.inc_after("scalar", "se", 1,
                            lambda e: e.activation(SQH[:, :], HS[:, :], AF.Square))
                se_h = p.now("se")
                p.wait("tensor", "se", se_h)
                p.inc_after("tensor", "te", 1,
                            lambda e, hc=hc: e.matmul(ps_sq[0:1, :], ONEC[:, 0:1], SQH[:, :],
                                                      start=(hc == 0), stop=(hc == NHC - 1)))
                for tt in range(TPS):
                    for half in range(2):
                        fn = lambda e, hc=hc, tt=tt, half=half: e.matmul(
                            ps_d[tt][:, half * 512:(half + 1) * 512],
                            HS[:, tt * 128:(tt + 1) * 128],
                            WDQ[:, hc * D + half * 512: hc * D + half * 512 + 512],
                            start=(hc == 0), stop=(hc == NHC - 1))
                        if tt == TPS - 1 and half == 1:
                            p.inc_after("tensor", "te", 1, fn)
                        else:
                            p.t(fn)
            te_dn = p.now("te")
            p.wait("vector", "te", te_dn)
            p.inc_after("vector", "ve", 1,
                        lambda e: e.tensor_scalar_mul(RROW[0:1, :], ps_sq[0:1, :], 1.0 / H))
            p.wait("scalar", "ve")
            p.inc_after("scalar", "se", 1,
                        lambda e: e.activation(RROW[0:1, :], RROW[0:1, :], AF.Sqrt))
            p.wait("vector", "se")
            p.inc_after("vector", "ve", 1,
                        lambda e: e.tensor_scalar_add(RROW[0:1, :], RROW[0:1, :], REPS))
            p.wait("vector", "ve")
            p.inc_after("vector", "ve", 1,
                        lambda e: e.reciprocal(RROW[0:1, :], RROW[0:1, :]))
            ve_rr = p.now("ve")
            p.wait("tensor", "ve", ve_rr)
            for tt in range(TPS):
                p.inc_after("tensor", "te", 1,
                            lambda e, tt=tt: e.matmul(
                                ps_u[:, tt:tt + 1],
                                RROW[0:1, tt * 128:(tt + 1) * 128], ONE1[0:1, 0:1],
                                start=True, stop=True))
            te_rt = p.now("te")
            p.wait("vector", "te", te_rt)
            p.inc_after("vector", "ve", 1,
                        lambda e: e.tensor_copy(RSTDC[:, 0:TPS], ps_u[:, 0:TPS]))
            ve_rt = p.now("ve")

            for tt in range(TPS):
                p.wait("vector", "te", te_dn)
                p.wait("vector", "ve", ve_rt)
                p.wait("vector", "ld", ld_s)
                p.inc_after("vector", "ve", 1,
                            lambda e, tt=tt: e.scalar_tensor_tensor(
                                XS32[:, tt * D:(tt + 1) * D], ps_d[tt][:, :],
                                RSTDC[:, tt:tt + 1], XS32[:, tt * D:(tt + 1) * D],
                                ALU.mult, ALU.add))
                p.wait("scalar", "st")
                p.wait("vector", "st")
                emit_ln_tile(p, XS32[:, tt * D:(tt + 1) * D], OUT[:, 0:D], stats,
                             sq_scr=OUT[:, 0:D])
                ve_out = p.now("ve")
                p.wait("sync", "ve", ve_out)
                p.inc_after("sync", "st", 16,
                            lambda e, s=s, tt=tt: e.dma_start(out=yo_t[s * TPS + tt],
                                                              in_=OUT[:, 0:D]))

        p.wait("sync", "st")
        p.run()
    return nc



def build_l3_ffn(T_=None):
    Tl = T_ or T
    NTl = Tl // 128
    NQS = Tl // 512

    nc = bass.Bass("TRN2", target_bir_lowering=False)
    x = nc.dram_tensor("x", [Tl, D], F32, kind="ExternalInput").ap()
    wrT = nc.dram_tensor("wrT", [D, D], F32, kind="ExternalInput").ap()
    wkT = nc.dram_tensor("wkT", [D, D], F32, kind="ExternalInput").ap()
    wvT = nc.dram_tensor("wvT", [D, D], F32, kind="ExternalInput").ap()
    mur = nc.dram_tensor("mur", [128, ND], F32, kind="ExternalInput").ap()
    muk = nc.dram_tensor("muk", [128, ND], F32, kind="ExternalInput").ap()
    ident = nc.dram_tensor("c_ident", [128, 128], F32, kind="ExternalInput").ap()
    yo = nc.dram_tensor("y", [Tl, D], F32, kind="ExternalOutput").ap()

    x_t = x.rearrange("(nt p) d -> nt p d", p=128)
    yo_t = yo.rearrange("(nt p) d -> nt p d", p=128)

    with ExitStack() as ctx:
        sbt = lambda *a: ctx.enter_context(nc.sbuf_tensor(*a))
        pst = lambda *a: ctx.enter_context(nc.psum_tensor(*a))
        p = Prog(nc, ctx)

        IDN = sbt("IDN", [128, 128], BF16)
        XF = sbt("XF3", [128, ND * Tl], BF16)
        WRT = sbt("WRT", [128, ND * D], BF16)
        WKT = sbt("WKT", [128, ND * D], BF16)
        WVT = sbt("WVT", [128, ND * D], BF16)
        MUR = sbt("MUR", [128, ND], F32)
        MUK = sbt("MUK", [128, ND], F32)
        OMR = sbt("OMR", [128, ND], F32)
        OMK = sbt("OMK", [128, ND], F32)
        XRFb = [sbt(f"XRF{i}", [128, ND * 512], BF16) for i in range(2)]
        XKFb = [sbt(f"XKF{i}", [128, ND * 512], BF16) for i in range(2)]
        KA = sbt("KA", [128, ND * 512], BF16)
        RT = sbt("RT", [128, 4 * D], BF16)
        RSC = sbt("RSC3", [128, 512], F32)
        KSC = sbt("KSC3", [128, 512], F32)
        SHARE = sbt("SHARE3", [128, 4 * D], BF16)
        XSB = SHARE[:, 0:2 * D]
        S32 = SHARE[:, :].bitcast(F32)
        OUT = sbt("OUT3", [128, D], F32)
        stats = alloc_stats(sbt)

        ps_tr = pst("ps_tr", [128, ND * 128], BF16)
        ps_a = pst("ps_a", [128, 512], F32)
        ps_po = pst("ps_po", [128, D], F32)

        p.inc_after("vector", "ve", 1, lambda e: e.memset(stats["eps"][:, 0:1], LN_EPS))

        p.inc_after("gpsimd", "gld", 16, lambda e: e.dma_start(out=IDN[:, :], in_=ident))
        p.inc_after("gpsimd", "gld", 16,
                    lambda e: e.dma_start(out=WRT[:, :].rearrange("q (dc f) -> q dc f", f=D),
                                          in_=wrT.rearrange("(dc q) f -> q dc f", q=128)))
        p.inc_after("gpsimd", "gld", 16,
                    lambda e: e.dma_start(out=WKT[:, :].rearrange("q (dc f) -> q dc f", f=D),
                                          in_=wkT.rearrange("(dc q) f -> q dc f", q=128)))
        p.inc_after("gpsimd", "gld", 16,
                    lambda e: e.dma_start(out=WVT[:, :].rearrange("q (dc f) -> q dc f", f=D),
                                          in_=wvT.rearrange("(dc q) f -> q dc f", q=128)))
        p.inc_after("sync", "cld", 16, lambda e: e.dma_start(out=MUR[:, :], in_=mur))
        p.inc_after("sync", "cld", 16, lambda e: e.dma_start(out=MUK[:, :], in_=muk))
        cld_c = p.now("cld")
        p.wait("vector", "cld", cld_c)
        p.inc_after("vector", "ve", 1, lambda e: e.tensor_scalar(OMR[:, :], MUR[:, :], -1.0, 1.0,
                                                                 ALU.mult, ALU.add))
        p.inc_after("vector", "ve", 1, lambda e: e.tensor_scalar(OMK[:, :], MUK[:, :], -1.0, 1.0,
                                                                 ALU.mult, ALU.add))

        te_nt = []
        for nt in range(NTl):
            buf = (nt % 2) * D
            par = f"gx{nt % 2}"
            if nt >= 2:
                p.wait("gpsimd", "te", te_nt[nt - 2])
            p.inc_after("gpsimd", par, 16,
                        lambda e, nt=nt, buf=buf: e.dma_start(out=XSB[:, buf:buf + D], in_=x_t[nt]))
            p.wait("tensor", par)
            if nt == 0:
                p.wait("tensor", "gld")
            p.wait("tensor", "ve")
            fns = [
                (lambda e, nt=nt, dc=dc, buf=buf: e.transpose(
                    ps_tr[:, dc * 128:(dc + 1) * 128],
                    XSB[:, buf + dc * 128: buf + (dc + 1) * 128], IDN[:, :]))
                for dc in range(ND)]
            for i, fn in enumerate(fns):
                if i == ND - 1:
                    p.inc_after("tensor", "te", 1, fn)
                else:
                    p.t(fn)
            te_nt.append(p.now("te"))
            p.wait("vector", "te")
            xf_view = XF[:, :].rearrange("q (nd t) -> q nd t", t=Tl)
            p.inc_after("vector", "ve", 1,
                        lambda e, nt=nt, xf_view=xf_view: e.tensor_copy(
                            xf_view[:, :, nt * 128:(nt + 1) * 128],
                            ps_tr[:, :].rearrange("q (nd f) -> q nd f", f=128)))
        ve_xf = p.now("ve")
        p.wait("tensor", "ve", ve_xf)

        for qs in range(NQS):
            t0 = qs * 512
            XRF = XRFb[qs % 2]
            XKF = XKFb[qs % 2]
            p.wait("vector", "ve", ve_xf)
            p.wait("vector", "te")
            for dc in range(ND):
                for (XT, MUc, OMc) in ((XRF, MUR, OMR), (XKF, MUK, OMK)):
                    p.inc_after("vector", "ve", 1,
                                lambda e, dc=dc, XT=XT, MUc=MUc, t0=t0: e.tensor_scalar(
                                    RSC[:, :], XF[:, dc * Tl + t0: dc * Tl + t0 + 512],
                                    MUc[:, dc:dc + 1], None, ALU.mult))
                    p.wait("vector", "ve")
                    if t0 == 0:
                        p.inc_after("vector", "ve", 1,
                                    lambda e, dc=dc, XT=XT, OMc=OMc: e.scalar_tensor_tensor(
                                        XT[:, dc * 512 + 1: dc * 512 + 512],
                                        XF[:, dc * Tl: dc * Tl + 511],
                                        OMc[:, dc:dc + 1], RSC[:, 1:512],
                                        ALU.mult, ALU.add))
                        p.wait("vector", "ve")
                        p.inc_after("vector", "ve", 1,
                                    lambda e, dc=dc, XT=XT: e.tensor_copy(
                                        XT[:, dc * 512: dc * 512 + 1], RSC[:, 0:1]))
                    else:
                        p.inc_after("vector", "ve", 1,
                                    lambda e, dc=dc, XT=XT, OMc=OMc, t0=t0: e.scalar_tensor_tensor(
                                        XT[:, dc * 512: dc * 512 + 512],
                                        XF[:, dc * Tl + t0 - 1: dc * Tl + t0 + 511],
                                        OMc[:, dc:dc + 1], RSC[:, :],
                                        ALU.mult, ALU.add))
                    p.wait("vector", "ve")
            ve_mix = p.now("ve")
            p.wait("tensor", "ve", ve_mix)

            for tt in range(4):
                for half in range(2):
                    if qs > 0 or tt > 0 or half > 0:
                        p.wait("tensor", "se")
                    fns = [
                        (lambda e, tt=tt, half=half, dc=dc, XRF=XRF: e.matmul(
                            ps_a[:, :],
                            XRF[:, dc * 512 + tt * 128: dc * 512 + tt * 128 + 128],
                            WRT[:, dc * D + half * 512: dc * D + half * 512 + 512],
                            start=(dc == 0), stop=(dc == ND - 1)))
                        for dc in range(ND)]
                    for i, fn in enumerate(fns):
                        if i == ND - 1:
                            p.inc_after("tensor", "te", 1, fn)
                        else:
                            p.t(fn)
                    p.wait("scalar", "te")
                    p.inc_after("scalar", "se", 1,
                                lambda e, tt=tt, half=half: e.activation(
                                    RT[:, tt * D + half * 512: tt * D + half * 512 + 512],
                                    ps_a[:, :], AF.Sigmoid))
            for fc in range(ND):
                p.wait("tensor", "ve")
                p.wait("tensor", "se")
                fns = [
                    (lambda e, fc=fc, dc=dc, XKF=XKF: e.matmul(
                        ps_a[:, :], WKT[:, dc * D + fc * 128: dc * D + fc * 128 + 128],
                        XKF[:, dc * 512:(dc + 1) * 512],
                        start=(dc == 0), stop=(dc == ND - 1)))
                    for dc in range(ND)]
                for i, fn in enumerate(fns):
                    if i == ND - 1:
                        p.inc_after("tensor", "te", 1, fn)
                    else:
                        p.t(fn)
                p.wait("vector", "te")
                p.inc_after("vector", "ve", 1,
                            lambda e: e.tensor_scalar_max(KSC[:, :], ps_a[:, :], 0.0))
                p.wait("scalar", "ve")
                p.inc_after("scalar", "se", 1,
                            lambda e, fc=fc: e.activation(KA[:, fc * 512:(fc + 1) * 512],
                                                          KSC[:, :], AF.Square))
            se_ka = p.now("se")
            p.wait("tensor", "se", se_ka)

            for tt in range(4):
                nt = qs * 4 + tt
                p.wait("sync", "ve")
                p.inc_after("sync", "ld", 16,
                            lambda e, nt=nt: e.dma_start(out=S32[:, 0:D], in_=x_t[nt]))
                ld_nt = p.now("ld")
                if nt > 0:
                    p.wait("tensor", "ve")
                for half in range(2):
                    fns = [
                        (lambda e, tt=tt, half=half, fc=fc: e.matmul(
                            ps_po[:, half * 512:(half + 1) * 512],
                            KA[:, fc * 512 + tt * 128: fc * 512 + tt * 128 + 128],
                            WVT[:, fc * D + half * 512: fc * D + half * 512 + 512],
                            start=(fc == 0), stop=(fc == ND - 1)))
                        for fc in range(ND)]
                    for i, fn in enumerate(fns):
                        if i == ND - 1:
                            p.inc_after("tensor", "te", 1, fn)
                        else:
                            p.t(fn)
                te_po = p.now("te")
                p.wait("vector", "te", te_po)
                p.wait("vector", "ld", ld_nt)
                p.wait("vector", "se")
                p.wait("vector", "st")
                p.wait("scalar", "st")
                p.inc_after("vector", "ve", 1,
                            lambda e, tt=tt: e.tensor_tensor(OUT[:, 0:D], RT[:, tt * D:(tt + 1) * D],
                                                             ps_po[:, :], ALU.mult))
                p.wait("vector", "ve")
                p.inc_after("vector", "ve", 1,
                            lambda e: e.tensor_tensor(S32[:, 0:D], S32[:, 0:D], OUT[:, 0:D],
                                                      ALU.add))
                emit_ln_tile(p, S32[:, 0:D], OUT[:, 0:D], stats, sq_scr=OUT[:, 0:D])
                ve_out = p.now("ve")
                p.wait("sync", "ve", ve_out)
                p.inc_after("sync", "st", 16,
                            lambda e, nt=nt: e.dma_start(out=yo_t[nt], in_=OUT[:, 0:D]))

        p.wait("sync", "st")
        p.run()
    return nc



def make_consts():
    return {"c_ident": np.eye(128, dtype=np.float32)}


def _rope_consts(Tl):
    theta = 1.0 / 10000.0 ** (np.arange(0, 64, 2, dtype=np.float32) / 64)
    COSC = np.zeros((128, 8), np.float32)
    SINC = np.zeros((128, 8), np.float32)
    for col in range(8):
        for p_ in range(128):
            r = col * 128 + p_
            h = r // 64
            j = (r % 64) // 2
            ang = np.float32(h) * theta[j]
            COSC[p_, col] = np.cos(ang)
            SINC[p_, col] = (-np.sin(ang)) if (r % 2 == 0) else np.sin(ang)
    perm = np.zeros((128, 128), np.float32)
    for i in range(0, 128, 2):
        perm[i + 1, i] = 1.0
        perm[i, i + 1] = 1.0
    mask = np.triu(np.ones((128, 128), np.float32))
    return dict(c_cosc=COSC, c_sinc=SINC, c_perm=perm, c_mask=mask)


def _l2_consts():
    NH = 8
    gammas = 1.0 - np.exp(np.linspace(np.log(1 / 512), np.log(1 / 32), NH))
    GP = np.zeros((128, NH * 36), np.float32)
    for h in range(NH):
        for mm in range(-3, 33):
            e = 128.0 * mm - np.arange(128)
            GP[:, h * 36 + mm + 3] = np.exp(e * np.log(gammas[h]))
    RAMP = np.zeros((128, NH * 512), np.float32)
    for h in range(NH):
        RAMP[:, h * 512:(h + 1) * 512] = (gammas[h] ** np.arange(512))[None, :]
    return dict(c_gp=GP, c_ramp=RAMP, c_mask=np.triu(np.ones((128, 128), np.float32)))


def _ehbc():
    ehb = np.zeros((32, 16 * 128), np.float32)
    ehc = np.zeros((32, 16 * 128), np.float32)
    for n in range(16):
        ehb[n, n * 128:(n + 1) * 128] = 1.0
        ehc[16 + n, n * 128:(n + 1) * 128] = 1.0
    return dict(c_ehb=ehb, c_ehc=ehc)


def _col8(v):
    return np.ascontiguousarray(v.reshape(8, 128).T)


def _col16(v):
    return np.ascontiguousarray(v.reshape(16, 128).T)


def _ct(a):
    return np.ascontiguousarray(a)



def kernel(**inputs):
    ii = inputs
    consts = make_consts()

    w_mamba = dict(
        lnw=_ct(ii["L0_attn_ln_w"][None]), lnb=_ct(ii["L0_attn_ln_b"][None]),
        wipT=_ct(ii["L0_attn_in_proj_w"].T),
        convw=_ct(ii["L0_attn_conv_w"].reshape(2048, 4).reshape(16, 128, 4)
                  .transpose(1, 0, 2).reshape(128, 64)),
        convb=_col16(ii["L0_attn_conv_b"]),
        wxpT=_ct(ii["L0_attn_x_proj_w"].T),
        wdtT=_ct(ii["L0_attn_dt_proj_w"].T),
        dtb=_col16(ii["L0_attn_dt_proj_b"]),
        alog=_ct(ii["L0_attn_A_log"].reshape(16, 128, 16).transpose(1, 0, 2)
                 .reshape(128, 256)),
        dmat=_col16(ii["L0_attn_D"]),
        wopT=_ct(ii["L0_attn_out_proj_w"].T),
        **_ehbc(), **consts)
    w_ffn0 = dict(wgT=_ct(ii["L0_ffn_w_gate"].T), wuT=_ct(ii["L0_ffn_w_up"].T),
                  wdT=_ct(ii["L0_ffn_w_down"].T), **consts)
    w_attn1 = dict(wqkv=_ct(ii["L1_attn_W_qkv"]), bqkv=_ct(ii["L1_attn_b_qkv"][None]),
                   bqkvc=_ct(ii["L1_attn_b_qkv"].reshape(24, 128).T),
                   wproj=_ct(ii["L1_attn_W_proj"]), bproj=_ct(ii["L1_attn_b_proj"][None]),
                   aqkv=_ct(ii["L1_attn_A_qkv"]), bl_qkv=_ct(ii["L1_attn_B_qkv"]),
                   aproj=_ct(ii["L1_attn_A_proj"]), bl_proj=_ct(ii["L1_attn_B_proj"]),
                   **consts)
    w_ffn1 = dict(wgT=_ct(ii["L1_ffn_w_gate"].T), wuT=_ct(ii["L1_ffn_w_up"].T),
                  wdT=_ct(ii["L1_ffn_w_down"].T), **consts)
    w_attn2 = dict(wq=_ct(ii["L2_attn_Wq"].transpose(1, 0, 2).reshape(128, 8 * 128)),
                   wk=_ct(ii["L2_attn_Wk"].transpose(1, 0, 2).reshape(128, 8 * 128)),
                   wv=_ct(ii["L2_attn_Wv"].transpose(1, 0, 2).reshape(128, 8 * 128)),
                   w1=_ct(ii["L2_attn_W1"]), w2=_ct(ii["L2_attn_W2"]),
                   gnw=_col8(ii["L2_attn_gn_w"]), gnb=_col8(ii["L2_attn_gn_b"]),
                   **_l2_consts(), **consts)
    w_ffn2 = dict(wupT=_ct(ii["L2_ffn_w_up"].T), wdnT=_ct(ii["L2_ffn_w_down"].T), **consts)
    w_attn3 = dict(wqkv=_ct(ii["L3_attn_W_attn"]), bqkv=_ct(ii["L3_attn_b_attn"][None]),
                   bqkvc=_ct(ii["L3_attn_b_attn"].reshape(24, 128).T),
                   wproj=_ct(ii["L3_attn_W_proj"]), bproj=_ct(ii["L3_attn_b_proj"][None]),
                   **_rope_consts(T), **consts)
    w_ffn3 = dict(wrT=_ct(ii["L3_ffn_W_r"].T), wkT=_ct(ii["L3_ffn_W_k"].T),
                  wvT=_ct(ii["L3_ffn_W_v"].T),
                  mur=_col8(ii["L3_ffn_mu_r"]), muk=_col8(ii["L3_ffn_mu_k"]),
                  **consts)

    nc_ffn = build_ffn_int8()
    launches = [
        (build_mamba(), w_mamba),
        (nc_ffn, w_ffn0),
        (build_attn("l1"), w_attn1),
        (nc_ffn, w_ffn1),
        (build_l2_attn(), w_attn2),
        (build_l2_ffn(), w_ffn2),
        (build_attn("l3"), w_attn3),
        (build_l3_ffn(), w_ffn3),
    ]

    xs = [np.ascontiguousarray(ii["L0_attn_x"][b]) for b in range(8)]
    for nc, wmap in launches:
        in_maps = [dict(x=xs[b], **wmap) for b in range(8)]
        res = run_bass_kernel_spmd(nc, in_maps, core_ids=list(range(8)))
        xs = [np.asarray(res.results[b]["y"]) for b in range(8)]
    return np.stack(xs, axis=0)


# version 04
# speedup vs baseline: 1.0032x; 1.0032x over previous
import numpy as np
from contextlib import ExitStack

import concourse.bass as bass
import concourse.mybir as mybir
from concourse.bass_utils import run_bass_kernel_spmd

F32 = mybir.dt.float32
BF16 = mybir.dt.bfloat16
I32 = mybir.dt.int32
AF = mybir.ActivationFunctionType
ALU = mybir.AluOpType
AX = mybir.AxisListType

T = 4096
D = 1024
NT = T // 128
ND = D // 128

LN_EPS = 1e-5


class Ctr:
    def __init__(self):
        self.v = 0

    def add(self, n):
        self.v += n
        return self.v


class Prog:

    ENGS = ("tensor", "vector", "scalar", "sync", "gpsimd")

    def __init__(self, nc, ctx):
        self.nc = nc
        self.ctx = ctx
        self.streams = {e: [] for e in self.ENGS}
        self.sems = {}
        self.ctrs = {}
        self.pfx = ""

    def sem(self, name):
        if name not in self.sems:
            self.sems[name] = self.ctx.enter_context(self.nc.semaphore(name))
            self.ctrs[name] = Ctr()
        return self.sems[name]

    def emit(self, eng, fn):
        self.streams[eng].append(fn)

    def wait(self, eng, sem_name, val=None):
        s = self.sem(sem_name)
        v = self.ctrs[sem_name].v if val is None else val
        if v > 0:
            self.emit(eng, lambda e, s=s, v=v: e.wait_ge(s, v))
        return v

    def inc_after(self, eng, sem_name, n, fn):
        s = self.sem(sem_name)
        self.ctrs[sem_name].add(n)
        self.emit(eng, lambda e, s=s, n=n, fn=fn: fn(e).then_inc(s, n))

    def now(self, sem_name):
        self.sem(sem_name)
        return self.ctrs[sem_name].v

    def barrier(self):
        s = self.sem("bar")
        self.ctrs["bar"].add(len(self.ENGS))
        target = self.ctrs["bar"].v
        for eng in self.ENGS:
            self.emit(eng, lambda e, s=s: e.sem_inc(s, 1))
        for eng in self.ENGS:
            self.emit(eng, lambda e, s=s, v=target: e.wait_ge(s, v))

    def run(self):
        block = self.ctx.enter_context(self.nc.Block())
        for eng in self.ENGS:
            fns = self.streams[eng]
            getattr(block, eng)(lambda e, fns=fns: [f(e) for f in fns])

    def t(self, fn):
        self.emit("tensor", fn)

    def v(self, fn):
        self.emit("vector", fn)

    def s(self, fn):
        self.emit("scalar", fn)


def emit_ln_tile(p, s32_ap, out_ap, stats, sq_scr=None):
    ssq, ssum, mean, var, rstd = (
        stats["ssq"], stats["ssum"], stats["mean"], stats["var"], stats["rstd"])
    if sq_scr is None:
        sq_scr = stats["sq_scr"][:, :]
    p.wait("scalar", "ve")
    p.inc_after("scalar", "se", 1,
                lambda e: e.activation(sq_scr, s32_ap, AF.Square, accum_out=ssq[:, 0:1]))
    p.wait("vector", "ve")
    p.inc_after("vector", "ve", 1,
                lambda e: e.reduce_sum(ssum[:, 0:1], s32_ap, axis=AX.X))
    p.wait("vector", "ve")
    p.wait("vector", "se")
    p.inc_after("vector", "ve", 1,
                lambda e: e.tensor_scalar_mul(mean[:, 0:1], ssum[:, 0:1], 1.0 / D))
    p.wait("vector", "ve")
    p.inc_after("vector", "ve", 1,
                lambda e: e.tensor_scalar_mul(var[:, 0:1], ssq[:, 0:1], 1.0 / D))
    p.wait("vector", "ve")
    p.inc_after("vector", "ve", 1,
                lambda e: e.tensor_mul(ssum[:, 0:1], mean[:, 0:1], mean[:, 0:1]))
    p.wait("vector", "ve")
    p.inc_after("vector", "ve", 1,
                lambda e: e.tensor_sub(var[:, 0:1], var[:, 0:1], ssum[:, 0:1]))
    p.wait("scalar", "ve")
    p.inc_after("scalar", "se", 1,
                lambda e: e.activation(rstd[:, 0:1], var[:, 0:1], AF.Sqrt,
                                       bias=stats["eps"][:, 0:1]))
    p.wait("vector", "se")
    p.inc_after("vector", "ve", 1,
                lambda e: e.reciprocal(rstd[:, 0:1], rstd[:, 0:1]))
    p.wait("vector", "ve")
    p.inc_after("vector", "ve", 1,
                lambda e: e.tensor_scalar(out_ap, s32_ap, mean[:, 0:1], rstd[:, 0:1],
                                          ALU.subtract, ALU.mult))


def alloc_stats(sbt, tag=""):
    return dict(
        ssq=sbt("ssq" + tag, [128, 1], F32),
        ssum=sbt("ssum" + tag, [128, 1], F32),
        mean=sbt("mean" + tag, [128, 1], F32),
        var=sbt("var" + tag, [128, 1], F32),
        rstd=sbt("rstd" + tag, [128, 1], F32),
        eps=sbt("eps" + tag, [128, 1], F32),
    )



def build_ffn_int8(H=3072, T_=None, p=None, dram=None):
    T = T_ or globals()["T"]
    NH = H // 128
    TS = 256
    NS = T // TS
    TPS = TS // 128

    standalone = p is None
    if standalone:
        _nc = bass.Bass("TRN2", target_bir_lowering=False)
        _octx = ExitStack()
        _octx.__enter__()
        p = Prog(_nc, _octx)
        p.pfx = ""
        dram = lambda name, shape, dt, kind: _nc.dram_tensor(name, shape, dt, kind=kind).ap()
    nc = p.nc
    x = dram("x", [T, D], F32, "ExternalInput")
    wgT = dram("wgT", [D, H], I32, "ExternalInput")
    wuT = dram("wuT", [D, H], I32, "ExternalInput")
    wdT = dram("wdT", [H, D], I32, "ExternalInput")
    ident = dram("c_ident", [128, 128], F32, "ExternalInput")
    yo = dram("y", [T, D], F32, "ExternalOutput")

    with ExitStack() as ctx:
        sbt = lambda n, *a: ctx.enter_context(nc.sbuf_tensor(p.pfx + n, *a))
        pst = lambda n, *a: ctx.enter_context(nc.psum_tensor(p.pfx + n, *a))

        IDN = sbt("IDN", [128, 128], BF16)
        WG = sbt("WG", [128, NH * ND * 128], BF16)
        WU = sbt("WU", [128, NH * ND * 128], BF16)
        WD = sbt("WD", [128, NH * D], BF16)
        WSTG = [sbt("WSTG0", [128, 3072], I32)]
        XSB = sbt("XSB", [128, TPS * D], BF16)
        XS32 = sbt("XS32", [128, TPS * D], F32)
        XF = [sbt(f"XF{i}", [128, ND * TS], BF16) for i in range(2)]
        GS = sbt("GS", [128, TS], BF16)
        HS = sbt("HS", [128, TS], BF16)
        S32 = sbt("S32", [128, TPS * D], F32)
        OUT = sbt("OUT", [128, TPS * D], F32)
        stats = alloc_stats(sbt)

        ps_tr = pst("ps_tr", [128, ND * 128], BF16)
        ps_g = pst("ps_g", [128, TS], F32)
        ps_u = pst("ps_u", [128, TS], F32)
        ps_d = [pst(f"ps_d{i}", [128, D], F32) for i in range(TPS)]

        p.inc_after("vector", "ve", 1, lambda e: e.memset(stats["eps"][:, 0:1], LN_EPS))
        p.inc_after("gpsimd", "gld", 16, lambda e: e.dma_start(out=IDN[:, :], in_=ident))

        def load_dequant(wT, dest, nrow_chunks, ncols, kind):
            src = wT.rearrange("(rc p) f -> rc p f", p=128)
            for rc in range(nrow_chunks):
                stg = WSTG[0]
                p.wait("sync", "se")
                p.inc_after("sync", "wld", 16,
                            lambda e, rc=rc, stg=stg, ncols=ncols: e.dma_start(
                                out=stg[:, 0:ncols], in_=src[rc]))
                p.wait("scalar", "wld")
                if kind == "d":
                    out_ap = dest[:, rc * ncols:(rc + 1) * ncols]
                else:
                    out_ap = dest[:, :].rearrange(
                        "p (hc nd f) -> p hc nd f", nd=ND, f=128)[:, :, rc, :]
                p.inc_after("scalar", "se", 1,
                            lambda e, stg=stg, out_ap=out_ap, ncols=ncols:
                            e.activation(out_ap, stg[:, 0:ncols], AF.Copy, scale=1.0 / 127.0))

        load_dequant(wgT, WG, ND, H, "g")
        load_dequant(wuT, WU, ND, H, "u")
        load_dequant(wdT, WD, NH, D, "d")
        se_w = p.now("se")

        x_t = x.rearrange("(nt p) d -> nt p d", p=128)
        yo_t = yo.rearrange("(nt p) d -> nt p d", p=128)

        p.wait("tensor", "se", se_w)

        for s in range(NS):
            xf = XF[s % 2]
            if s > 0:
                p.wait("gpsimd", "te")
                p.wait("sync", "ve")
            for tt in range(TPS):
                nt = s * TPS + tt
                p.inc_after("gpsimd", "gld", 16,
                            lambda e, nt=nt, tt=tt: e.dma_start(
                                out=XSB[:, tt * D:(tt + 1) * D], in_=x_t[nt]))
                p.inc_after("sync", "ld", 16,
                            lambda e, nt=nt, tt=tt: e.dma_start(
                                out=XS32[:, tt * D:(tt + 1) * D], in_=x_t[nt]))
            ld_slab = p.now("ld")
            gld_slab = p.now("gld")

            p.wait("tensor", "ld", ld_slab)
            p.wait("tensor", "gld", gld_slab)
            xf_view = xf[:, :].rearrange("p (nd ts) -> p nd ts", ts=TS)
            for tt in range(TPS):
                p.wait("tensor", "ve")
                for dc in range(ND):
                    fn = lambda e, tt=tt, dc=dc: e.transpose(
                        ps_tr[:, dc * 128:(dc + 1) * 128],
                        XSB[:, tt * D + dc * 128: tt * D + (dc + 1) * 128], IDN[:, :])
                    if dc == ND - 1:
                        p.inc_after("tensor", "te", 1, fn)
                    else:
                        p.t(fn)
                p.wait("vector", "te")
                p.inc_after("vector", "ve", 1,
                            lambda e, xf_view=xf_view, tt=tt: e.tensor_copy(
                                xf_view[:, :, tt * 128:(tt + 1) * 128],
                                ps_tr[:, :].rearrange("p (nd f) -> p nd f", f=128)))
            ve_xf = p.now("ve")

            p.wait("tensor", "ve", ve_xf)
            for hc in range(NH):
                p.wait("tensor", "se")
                p.wait("tensor", "ve")
                for dc in range(ND):
                    fn = lambda e, hc=hc, dc=dc, xf=xf: e.matmul(
                        ps_g[:, :], WG[:, (hc * ND + dc) * 128:(hc * ND + dc + 1) * 128],
                        xf[:, dc * TS:(dc + 1) * TS],
                        start=(dc == 0), stop=(dc == ND - 1))
                    if dc == ND - 1:
                        p.inc_after("tensor", "te", 1, fn)
                    else:
                        p.t(fn)
                for dc in range(ND):
                    fn = lambda e, hc=hc, dc=dc, xf=xf: e.matmul(
                        ps_u[:, :], WU[:, (hc * ND + dc) * 128:(hc * ND + dc + 1) * 128],
                        xf[:, dc * TS:(dc + 1) * TS],
                        start=(dc == 0), stop=(dc == ND - 1))
                    if dc == ND - 1:
                        p.inc_after("tensor", "te", 1, fn)
                    else:
                        p.t(fn)
                te_gu = p.now("te")
                p.wait("scalar", "te", te_gu - 1)
                p.wait("scalar", "ve")
                p.inc_after("scalar", "se", 1,
                            lambda e: e.activation(GS[:, :], ps_g[:, :], AF.Silu))
                p.wait("vector", "te", te_gu)
                p.wait("vector", "se")
                p.inc_after("vector", "ve", 1,
                            lambda e: e.tensor_tensor(HS[:, :], GS[:, :], ps_u[:, :], ALU.mult))
                p.wait("tensor", "ve")
                for i, (tt, half) in enumerate([(a, b) for a in range(TPS) for b in range(2)]):
                    fn = lambda e, hc=hc, tt=tt, half=half: e.matmul(
                        ps_d[tt][:, half * 512:(half + 1) * 512],
                        HS[:, tt * 128:(tt + 1) * 128],
                        WD[:, hc * D + half * 512: hc * D + half * 512 + 512],
                        start=(hc == 0), stop=(hc == NH - 1))
                    if i == 2 * TPS - 1:
                        p.inc_after("tensor", "te", 1, fn)
                    else:
                        p.t(fn)
            te_down = p.now("te")

            for tt in range(TPS):
                nt = s * TPS + tt
                p.wait("vector", "te", te_down)
                p.wait("vector", "ld", ld_slab)
                p.inc_after("vector", "ve", 1,
                            lambda e, tt=tt: e.tensor_tensor(
                                S32[:, tt * D:(tt + 1) * D], XS32[:, tt * D:(tt + 1) * D],
                                ps_d[tt][:, :], ALU.add))
                p.wait("scalar", "st")
                p.wait("vector", "st")
                emit_ln_tile(p, S32[:, tt * D:(tt + 1) * D], OUT[:, tt * D:(tt + 1) * D], stats,
                             sq_scr=OUT[:, tt * D:(tt + 1) * D])
                ve_out = p.now("ve")
                p.wait("sync", "ve", ve_out)
                p.inc_after("sync", "st", 16,
                            lambda e, nt=nt, tt=tt: e.dma_start(
                                out=yo_t[nt], in_=OUT[:, tt * D:(tt + 1) * D]))

        p.wait("sync", "st")
        p.barrier()
    if standalone:
        p.run()
        _octx.__exit__(None, None, None)
    return nc




def build_attn(kind, T_=None, NHEADS=16, HG=4, p=None, dram=None):
    Tl = T_ or T
    NTl = Tl // 128
    HD = 64
    NG = NHEADS // HG
    GR = HG * HD
    GC = GR // 128
    NQS = Tl // 512
    F = 3 * D
    lora = kind == "l1"

    standalone = p is None
    if standalone:
        _nc = bass.Bass("TRN2", target_bir_lowering=False)
        _octx = ExitStack()
        _octx.__enter__()
        p = Prog(_nc, _octx)
        p.pfx = ""
        dram = lambda name, shape, dt, kind: _nc.dram_tensor(name, shape, dt, kind=kind).ap()
    nc = p.nc
    x = dram("x", [Tl, D], F32, "ExternalInput")
    wqkv = dram("wqkv", [D, F], F32, "ExternalInput")
    bqkv = dram("bqkv", [1, F], F32, "ExternalInput")
    bqkvc = dram("bqkvc", [128, F // 128], F32, "ExternalInput")
    wproj = dram("wproj", [D, D], F32, "ExternalInput")
    bproj = dram("bproj", [1, D], F32, "ExternalInput")
    if lora:
        aqkv = dram("aqkv", [D, 8], F32, "ExternalInput")
        bl_qkv = dram("bl_qkv", [8, F], F32, "ExternalInput")
        aproj = dram("aproj", [D, 8], F32, "ExternalInput")
        bl_proj = dram("bl_proj", [8, D], F32, "ExternalInput")
    else:
        ccos = dram("c_cosc", [128, 8], F32, "ExternalInput")
        csin = dram("c_sinc", [128, 8], F32, "ExternalInput")
        cperm = dram("c_perm", [128, 128], F32, "ExternalInput")
        cmask = dram("c_mask", [128, 128], F32, "ExternalInput")
    ident = dram("c_ident", [128, 128], F32, "ExternalInput")
    yo = dram("y", [Tl, D], F32, "ExternalOutput")

    x_t = x.rearrange("(nt p) d -> nt p d", p=128)
    yo_t = yo.rearrange("(nt p) d -> nt p d", p=128)

    with ExitStack() as ctx:
        sbt = lambda n, *a: ctx.enter_context(nc.sbuf_tensor(p.pfx + n, *a))
        pst = lambda n, *a: ctx.enter_context(nc.psum_tensor(p.pfx + n, *a))

        IDN = sbt("IDN", [128, 128], BF16)
        XF = sbt("XFa", [128, ND * Tl], BF16)
        OF = sbt("OFa", [128, ND * Tl], BF16)
        WQG = sbt("WQG", [128, ND * 2 * GR], BF16)
        KGW = sbt("KGW", [128, max(GC * Tl, ND * D)], BF16)
        KG = KGW
        WPROJ = KGW
        VT = sbt("VT", [128, NTl * HG * 65], BF16)
        QF = sbt("QF", [128, GC * 512], BF16)
        BQC = sbt("BQC", [128, F // 128], F32)
        BVB = sbt("BVB", [128, GR], F32)
        BVR = sbt("BVR", [1, GR], F32)
        PBJ = sbt("PBJ", [128, 1024], BF16)
        PB = [PBJ[:, 0:512], PBJ[:, 512:1024]]
        OS = sbt("OS", [128, 512], BF16)
        OUT = sbt("OUTa", [128, D], F32)
        ONE1 = sbt("ONE1", [1, 128], F32)
        SHARE = sbt("SHARE", [128, 4 * D], BF16)
        XSB = SHARE[:, 0:2 * D]
        S32 = SHARE[:, :].bitcast(F32)
        BPB = sbt("BPB", [128, D], BF16)
        BPR = sbt("BPR", [1, D], F32)
        DEN = BPR[0:1, 0:512]
        RECIP = BPR[0:1, 512:1024]
        stats = alloc_stats(sbt)
        if lora:
            AQ = sbt("AQ", [128, ND * 8], BF16)
            BLG = sbt("BLG", [8, 3 * GR], BF16)
            APJ = sbt("APJ", [128, ND * 8], BF16)
            BLP = sbt("BLP", [8, D], BF16)
            MIDB = sbt("MIDB", [8, Tl], BF16)
        else:
            COS = sbt("COSC", [128, 8], F32)
            SIN = sbt("SINC", [128, 8], F32)
            PERM = sbt("PERM", [128, 128], BF16)
            MASK = sbt("MASKT", [128, 128], BF16)
            RSC = PBJ[:, :].bitcast(F32)

        ps_tr = pst("ps_tr", [128, ND * 128], BF16)
        ps_a = pst("ps_a", [128, 512], F32)
        ps_st = [pst(f"ps_st{i}", [128, 512], F32) for i in range(2)]
        ps_o = pst("ps_o", [128, 512], F32)
        ps_rb = pst("ps_rb", [128, 512], F32)
        ps_po = pst("ps_po", [128, D], F32)

        def mm_chain(fns, extra=None):
            all_fns = fns + ([extra] if extra is not None else [])
            for i, fn in enumerate(all_fns):
                if i == len(all_fns) - 1:
                    p.inc_after("tensor", "te", 1, fn)
                else:
                    p.t(fn)
            return p.now("te")

        p.inc_after("vector", "ve", 1, lambda e: e.memset(stats["eps"][:, 0:1], LN_EPS))
        p.inc_after("vector", "ve", 1, lambda e: e.memset(ONE1[:, :], 1.0))
        p.inc_after("vector", "ve", 1, lambda e: e.memset(
            VT[:, :].rearrange("q (nt h c) -> q nt h c", h=HG, c=65)[:, :, :, 64:65], 1.0))

        p.inc_after("gpsimd", "gld", 16, lambda e: e.dma_start(out=IDN[:, :], in_=ident))
        p.inc_after("sync", "ld", 16, lambda e: e.dma_start(out=BQC[:, :], in_=bqkvc))
        if lora:
            p.inc_after("gpsimd", "gld", 16, lambda e: e.dma_start(
                out=AQ[:, :].rearrange("p (dc r) -> p dc r", r=8),
                in_=aqkv.rearrange("(dc p) r -> p dc r", p=128)))
            p.inc_after("gpsimd", "gld", 16, lambda e: e.dma_start(
                out=APJ[:, :].rearrange("p (dc r) -> p dc r", r=8),
                in_=aproj.rearrange("(dc p) r -> p dc r", p=128)))
            p.inc_after("gpsimd", "gld", 16, lambda e: e.dma_start(out=BLP[:, :], in_=bl_proj))
        else:
            p.inc_after("sync", "ld", 16, lambda e: e.dma_start(out=COS[:, :], in_=ccos))
            p.inc_after("sync", "ld", 16, lambda e: e.dma_start(out=SIN[:, :], in_=csin))
            p.inc_after("gpsimd", "gld", 16, lambda e: e.dma_start(out=PERM[:, :], in_=cperm))
            p.inc_after("gpsimd", "gld", 16, lambda e: e.dma_start(out=MASK[:, :], in_=cmask))

        te_nt = []
        for nt in range(NTl):
            buf = (nt % 2) * D
            par = f"gx{nt % 2}"
            if nt >= 2:
                p.wait("gpsimd", "te", te_nt[nt - 2])
            p.inc_after("gpsimd", par, 16,
                        lambda e, nt=nt, buf=buf: e.dma_start(out=XSB[:, buf:buf + D], in_=x_t[nt]))
            p.wait("tensor", par)
            if nt == 0:
                p.wait("tensor", "gld")
            p.wait("tensor", "ve")
            fns = [
                (lambda e, nt=nt, dc=dc, buf=buf: e.transpose(
                    ps_tr[:, dc * 128:(dc + 1) * 128],
                    XSB[:, buf + dc * 128: buf + (dc + 1) * 128], IDN[:, :]))
                for dc in range(ND)]
            te_nt.append(mm_chain(fns))
            p.wait("vector", "te")
            xf_view = XF[:, :].rearrange("q (nd t) -> q nd t", t=Tl)
            p.inc_after("vector", "ve", 1,
                        lambda e, nt=nt, xf_view=xf_view: e.tensor_copy(
                            xf_view[:, :, nt * 128:(nt + 1) * 128],
                            ps_tr[:, :].rearrange("q (nd f) -> q nd f", f=128)))
        ve_xf = p.now("ve")
        p.wait("tensor", "ve", ve_xf)

        if lora:
            for tc in range(Tl // 512):
                if tc > 0:
                    p.wait("tensor", "se")
                mm_chain([
                    (lambda e, tc=tc, dc=dc: e.matmul(
                        ps_a[0:8, :], AQ[:, dc * 8:(dc + 1) * 8],
                        XF[:, dc * Tl + tc * 512: dc * Tl + tc * 512 + 512],
                        start=(dc == 0), stop=(dc == ND - 1)))
                    for dc in range(ND)])
                p.wait("scalar", "te")
                p.inc_after("scalar", "se", 1,
                            lambda e, tc=tc: e.activation(
                                MIDB[0:8, tc * 512:(tc + 1) * 512], ps_a[0:8, :],
                                AF.Copy, scale=2.0))
        se_mid = p.now("se")
        p.wait("tensor", "se", se_mid)

        def emit_rope(dst_sl, col):
            tgt, sl = dst_sl
            p.wait("vector", "te")
            p.wait("tensor", "ve")
            p.inc_after("tensor", "te", 1,
                        lambda e, tgt=tgt, sl=sl: e.matmul(ps_rb[:, :], PERM[:, :], tgt[:, sl],
                                                           start=True, stop=True))
            p.wait("vector", "te")
            p.wait("vector", "ld")
            p.inc_after("vector", "ve", 1,
                        lambda e, tgt=tgt, sl=sl, col=col: e.tensor_scalar(
                            RSC[:, :], tgt[:, sl], COS[:, col:col + 1], None, ALU.mult))
            p.wait("vector", "ve")
            p.inc_after("vector", "ve", 1,
                        lambda e, tgt=tgt, sl=sl, col=col: e.scalar_tensor_tensor(
                            tgt[:, sl], ps_rb[:, :], SIN[:, col:col + 1], RSC[:, :],
                            ALU.mult, ALU.add))

        for g in range(NG):
            p.wait("gpsimd", "te")
            for si, base in ((0, D + g * GR), (1, 2 * D + g * GR)):
                p.inc_after("gpsimd", "gw", 16,
                            lambda e, si=si, base=base: e.dma_start(
                                out=WQG[:, :].rearrange("q (nd s c) -> q nd s c", s=2, c=GR)[:, :, si, :],
                                in_=wqkv.rearrange("(dc q) f -> q dc f", q=128)[:, :, base:base + GR]))
            if lora:
                for si, base in enumerate((g * GR, D + g * GR, 2 * D + g * GR)):
                    p.inc_after("gpsimd", "gw", 16,
                                lambda e, si=si, base=base: e.dma_start(
                                    out=BLG[0:8, si * GR:(si + 1) * GR],
                                    in_=bl_qkv[0:8, base:base + GR]))
            p.wait("sync", "ve")
            p.inc_after("sync", "ld", 16,
                        lambda e, g=g: e.dma_start(out=BVR[0:1, :],
                                                   in_=bqkv[0:1, 2 * D + g * GR: 2 * D + g * GR + GR]))
            ld_bv = p.now("ld")
            gld_w = p.now("gw")
            p.wait("tensor", "ld", ld_bv)
            p.wait("tensor", "gw", gld_w)
            p.wait("tensor", "ve")
            mm_chain([lambda e: e.matmul(ps_rb[:, 0:GR], ONE1[0:1, :], BVR[0:1, :],
                                         start=True, stop=True)])
            p.wait("vector", "te")
            p.inc_after("vector", "ve", 1,
                        lambda e: e.tensor_copy(BVB[:, :], ps_rb[:, 0:GR]))
            ve_bvb = p.now("ve")

            for cc in range(GC):
                for tc in range(Tl // 512):
                    p.wait("tensor", "ve")
                    extra = None
                    if lora:
                        extra = (lambda e, cc=cc, tc=tc: e.matmul(
                            ps_a[:, :], BLG[0:8, GR + cc * 128: GR + cc * 128 + 128],
                            MIDB[0:8, tc * 512:(tc + 1) * 512], start=False, stop=True))
                    mm_chain([
                        (lambda e, cc=cc, tc=tc, dc=dc: e.matmul(
                            ps_a[:, :], WQG[:, dc * 2 * GR + cc * 128: dc * 2 * GR + cc * 128 + 128],
                            XF[:, dc * Tl + tc * 512: dc * Tl + tc * 512 + 512],
                            start=(dc == 0), stop=(dc == ND - 1 and not lora)))
                        for dc in range(ND)], extra)
                    p.wait("vector", "te")
                    fglob = D + g * GR + cc * 128
                    p.inc_after("vector", "ve", 1,
                                lambda e, cc=cc, tc=tc, fglob=fglob: e.tensor_scalar_add(
                                    KG[:, cc * Tl + tc * 512: cc * Tl + tc * 512 + 512],
                                    ps_a[:, :], BQC[:, fglob // 128: fglob // 128 + 1]))
            se_k = p.now("ve")
            if not lora:
                p.wait("vector", "ve", se_k)
                for cc in range(GC):
                    for tc in range(Tl // 512):
                        emit_rope((KG, slice(cc * Tl + tc * 512, cc * Tl + tc * 512 + 512)),
                                  g * GC + cc)
                p.wait("tensor", "ve")

            p.wait("vector", "ve", ve_bvb)
            for kt in range(NTl):
                p.wait("tensor", "ve")
                extra = None
                if lora:
                    extra = (lambda e, kt=kt: e.matmul(
                        ps_st[0][:, 0:GR], MIDB[0:8, kt * 128:(kt + 1) * 128],
                        BLG[0:8, 2 * GR: 3 * GR], start=False, stop=True))
                mm_chain([
                    (lambda e, kt=kt, dc=dc: e.matmul(
                        ps_st[0][:, 0:GR], XF[:, dc * Tl + kt * 128: dc * Tl + kt * 128 + 128],
                        WQG[:, dc * 2 * GR + GR: dc * 2 * GR + 2 * GR],
                        start=(dc == 0), stop=(dc == ND - 1 and not lora)))
                    for dc in range(ND)], extra)
                p.wait("vector", "te")
                vt_view = VT[:, :].rearrange("q (nt h c) -> q nt h c", h=HG, c=65)
                p.inc_after("vector", "ve", 1,
                            lambda e, kt=kt, vt_view=vt_view: e.tensor_tensor(
                                vt_view[:, kt, :, 0:64],
                                ps_st[0][:, 0:GR].rearrange("q (h c) -> q h c", c=64),
                                BVB[:, :].rearrange("q (h c) -> q h c", c=64), ALU.add))
            ve_v = p.now("ve")
            p.wait("tensor", "ve", ve_v)

            p.wait("gpsimd", "te")
            p.inc_after("gpsimd", "gw", 16,
                        lambda e, g=g: e.dma_start(
                            out=WQG[:, :].rearrange("q (nd s c) -> q nd s c", s=2, c=GR)[:, :, 0, :],
                            in_=wqkv.rearrange("(dc q) f -> q dc f", q=128)[:, :, g * GR: g * GR + GR]))
            gw_q = p.now("gw")
            p.wait("tensor", "gw", gw_q)

            for qs in range(NQS):
                for cc in range(GC):
                    p.wait("tensor", "ve")
                    extra = None
                    if lora:
                        extra = (lambda e, cc=cc, qs=qs: e.matmul(
                            ps_a[:, :], BLG[0:8, cc * 128: cc * 128 + 128],
                            MIDB[0:8, qs * 512:(qs + 1) * 512], start=False, stop=True))
                    mm_chain([
                        (lambda e, cc=cc, dc=dc, qs=qs: e.matmul(
                            ps_a[:, :], WQG[:, dc * 2 * GR + cc * 128: dc * 2 * GR + cc * 128 + 128],
                            XF[:, dc * Tl + qs * 512: dc * Tl + qs * 512 + 512],
                            start=(dc == 0), stop=(dc == ND - 1 and not lora)))
                        for dc in range(ND)], extra)
                    p.wait("vector", "te")
                    fglob = g * GR + cc * 128
                    p.inc_after("vector", "ve", 1,
                                lambda e, cc=cc, fglob=fglob: e.tensor_scalar_add(
                                    QF[:, cc * 512:(cc + 1) * 512], ps_a[:, :],
                                    BQC[:, fglob // 128: fglob // 128 + 1]))
                ve_q = p.now("ve")
                if not lora:
                    p.wait("vector", "ve", ve_q)
                    for cc in range(GC):
                        emit_rope((QF, slice(cc * 512, cc * 512 + 512)), g * GC + cc)
                p.wait("tensor", "ve")

                for hh in range(HG):
                    cc = hh // 2
                    rb = (hh % 2) * 64
                    kts = list(range(NTl)) if lora else list(range(min(NTl, qs * 4 + 4)))
                    for i, kt in enumerate(kts):
                        if i >= 2:
                            p.wait("tensor", "se")
                        p.inc_after("tensor", "te", 1,
                                    lambda e, kt=kt, cc=cc, rb=rb, i=i: e.matmul(
                                        ps_st[i % 2][:, :],
                                        KG[rb:rb + 64, cc * Tl + kt * 128: cc * Tl + kt * 128 + 128],
                                        QF[rb:rb + 64, cc * 512:(cc + 1) * 512],
                                        start=True, stop=True))
                        te_s = p.now("te")
                        p.wait("scalar", "te", te_s)
                        if (not lora) and kt >= qs * 4:
                            p.wait("scalar", "ve")
                        p.inc_after("scalar", "se", 1,
                                    lambda e, i=i: e.activation(
                                        PB[i % 2][:, :], ps_st[i % 2][:, :], AF.Exp, scale=0.125))
                        se_e = p.now("se")
                        if (not lora) and kt >= qs * 4:
                            mt = kt - qs * 4
                            p.wait("vector", "se", se_e)
                            if mt > 0:
                                p.inc_after("vector", "ve", 1,
                                            lambda e, i=i, mt=mt: e.memset(
                                                PB[i % 2][:, 0:128 * mt], 0.0))
                            p.inc_after("vector", "ve", 1,
                                        lambda e, i=i, mt=mt: e.tensor_tensor(
                                            PB[i % 2][:, 128 * mt:128 * mt + 128],
                                            PB[i % 2][:, 128 * mt:128 * mt + 128],
                                            MASK[:, :], ALU.mult))
                            p.wait("tensor", "ve")
                        else:
                            p.wait("tensor", "se", se_e)
                        fn = (lambda e, kt=kt, hh=hh, i=i, last=(i == len(kts) - 1): e.matmul(
                            ps_o[0:65, :], VT[:, kt * HG * 65 + hh * 65: kt * HG * 65 + hh * 65 + 65],
                            PB[i % 2][:, :], start=(i == 0), stop=last))
                        if i == len(kts) - 1:
                            p.inc_after("tensor", "te", 1, fn)
                        else:
                            p.t(fn)
                    te_pv = p.now("te")
                    p.wait("vector", "te", te_pv)
                    p.inc_after("vector", "ve", 1,
                                lambda e: e.tensor_copy(DEN, ps_o[64:65, :]))
                    p.wait("vector", "ve")
                    p.inc_after("vector", "ve", 1,
                                lambda e: e.reciprocal(RECIP, DEN))
                    ve_r = p.now("ve")
                    p.wait("tensor", "ve", ve_r)
                    p.inc_after("tensor", "te", 1,
                                lambda e: e.matmul(ps_rb[0:64, :], ONE1[0:1, 0:64], RECIP,
                                                   start=True, stop=True))
                    te_rb = p.now("te")
                    p.wait("scalar", "te", te_rb)
                    p.inc_after("scalar", "se", 1,
                                lambda e: e.activation(OS[0:64, :], ps_o[0:64, :], AF.Copy))
                    p.wait("vector", "se")
                    p.wait("vector", "te", te_rb)
                    grow = g * GR + hh * 64
                    ocn, orb = grow // 128, grow % 128
                    p.inc_after("vector", "ve", 1,
                                lambda e, ocn=ocn, orb=orb, qs=qs: e.tensor_tensor(
                                    OF[orb:orb + 64, ocn * Tl + qs * 512: ocn * Tl + qs * 512 + 512],
                                    OS[0:64, :], ps_rb[0:64, :], ALU.mult))
                    p.wait("tensor", "se")
                    p.wait("tensor", "ve")
        ve_of = p.now("ve")
        p.wait("tensor", "ve", ve_of)

        if lora:
            for tc in range(Tl // 512):
                p.wait("tensor", "se")
                mm_chain([
                    (lambda e, tc=tc, dc=dc: e.matmul(
                        ps_a[0:8, :], APJ[:, dc * 8:(dc + 1) * 8],
                        OF[:, dc * Tl + tc * 512: dc * Tl + tc * 512 + 512],
                        start=(dc == 0), stop=(dc == ND - 1)))
                    for dc in range(ND)])
                p.wait("scalar", "te")
                p.inc_after("scalar", "se", 1,
                            lambda e, tc=tc: e.activation(
                                MIDB[0:8, tc * 512:(tc + 1) * 512], ps_a[0:8, :],
                                AF.Copy, scale=2.0))
        se_mid2 = p.now("se")

        p.wait("sync", "ve")
        p.inc_after("sync", "ld", 16, lambda e: e.dma_start(out=BPR[0:1, :], in_=bproj))
        p.wait("gpsimd", "te")
        p.inc_after("gpsimd", "gwp", 16,
                    lambda e: e.dma_start(out=WPROJ[:, 0:ND * D].rearrange("q (dc f) -> q dc f", f=D),
                                          in_=wproj.rearrange("(dc q) f -> q dc f", q=128)))
        gld_wp = p.now("gwp")
        p.wait("tensor", "gwp", gld_wp)
        p.wait("tensor", "ld")
        for half in range(2):
            p.wait("tensor", "ve")
            mm_chain([lambda e, half=half: e.matmul(
                ps_rb[:, :], ONE1[0:1, :], BPR[0:1, half * 512:(half + 1) * 512],
                start=True, stop=True)])
            p.wait("vector", "te")
            p.inc_after("vector", "ve", 1,
                        lambda e, half=half: e.tensor_copy(BPB[:, half * 512:(half + 1) * 512],
                                                           ps_rb[:, :]))
        ve_bpb = p.now("ve")

        p.wait("tensor", "se", se_mid2)
        p.wait("tensor", "ve", ve_bpb)
        for nt in range(NTl):
            buf = (nt % 2) * D
            p.wait("sync", "ve")
            p.inc_after("sync", "ld", 16,
                        lambda e, nt=nt, buf=buf: e.dma_start(out=S32[:, buf:buf + D], in_=x_t[nt]))
            ld_nt = p.now("ld")
            if nt >= 1:
                p.wait("tensor", "ve")
            for half in range(2):
                extra = None
                if lora:
                    extra = (lambda e, nt=nt, half=half: e.matmul(
                        ps_po[:, half * 512:(half + 1) * 512],
                        MIDB[0:8, nt * 128:(nt + 1) * 128],
                        BLP[0:8, half * 512: half * 512 + 512], start=False, stop=True))
                mm_chain([
                    (lambda e, nt=nt, half=half, oc=oc: e.matmul(
                        ps_po[:, half * 512:(half + 1) * 512],
                        OF[:, oc * Tl + nt * 128: oc * Tl + nt * 128 + 128],
                        WPROJ[:, oc * D + half * 512: oc * D + half * 512 + 512],
                        start=(oc == 0), stop=(oc == ND - 1 and not lora)))
                    for oc in range(ND)], extra)
            te_po = p.now("te")
            p.wait("vector", "te", te_po)
            p.wait("vector", "ld", ld_nt)
            p.inc_after("vector", "ve", 1,
                        lambda e, buf=buf: e.tensor_tensor(S32[:, buf:buf + D], S32[:, buf:buf + D],
                                                           ps_po[:, :], ALU.add))
            p.wait("vector", "ve")
            p.inc_after("vector", "ve", 1,
                        lambda e, buf=buf: e.tensor_tensor(S32[:, buf:buf + D], S32[:, buf:buf + D],
                                                           BPB[:, :], ALU.add))
            p.wait("scalar", "st")
            p.wait("vector", "st")
            emit_ln_tile(p, S32[:, buf:buf + D], OUT[:, 0:D], stats,
                         sq_scr=OUT[:, 0:D])
            ve_out = p.now("ve")
            p.wait("sync", "ve", ve_out)
            p.inc_after("sync", "st", 16,
                        lambda e, nt=nt: e.dma_start(out=yo_t[nt], in_=OUT[:, 0:D]))

        p.wait("sync", "st")
        p.barrier()
    if standalone:
        p.run()
        _octx.__exit__(None, None, None)
    return nc



def build_mamba(T_=None, p=None, dram=None):
    Tl = T_ or T
    DI = 2048
    NDI = DI // 128
    NST = 16
    TS = 256
    NS = Tl // TS
    TPS = TS // 128

    standalone = p is None
    if standalone:
        _nc = bass.Bass("TRN2", target_bir_lowering=False)
        _octx = ExitStack()
        _octx.__enter__()
        p = Prog(_nc, _octx)
        p.pfx = ""
        dram = lambda name, shape, dt, kind: _nc.dram_tensor(name, shape, dt, kind=kind).ap()
    nc = p.nc
    x = dram("x", [Tl, D], F32, "ExternalInput")
    lnw = dram("lnw", [1, D], F32, "ExternalInput")
    lnb = dram("lnb", [1, D], F32, "ExternalInput")
    wipT = dram("wipT", [D, 2 * DI], F32, "ExternalInput")
    convw = dram("convw", [128, NDI * 4], F32, "ExternalInput")
    convb = dram("convb", [128, NDI], F32, "ExternalInput")
    wxpT = dram("wxpT", [DI, 96], F32, "ExternalInput")
    wdtT = dram("wdtT", [64, DI], F32, "ExternalInput")
    dtb = dram("dtb", [128, NDI], F32, "ExternalInput")
    alog = dram("alog", [128, NDI * NST], F32, "ExternalInput")
    dmat = dram("dmat", [128, NDI], F32, "ExternalInput")
    wopT = dram("wopT", [DI, D], F32, "ExternalInput")
    ehb = dram("c_ehb", [32, NST * 128], F32, "ExternalInput")
    ehc = dram("c_ehc", [32, NST * 128], F32, "ExternalInput")
    ident = dram("c_ident", [128, 128], F32, "ExternalInput")
    yo = dram("y", [Tl, D], F32, "ExternalOutput")

    x_t = x.rearrange("(nt p) d -> nt p d", p=128)
    yo_t = yo.rearrange("(nt p) d -> nt p d", p=128)

    with ExitStack() as ctx:
        sbt = lambda n, *a: ctx.enter_context(nc.sbuf_tensor(p.pfx + n, *a))
        pst = lambda n, *a: ctx.enter_context(nc.psum_tensor(p.pfx + n, *a))

        IDN = sbt("IDN", [128, 128], BF16)
        WIP = sbt("WIP", [128, ND * 2 * DI], BF16)
        WOP = sbt("WOP", [128, NDI * D], BF16)
        WXP = sbt("WXP", [128, NDI * 96], BF16)
        WDT = sbt("WDT", [64, NDI * 128], BF16)
        CW = sbt("CW", [128, NDI * 4], F32)
        CB = sbt("CB", [128, NDI], F32)
        DTB = sbt("DTB", [128, NDI], F32)
        AC = sbt("AC", [128, NDI * NST], F32)
        DM = sbt("DM", [128, NDI], F32)
        EHB = sbt("EHB", [32, NST * 128], BF16)
        EHC = sbt("EHC", [32, NST * 128], BF16)
        XS32 = sbt("XS32m", [128, TPS * D], F32)
        XLNF = sbt("XLNF", [128, ND * TS], BF16)
        XCB = sbt("XCB", [128, NDI * (TS + 3)], BF16)
        ZS = sbt("ZSm", [128, NDI * TS], BF16)
        XCONV = sbt("XCONV", [128, NDI * TS], BF16)
        DELTA = sbt("DELTA", [128, NDI * TS], BF16)
        DU = sbt("DU", [128, NDI * TS], BF16)
        XLNT = DU[:, 0:TPS * D]
        BB = sbt("BBm", [128, NST * TS], BF16)
        CCB = sbt("CCm", [128, NST * TS], BF16)
        DT_ = sbt("DTm", [64, TS], BF16)
        BCB = sbt("BCB", [32, TS], BF16)
        DA = [sbt(f"DAm{i}", [128, TS], F32) for i in range(2)]
        DBu = [sbt(f"DBm{i}", [128, TS], BF16) for i in range(2)]
        HS = sbt("HSm", [128, TS], F32)
        YA = sbt("YAm", [128, TS], F32)
        SCR1 = sbt("SCR1m", [128, TS], F32)
        TS3 = SCR1
        CV = SCR1
        AB = SCR1
        YF = sbt("YFm", [128, NDI * TS], BF16)
        HC = sbt("HCm", [128, NDI * NST], F32)
        EL = sbt("ELm", [128, TS], F32)
        D0 = sbt("D0m", [128, TS], F32)
        WLNB = sbt("WLNB", [128, D], BF16)
        BLNB = sbt("BLNB", [128, D], BF16)
        ONE1 = sbt("ONE1m", [1, 128], F32)
        OUT = sbt("OUTm", [128, D], F32)
        LNROW = OUT[:, 0:512].bitcast(F32)
        stats = alloc_stats(sbt)

        ps_tr = pst("ps_tr", [128, ND * 128], BF16)
        ps_ip = pst("ps_ip", [128, TS], F32)
        ps_xp = pst("ps_xp", [128, TS], F32)
        ps_dt = pst("ps_dt", [128, TS], F32)
        ps_bc = pst("ps_bc", [128, TS], F32)
        ps_o = pst("ps_o", [128, D], F32)

        p.inc_after("vector", "ve", 1, lambda e: e.memset(stats["eps"][:, 0:1], LN_EPS))
        p.inc_after("vector", "ve", 1, lambda e: e.memset(ONE1[:, :], 1.0))
        p.inc_after("vector", "ve", 1, lambda e: e.memset(HC[:, :], 0.0))
        p.inc_after("vector", "ve", 1, lambda e: e.memset(
            XCB[:, :].rearrange("q (dc c) -> q dc c", c=TS + 3)[:, :, 0:3], 0.0))

        p.inc_after("gpsimd", "gld", 16, lambda e: e.dma_start(out=IDN[:, :], in_=ident))
        p.inc_after("gpsimd", "gld", 16, lambda e: e.dma_start(out=EHB[:, :], in_=ehb))
        p.inc_after("gpsimd", "gld", 16, lambda e: e.dma_start(out=EHC[:, :], in_=ehc))
        for piece in range(4):
            p.inc_after("gpsimd", "gld", 16,
                        lambda e, piece=piece: e.dma_start(
                            out=WIP[:, :].rearrange("q (dc f) -> q dc f", f=2 * DI)[:, 2 * piece:2 * piece + 2, :],
                            in_=wipT.rearrange("(dc q) f -> q dc f", q=128)[:, 2 * piece:2 * piece + 2, :]))
        for piece in range(4):
            p.inc_after("gpsimd", "gld", 16,
                        lambda e, piece=piece: e.dma_start(
                            out=WOP[:, :].rearrange("q (fc f) -> q fc f", f=D)[:, 4 * piece:4 * piece + 4, :],
                            in_=wopT.rearrange("(fc q) f -> q fc f", q=128)[:, 4 * piece:4 * piece + 4, :]))
        p.inc_after("gpsimd", "gld", 16,
                    lambda e: e.dma_start(out=WXP[:, :].rearrange("q (dc f) -> q dc f", f=96),
                                          in_=wxpT.rearrange("(dc q) f -> q dc f", q=128)))
        p.inc_after("gpsimd", "gld", 16, lambda e: e.dma_start(out=WDT[0:64, :], in_=wdtT))
        p.inc_after("sync", "cld", 16, lambda e: e.dma_start(out=CW[:, :], in_=convw))
        p.inc_after("sync", "cld", 16, lambda e: e.dma_start(out=CB[:, :], in_=convb))
        p.inc_after("sync", "cld", 16, lambda e: e.dma_start(out=DTB[:, :], in_=dtb))
        p.inc_after("sync", "cld", 16, lambda e: e.dma_start(out=AC[:, :], in_=alog))
        p.inc_after("sync", "cld", 16, lambda e: e.dma_start(out=DM[:, :], in_=dmat))
        ld_c = p.now("cld")
        p.wait("scalar", "cld", ld_c)
        p.inc_after("scalar", "se", 1, lambda e: e.activation(AC[:, :], AC[:, :], AF.Exp))
        p.wait("vector", "se")
        p.inc_after("vector", "ve", 1, lambda e: e.tensor_scalar_mul(AC[:, :], AC[:, :], -1.0))
        ve_a = p.now("ve")
        LNW = XS32[:, :].bitcast(F32)[0:1, 0:D]
        LNB2 = XS32[:, :].bitcast(F32)[0:1, D:2 * D]
        p.inc_after("sync", "lnld", 16, lambda e: e.dma_start(out=LNW, in_=lnw))
        p.inc_after("sync", "lnld", 16, lambda e: e.dma_start(out=LNB2, in_=lnb))
        ld_ln = p.now("lnld")
        p.wait("tensor", "lnld", ld_ln)
        p.wait("tensor", "ve", ve_a)
        for half in range(2):
            p.inc_after("tensor", "te", 1,
                        lambda e, half=half: e.matmul(ps_o[:, 0:512], ONE1[0:1, :],
                                                      LNW[0:1, half * 512:(half + 1) * 512],
                                                      start=True, stop=True))
            p.wait("vector", "te")
            p.inc_after("vector", "ve", 1,
                        lambda e, half=half: e.tensor_copy(WLNB[:, half * 512:(half + 1) * 512],
                                                           ps_o[:, 0:512]))
            p.wait("tensor", "ve")
            p.inc_after("tensor", "te", 1,
                        lambda e, half=half: e.matmul(ps_o[:, 0:512], ONE1[0:1, :],
                                                      LNB2[0:1, half * 512:(half + 1) * 512],
                                                      start=True, stop=True))
            p.wait("vector", "te")
            p.inc_after("vector", "ve", 1,
                        lambda e, half=half: e.tensor_copy(BLNB[:, half * 512:(half + 1) * 512],
                                                           ps_o[:, 0:512]))
            p.wait("tensor", "ve")
        ve_pre = p.now("ve")
        p.wait("sync", "ve", ve_pre)

        for s in range(NS):
            p.wait("sync", "ve")
            for tt in range(TPS):
                p.inc_after("sync", "ld", 16,
                            lambda e, s=s, tt=tt: e.dma_start(out=XS32[:, tt * D:(tt + 1) * D],
                                                              in_=x_t[s * TPS + tt]))
            ld_s = p.now("ld")
            for tt in range(TPS):
                p.wait("vector", "ld", ld_s)
                p.wait("scalar", "ld", ld_s)
                p.wait("scalar", "st")
                p.wait("vector", "st")
                emit_ln_tile(p, XS32[:, tt * D:(tt + 1) * D], OUT[:, 0:D], stats,
                             sq_scr=OUT[:, 0:D])
                p.wait("vector", "ve")
                p.inc_after("vector", "ve", 1,
                            lambda e, tt=tt: e.tensor_tensor(OUT[:, 0:D], OUT[:, 0:D],
                                                             WLNB[:, :], ALU.mult))
                p.wait("vector", "ve")
                p.inc_after("vector", "ve", 1,
                            lambda e, tt=tt: e.tensor_tensor(XLNT[:, tt * D:(tt + 1) * D],
                                                             OUT[:, 0:D], BLNB[:, :], ALU.add))
            ve_ln = p.now("ve")
            p.wait("tensor", "gld")
            p.wait("tensor", "ve", ve_ln)
            xlnf_view = XLNF[:, :].rearrange("q (nd t) -> q nd t", t=TS)
            for tt in range(TPS):
                p.wait("tensor", "ve")
                fns = [
                    (lambda e, tt=tt, dc=dc: e.transpose(
                        ps_tr[:, dc * 128:(dc + 1) * 128],
                        XLNT[:, tt * D + dc * 128: tt * D + (dc + 1) * 128], IDN[:, :]))
                    for dc in range(ND)]
                for i, fn in enumerate(fns):
                    if i == ND - 1:
                        p.inc_after("tensor", "te", 1, fn)
                    else:
                        p.t(fn)
                p.wait("vector", "te")
                p.inc_after("vector", "ve", 1,
                            lambda e, tt=tt, xlnf_view=xlnf_view: e.tensor_copy(
                                xlnf_view[:, :, tt * 128:(tt + 1) * 128],
                                ps_tr[:, :].rearrange("q (nd f) -> q nd f", f=128)))
            ve_xf = p.now("ve")
            p.wait("tensor", "ve", ve_xf)
            p.wait("tensor", "gld")
            xcb_view = XCB[:, :].rearrange("q (dc c) -> q dc c", c=TS + 3)
            for fc in range(32):
                if fc > 0 or s > 0:
                    p.wait("tensor", "se")
                for dc in range(ND):
                    fn = lambda e, fc=fc, dc=dc: e.matmul(
                        ps_ip[:, :], WIP[:, (dc * 32 + fc) * 128:(dc * 32 + fc + 1) * 128],
                        XLNF[:, dc * TS:(dc + 1) * TS],
                        start=(dc == 0), stop=(dc == ND - 1))
                    if dc == ND - 1:
                        p.inc_after("tensor", "te", 1, fn)
                    else:
                        p.t(fn)
                p.wait("scalar", "te")
                if fc < NDI:
                    p.wait("scalar", "ve")
                    p.inc_after("scalar", "se", 1,
                                lambda e, fc=fc, xcb_view=xcb_view: e.activation(
                                    xcb_view[:, fc, 3:TS + 3], ps_ip[:, :], AF.Copy))
                else:
                    p.inc_after("scalar", "se", 1,
                                lambda e, fc=fc: e.activation(
                                    ZS[:, (fc - NDI) * TS:(fc - NDI + 1) * TS], ps_ip[:, :], AF.Copy))
            se_ip = p.now("se")
            for dc in range(NDI):
                p.wait("vector", "se")
                p.inc_after("vector", "ve", 1,
                            lambda e, dc=dc: e.tensor_scalar(
                                CV[:, :], XCB[:, dc * (TS + 3): dc * (TS + 3) + TS],
                                CW[:, dc * 4: dc * 4 + 1], None, ALU.mult))
                for k in range(1, 4):
                    p.wait("vector", "ve")
                    p.inc_after("vector", "ve", 1,
                                lambda e, dc=dc, k=k: e.scalar_tensor_tensor(
                                    CV[:, :], XCB[:, dc * (TS + 3) + k: dc * (TS + 3) + k + TS],
                                    CW[:, dc * 4 + k: dc * 4 + k + 1], CV[:, :],
                                    ALU.mult, ALU.add))
                p.wait("scalar", "ve")
                p.wait("scalar", "te")
                p.inc_after("scalar", "se", 1,
                            lambda e, dc=dc: e.activation(XCONV[:, dc * TS:(dc + 1) * TS],
                                                          CV[:, :], AF.Silu,
                                                          bias=CB[:, dc:dc + 1]))
            se_cv = p.now("se")
            p.wait("vector", "se", se_cv)
            p.inc_after("vector", "ve", 1,
                        lambda e: e.tensor_copy(xcb_view[:, :, 0:3], xcb_view[:, :, TS:TS + 3]))
            p.wait("tensor", "se", se_cv)
            for dc in range(NDI):
                fn = lambda e, dc=dc: e.matmul(
                    ps_xp[0:96, :], WXP[:, dc * 96:(dc + 1) * 96],
                    XCONV[:, dc * TS:(dc + 1) * TS],
                    start=(dc == 0), stop=(dc == NDI - 1))
                if dc == NDI - 1:
                    p.inc_after("tensor", "te", 1, fn)
                else:
                    p.t(fn)
            te_xp = p.now("te")
            p.wait("scalar", "te", te_xp)
            p.inc_after("scalar", "se", 1,
                        lambda e: e.activation(DT_[0:64, :], ps_xp[0:64, :], AF.Copy))
            p.inc_after("scalar", "se", 1,
                        lambda e: e.activation(BCB[0:32, :], ps_xp[64:96, :], AF.Copy))
            se_bc = p.now("se")
            p.wait("tensor", "se", se_bc)
            p.wait("vector", "se")
            for fc in range(NDI):
                p.inc_after("tensor", "te", 1,
                            lambda e, fc=fc: e.matmul(ps_dt[:, :],
                                                      WDT[0:64, fc * 128:(fc + 1) * 128],
                                                      DT_[0:64, :], start=True, stop=True))
                te_d = p.now("te")
                p.wait("vector", "te", te_d)
                p.inc_after("vector", "ve", 1,
                            lambda e, fc=fc: e.tensor_scalar_add(D0[:, :], ps_dt[:, :],
                                                                 DTB[:, fc:fc + 1]))
                p.wait("vector", "ve")
                p.inc_after("vector", "ve", 1,
                            lambda e: e.scalar_tensor_tensor(AB[:, :], D0[:, :], -1.0, D0[:, :],
                                                             ALU.mult, ALU.max))
                p.wait("scalar", "ve")
                p.inc_after("scalar", "se", 1,
                            lambda e: e.activation(EL[:, :], AB[:, :], AF.Exp, scale=-1.0))
                p.wait("scalar", "se")
                p.inc_after("scalar", "se", 1,
                            lambda e: e.activation(EL[:, :], EL[:, :], AF.Ln, bias=1.0))
                p.wait("vector", "ve")
                p.inc_after("vector", "ve", 1,
                            lambda e: e.tensor_scalar_max(D0[:, :], D0[:, :], 0.0))
                p.wait("vector", "se")
                p.wait("vector", "ve")
                p.inc_after("vector", "ve", 1,
                            lambda e, fc=fc: e.tensor_tensor(DELTA[:, fc * TS:(fc + 1) * TS],
                                                             EL[:, :], D0[:, :], ALU.add))
                p.wait("tensor", "ve")
            p.wait("vector", "ve")
            for dc in range(NDI):
                p.inc_after("vector", "ve", 1,
                            lambda e, dc=dc: e.tensor_tensor(DU[:, dc * TS:(dc + 1) * TS],
                                                             DELTA[:, dc * TS:(dc + 1) * TS],
                                                             XCONV[:, dc * TS:(dc + 1) * TS],
                                                             ALU.mult))
            ve_du = p.now("ve")
            p.wait("tensor", "se", se_bc)
            p.wait("scalar", "ve")
            for n in range(NST):
                p.wait("tensor", "se")
                p.inc_after("tensor", "te", 1,
                            lambda e, n=n: e.matmul(ps_bc[:, :], EHB[:, n * 128:(n + 1) * 128],
                                                    BCB[0:32, :], start=True, stop=True))
                p.wait("scalar", "te")
                p.inc_after("scalar", "se", 1,
                            lambda e, n=n: e.activation(BB[:, n * TS:(n + 1) * TS],
                                                        ps_bc[:, :], AF.Copy))
                p.wait("tensor", "se")
                p.inc_after("tensor", "te", 1,
                            lambda e, n=n: e.matmul(ps_bc[:, :], EHC[:, n * 128:(n + 1) * 128],
                                                    BCB[0:32, :], start=True, stop=True))
                p.wait("scalar", "te")
                p.inc_after("scalar", "se", 1,
                            lambda e, n=n: e.activation(CCB[:, n * TS:(n + 1) * TS],
                                                        ps_bc[:, :], AF.Copy))
            se_bb = p.now("se")
            p.wait("vector", "se", se_bb)
            p.wait("scalar", "ve", ve_du)
            for dc in range(NDI):
                for n in range(NST):
                    par = (dc * NST + n) % 2
                    p.wait("scalar", "ve")
                    p.inc_after("scalar", "se", 1,
                                lambda e, dc=dc, n=n, par=par: e.activation(
                                    DA[par][:, :], DELTA[:, dc * TS:(dc + 1) * TS], AF.Exp,
                                    scale=AC[:, dc * NST + n: dc * NST + n + 1]))
                    p.inc_after("vector", "ve", 1,
                                lambda e, dc=dc, n=n, par=par: e.tensor_tensor(
                                    DBu[par][:, :], DU[:, dc * TS:(dc + 1) * TS],
                                    BB[:, n * TS:(n + 1) * TS], ALU.mult))
                    p.wait("vector", "se")
                    p.wait("vector", "ve")
                    p.inc_after("vector", "ve", 1,
                                lambda e, dc=dc, n=n, par=par: e.tensor_tensor_scan(
                                    HS[:, :], DA[par][:, :], DBu[par][:, :],
                                    HC[:, dc * NST + n: dc * NST + n + 1],
                                    ALU.mult, ALU.add))
                    p.wait("vector", "ve")
                    p.inc_after("vector", "ve", 1,
                                lambda e, dc=dc, n=n: e.tensor_copy(
                                    HC[:, dc * NST + n: dc * NST + n + 1], HS[:, TS - 1:TS]))
                    if n == 0:
                        p.inc_after("vector", "ve", 1,
                                    lambda e, n=n: e.tensor_tensor(YA[:, :], HS[:, :],
                                                                   CCB[:, n * TS:(n + 1) * TS],
                                                                   ALU.mult))
                    else:
                        p.inc_after("vector", "ve", 1,
                                    lambda e, n=n: e.tensor_tensor(TS3[:, :], HS[:, :],
                                                                   CCB[:, n * TS:(n + 1) * TS],
                                                                   ALU.mult))
                        p.wait("vector", "ve")
                        p.inc_after("vector", "ve", 1,
                                    lambda e: e.tensor_tensor(YA[:, :], YA[:, :], TS3[:, :],
                                                              ALU.add))
                p.wait("vector", "ve")
                p.inc_after("vector", "ve", 1,
                            lambda e, dc=dc: e.scalar_tensor_tensor(
                                YA[:, :], XCONV[:, dc * TS:(dc + 1) * TS], DM[:, dc:dc + 1],
                                YA[:, :], ALU.mult, ALU.add))
                p.wait("scalar", "ve")
                p.inc_after("scalar", "se", 1,
                            lambda e, dc=dc: e.activation(TS3[:, :].bitcast(BF16)[:, 0:TS],
                                                          ZS[:, dc * TS:(dc + 1) * TS], AF.Silu))
                p.wait("vector", "se")
                p.inc_after("vector", "ve", 1,
                            lambda e, dc=dc: e.tensor_tensor(YF[:, dc * TS:(dc + 1) * TS],
                                                             YA[:, :],
                                                             TS3[:, :].bitcast(BF16)[:, 0:TS],
                                                             ALU.mult))
            ve_y = p.now("ve")
            p.wait("tensor", "ve", ve_y)
            for tt in range(TPS):
                if s > 0 or tt > 0:
                    p.wait("tensor", "ve")
                for half in range(2):
                    for fc in range(NDI):
                        fn = lambda e, tt=tt, half=half, fc=fc: e.matmul(
                            ps_o[:, half * 512:(half + 1) * 512],
                            YF[:, fc * TS + tt * 128: fc * TS + tt * 128 + 128],
                            WOP[:, fc * D + half * 512: fc * D + half * 512 + 512],
                            start=(fc == 0), stop=(fc == NDI - 1))
                        if fc == NDI - 1:
                            p.inc_after("tensor", "te", 1, fn)
                        else:
                            p.t(fn)
                te_o = p.now("te")
                p.wait("vector", "te", te_o)
                p.inc_after("vector", "ve", 1,
                            lambda e, tt=tt: e.scalar_tensor_tensor(
                                XS32[:, tt * D:(tt + 1) * D], XS32[:, tt * D:(tt + 1) * D],
                                2.0, ps_o[:, :], ALU.mult, ALU.add))
                p.wait("scalar", "st")
                p.wait("vector", "st")
                emit_ln_tile(p, XS32[:, tt * D:(tt + 1) * D], OUT[:, 0:D], stats,
                             sq_scr=OUT[:, 0:D])
                ve_out = p.now("ve")
                p.wait("sync", "ve", ve_out)
                p.inc_after("sync", "st", 16,
                            lambda e, s=s, tt=tt: e.dma_start(out=yo_t[s * TPS + tt],
                                                              in_=OUT[:, 0:D]))

        p.wait("sync", "st")
        p.barrier()
    if standalone:
        p.run()
        _octx.__exit__(None, None, None)
    return nc



def build_l2_attn(T_=None, p=None, dram=None):
    Tl = T_ or T
    NTl = Tl // 128
    NH = 8
    NQS = Tl // 512
    GEPS = 1e-5

    standalone = p is None
    if standalone:
        _nc = bass.Bass("TRN2", target_bir_lowering=False)
        _octx = ExitStack()
        _octx.__enter__()
        p = Prog(_nc, _octx)
        p.pfx = ""
        dram = lambda name, shape, dt, kind: _nc.dram_tensor(name, shape, dt, kind=kind).ap()
    nc = p.nc
    x = dram("x", [Tl, D], F32, "ExternalInput")
    wq = dram("wq", [128, NH * 128], F32, "ExternalInput")
    wk = dram("wk", [128, NH * 128], F32, "ExternalInput")
    wv = dram("wv", [128, NH * 128], F32, "ExternalInput")
    w1 = dram("w1", [D, D], F32, "ExternalInput")
    w2 = dram("w2", [D, D], F32, "ExternalInput")
    gnw = dram("gnw", [128, NH], F32, "ExternalInput")
    gnb = dram("gnb", [128, NH], F32, "ExternalInput")
    gp = dram("c_gp", [128, NH * 36], F32, "ExternalInput")
    ramp = dram("c_ramp", [128, NH * 512], F32, "ExternalInput")
    cmask = dram("c_mask", [128, 128], F32, "ExternalInput")
    ident = dram("c_ident", [128, 128], F32, "ExternalInput")
    yo = dram("y", [Tl, D], F32, "ExternalOutput")

    gammas = 1.0 - np.exp(np.linspace(np.log(1 / 512), np.log(1 / 32), NH))
    mm_max = [min(NTl, int(np.ceil(37.0 / (128 * -np.log(g)))) + 1) for g in gammas]

    x_t = x.rearrange("(nt p) d -> nt p d", p=128)
    yo_t = yo.rearrange("(nt p) d -> nt p d", p=128)

    with ExitStack() as ctx:
        sbt = lambda n, *a: ctx.enter_context(nc.sbuf_tensor(p.pfx + n, *a))
        pst = lambda n, *a: ctx.enter_context(nc.psum_tensor(p.pfx + n, *a))

        IDN = sbt("IDN", [128, 128], BF16)
        XF = sbt("XF2", [128, ND * Tl], BF16)
        RETF = sbt("RETF", [128, ND * Tl], BF16)
        WQ = sbt("WQ2", [128, NH * 128], BF16)
        WK = sbt("WK2", [128, NH * 128], BF16)
        WV = sbt("WV2", [128, NH * 128], BF16)
        KVW = sbt("KVW", [128, max(2 * Tl, ND * D)], BF16)
        W2S = sbt("W2S", [128, ND * D], BF16)
        W1S = KVW[:, 0:ND * D]
        GNW = sbt("GNW", [128, NH], F32)
        GNB = sbt("GNB", [128, NH], F32)
        GP = sbt("GP2", [128, NH * 36], F32)
        RAMP = sbt("RAMP", [128, NH * 512], BF16)
        MASK = sbt("MASK2", [128, 128], BF16)
        KF = KVW[:, 0:Tl]
        VT = KVW[:, Tl:2 * Tl]
        QS_ = sbt("QS2", [128, 512], BF16)
        PBJ = sbt("PBJ2", [128, 512], BF16)
        RET_s = sbt("RET_s", [128, 512], BF16)
        SQ = sbt("SQ2", [128, 512], BF16)
        RSC = sbt("RSC2f", [128, 512], F32)
        SG = sbt("SG2", [128, 512], BF16)
        S2F = sbt("S2F", [128, ND * 512], BF16)
        MU = S2F[:, :].bitcast(F32)[0:1, 0:512]
        RSTD = S2F[:, :].bitcast(F32)[0:1, 512:1024]
        ONE1 = sbt("ONE1f", [1, 128], F32)
        ONEC = sbt("ONECb", [128, 1], BF16)
        SHARE = sbt("SHARE2", [128, 4 * D], BF16)
        XSB = SHARE[:, 0:2 * D]
        S32 = SHARE[:, :].bitcast(F32)
        OUT = sbt("OUT2", [128, D], F32)
        stats = alloc_stats(sbt)

        ps_tr = pst("ps_tr", [128, ND * 128], BF16)
        ps_a = pst("ps_a", [128, 512], F32)
        ps_sc = pst("ps_sc", [128, 512], F32)
        ps_ret = pst("ps_ret", [128, 512], F32)
        ps_rb = pst("ps_rb", [128, 512], F32)
        ps_po = pst("ps_po", [128, D], F32)

        p.inc_after("vector", "ve", 1, lambda e: e.memset(stats["eps"][:, 0:1], LN_EPS))
        p.inc_after("vector", "ve", 1, lambda e: e.memset(ONE1[:, :], 1.0))
        p.inc_after("vector", "ve", 1, lambda e: e.memset(ONEC[:, :], 1.0))

        p.inc_after("gpsimd", "gld", 16, lambda e: e.dma_start(out=IDN[:, :], in_=ident))
        p.inc_after("gpsimd", "gld", 16, lambda e: e.dma_start(out=WQ[:, :], in_=wq))
        p.inc_after("gpsimd", "gld", 16, lambda e: e.dma_start(out=WK[:, :], in_=wk))
        p.inc_after("gpsimd", "gld", 16, lambda e: e.dma_start(out=WV[:, :], in_=wv))
        p.inc_after("gpsimd", "gld", 16,
                    lambda e: e.dma_start(out=W2S[:, :].rearrange("q (dc f) -> q dc f", f=D),
                                          in_=w2.rearrange("(dc q) f -> q dc f", q=128)))
        p.inc_after("gpsimd", "gld", 16, lambda e: e.dma_start(out=RAMP[:, :], in_=ramp))
        p.inc_after("gpsimd", "gld", 16, lambda e: e.dma_start(out=MASK[:, :], in_=cmask))
        p.inc_after("sync", "cld", 16, lambda e: e.dma_start(out=GNW[:, :], in_=gnw))
        p.inc_after("sync", "cld", 16, lambda e: e.dma_start(out=GNB[:, :], in_=gnb))
        p.inc_after("sync", "cld", 16, lambda e: e.dma_start(out=GP[:, :], in_=gp))
        cld_c = p.now("cld")

        te_nt = []
        for nt in range(NTl):
            buf = (nt % 2) * D
            par = f"gx{nt % 2}"
            if nt >= 2:
                p.wait("gpsimd", "te", te_nt[nt - 2])
            p.inc_after("gpsimd", par, 16,
                        lambda e, nt=nt, buf=buf: e.dma_start(out=XSB[:, buf:buf + D], in_=x_t[nt]))
            p.wait("tensor", par)
            if nt == 0:
                p.wait("tensor", "gld")
            p.wait("tensor", "ve")
            fns = [
                (lambda e, nt=nt, dc=dc, buf=buf: e.transpose(
                    ps_tr[:, dc * 128:(dc + 1) * 128],
                    XSB[:, buf + dc * 128: buf + (dc + 1) * 128], IDN[:, :]))
                for dc in range(ND)]
            for i, fn in enumerate(fns):
                if i == ND - 1:
                    p.inc_after("tensor", "te", 1, fn)
                else:
                    p.t(fn)
            te_nt.append(p.now("te"))
            p.wait("vector", "te")
            xf_view = XF[:, :].rearrange("q (nd t) -> q nd t", t=Tl)
            p.inc_after("vector", "ve", 1,
                        lambda e, nt=nt, xf_view=xf_view: e.tensor_copy(
                            xf_view[:, :, nt * 128:(nt + 1) * 128],
                            ps_tr[:, :].rearrange("q (nd f) -> q nd f", f=128)))
        ve_xf = p.now("ve")
        p.wait("tensor", "ve", ve_xf)
        p.wait("scalar", "cld", cld_c)

        for h in range(NH):
            for tc in range(Tl // 512):
                if h > 0 or tc > 0:
                    p.wait("tensor", "se")
                p.inc_after("tensor", "te", 1,
                            lambda e, h=h, tc=tc: e.matmul(
                                ps_a[:, :], WK[:, h * 128:(h + 1) * 128],
                                XF[:, h * Tl + tc * 512: h * Tl + tc * 512 + 512],
                                start=True, stop=True))
                p.wait("scalar", "te")
                p.inc_after("scalar", "se", 1,
                            lambda e, tc=tc: e.activation(KF[:, tc * 512:(tc + 1) * 512],
                                                          ps_a[:, :], AF.Copy))
            for kt in range(NTl):
                p.wait("tensor", "se")
                p.inc_after("tensor", "te", 1,
                            lambda e, h=h, kt=kt: e.matmul(
                                ps_a[:, 0:128], XF[:, h * Tl + kt * 128: h * Tl + kt * 128 + 128],
                                WV[:, h * 128:(h + 1) * 128], start=True, stop=True))
                p.wait("scalar", "te")
                p.inc_after("scalar", "se", 1,
                            lambda e, kt=kt: e.activation(VT[:, kt * 128:(kt + 1) * 128],
                                                          ps_a[:, 0:128], AF.Copy))
            se_kv = p.now("se")
            p.wait("tensor", "se", se_kv)

            for qs in range(NQS):
                p.wait("tensor", "ve")
                p.inc_after("tensor", "te", 1,
                            lambda e, h=h, qs=qs: e.matmul(
                                ps_a[:, :], WQ[:, h * 128:(h + 1) * 128],
                                XF[:, h * Tl + qs * 512: h * Tl + qs * 512 + 512],
                                start=True, stop=True))
                p.wait("vector", "te")
                p.inc_after("vector", "ve", 1,
                            lambda e, h=h: e.tensor_tensor(QS_[:, :], ps_a[:, :],
                                                           RAMP[:, h * 512:(h + 1) * 512],
                                                           ALU.mult))
                ve_q = p.now("ve")
                p.wait("tensor", "ve", ve_q)
                kt_lo = max(0, qs * 4 - mm_max[h])
                kts = list(range(kt_lo, min(NTl, qs * 4 + 4)))
                for i, kt in enumerate(kts):
                    p.wait("tensor", "se")
                    p.inc_after("tensor", "te", 1,
                                lambda e, kt=kt: e.matmul(
                                    ps_sc[:, :], KF[:, kt * 128:(kt + 1) * 128], QS_[:, :],
                                    start=True, stop=True))
                    te_s = p.now("te")
                    p.wait("scalar", "te", te_s)
                    mm = qs * 4 - kt
                    p.wait("scalar", "ve")
                    p.inc_after("scalar", "se", 1,
                                lambda e, h=h, mm=mm: e.activation(
                                    PBJ[:, :], ps_sc[:, :], AF.Copy,
                                    scale=GP[:, h * 36 + mm + 3: h * 36 + mm + 4]))
                    se_e = p.now("se")
                    if mm <= 0:
                        mt = -mm
                        p.wait("vector", "se", se_e)
                        if mt > 0:
                            p.inc_after("vector", "ve", 1,
                                        lambda e, mt=mt: e.memset(PBJ[:, 0:128 * mt], 0.0))
                        p.inc_after("vector", "ve", 1,
                                    lambda e, mt=mt: e.tensor_tensor(
                                        PBJ[:, 128 * mt:128 * mt + 128],
                                        PBJ[:, 128 * mt:128 * mt + 128], MASK[:, :], ALU.mult))
                        p.wait("tensor", "ve")
                    else:
                        p.wait("tensor", "se", se_e)
                    fn = (lambda e, kt=kt, i=i, last=(i == len(kts) - 1): e.matmul(
                        ps_ret[:, :], VT[:, kt * 128:(kt + 1) * 128], PBJ[:, :],
                        start=(i == 0), stop=last))
                    if i == len(kts) - 1:
                        p.inc_after("tensor", "te", 1, fn)
                    else:
                        p.t(fn)
                te_pv = p.now("te")
                p.wait("scalar", "te", te_pv)
                p.inc_after("scalar", "se", 1,
                            lambda e: e.activation(RET_s[:, :], ps_ret[:, :], AF.Copy))
                p.wait("scalar", "se")
                p.inc_after("scalar", "se", 1,
                            lambda e: e.activation(SQ[:, :], RET_s[:, :], AF.Square))
                se_r = p.now("se")
                p.wait("tensor", "se", se_r)
                p.inc_after("tensor", "te", 1,
                            lambda e: e.matmul(ps_sc[0:1, :], ONEC[:, 0:1], RET_s[:, :],
                                               start=True, stop=True))
                p.inc_after("tensor", "te", 1,
                            lambda e: e.matmul(ps_rb[0:1, :], ONEC[:, 0:1], SQ[:, :],
                                               start=True, stop=True))
                te_st = p.now("te")
                p.wait("vector", "te", te_st)
                p.inc_after("vector", "ve", 1,
                            lambda e: e.tensor_scalar_mul(MU, ps_sc[0:1, :], 1.0 / 128))
                p.wait("vector", "ve")
                p.inc_after("vector", "ve", 1,
                            lambda e: e.tensor_scalar_mul(RSTD, ps_rb[0:1, :], 1.0 / 128))
                p.wait("vector", "ve")
                p.inc_after("vector", "ve", 1,
                            lambda e: e.tensor_tensor(ps_sc[0:1, :], MU[0:1, :], MU[0:1, :],
                                                      ALU.mult) if False else
                            e.tensor_mul(RSC[0:1, 0:512], MU, MU))
                p.wait("vector", "ve")
                p.inc_after("vector", "ve", 1,
                            lambda e: e.tensor_sub(RSTD, RSTD, RSC[0:1, 0:512]))
                p.wait("scalar", "ve")
                p.inc_after("scalar", "se", 1,
                            lambda e: e.activation(RSTD, RSTD, AF.Sqrt,
                                                   bias=stats["eps"][0:1, 0:1]))
                p.wait("vector", "se")
                p.inc_after("vector", "ve", 1,
                            lambda e: e.reciprocal(RSTD, RSTD))
                ve_st = p.now("ve")
                p.wait("tensor", "ve", ve_st)
                p.inc_after("tensor", "te", 1,
                            lambda e: e.matmul(ps_sc[:, :], ONE1[0:1, :], MU,
                                               start=True, stop=True))
                p.inc_after("tensor", "te", 1,
                            lambda e: e.matmul(ps_rb[:, :], ONE1[0:1, :], RSTD,
                                               start=True, stop=True))
                te_bc = p.now("te")
                p.wait("vector", "te", te_bc)
                p.inc_after("vector", "ve", 1,
                            lambda e: e.tensor_tensor(RSC[:, :], RET_s[:, :], ps_sc[:, :],
                                                      ALU.subtract))
                p.wait("vector", "ve")
                p.inc_after("vector", "ve", 1,
                            lambda e: e.tensor_tensor(RSC[:, :], RSC[:, :], ps_rb[:, :],
                                                      ALU.mult))
                p.wait("vector", "ve")
                p.inc_after("vector", "ve", 1,
                            lambda e, h=h, qs=qs: e.tensor_scalar(
                                RETF[:, h * Tl + qs * 512: h * Tl + qs * 512 + 512],
                                RSC[:, :], GNW[:, h:h + 1], GNB[:, h:h + 1],
                                ALU.mult, ALU.add))
                p.wait("tensor", "ve")
        ve_ret = p.now("ve")

        p.wait("gpsimd", "te")
        p.inc_after("gpsimd", "gw1", 16,
                    lambda e: e.dma_start(out=W1S[0:128, 0:ND * D].rearrange("q (dc f) -> q dc f", f=D),
                                          in_=w1.rearrange("(dc q) f -> q dc f", q=128)))
        gw1 = p.now("gw1")
        p.wait("tensor", "gw1", gw1)

        p.wait("tensor", "ve", ve_ret)
        for ts_i in range(NQS):
            for fc in range(ND):
                if ts_i > 0 or fc > 0:
                    p.wait("tensor", "ve")
                    p.wait("tensor", "se")
                fns = [
                    (lambda e, ts_i=ts_i, fc=fc, dc=dc: e.matmul(
                        ps_a[:, :], W1S[:, dc * D + fc * 128: dc * D + fc * 128 + 128],
                        XF[:, dc * Tl + ts_i * 512: dc * Tl + ts_i * 512 + 512],
                        start=(dc == 0), stop=(dc == ND - 1)))
                    for dc in range(ND)]
                for i, fn in enumerate(fns):
                    if i == ND - 1:
                        p.inc_after("tensor", "te", 1, fn)
                    else:
                        p.t(fn)
                p.wait("scalar", "te")
                p.wait("scalar", "ve")
                p.inc_after("scalar", "se", 1,
                            lambda e: e.activation(SG[:, :], ps_a[:, :], AF.Silu))
                p.wait("vector", "se")
                p.inc_after("vector", "ve", 1,
                            lambda e, ts_i=ts_i, fc=fc: e.tensor_tensor(
                                S2F[:, fc * 512:(fc + 1) * 512], SG[:, :],
                                RETF[:, fc * Tl + ts_i * 512: fc * Tl + ts_i * 512 + 512],
                                ALU.add))
            ve_s2 = p.now("ve")
            p.wait("tensor", "ve", ve_s2)
            for tt in range(4):
                nt = ts_i * 4 + tt
                p.wait("sync", "ve")
                p.inc_after("sync", "ld", 16,
                            lambda e, nt=nt: e.dma_start(out=S32[:, 0:D], in_=x_t[nt]))
                ld_nt = p.now("ld")
                if nt > 0:
                    p.wait("tensor", "ve")
                for half in range(2):
                    fns = [
                        (lambda e, tt=tt, half=half, fc=fc: e.matmul(
                            ps_po[:, half * 512:(half + 1) * 512],
                            S2F[:, fc * 512 + tt * 128: fc * 512 + tt * 128 + 128],
                            W2S[:, fc * D + half * 512: fc * D + half * 512 + 512],
                            start=(fc == 0), stop=(fc == ND - 1)))
                        for fc in range(ND)]
                    for i, fn in enumerate(fns):
                        if i == ND - 1:
                            p.inc_after("tensor", "te", 1, fn)
                        else:
                            p.t(fn)
                te_po = p.now("te")
                p.wait("vector", "te", te_po)
                p.wait("vector", "ld", ld_nt)
                p.inc_after("vector", "ve", 1,
                            lambda e: e.tensor_tensor(S32[:, 0:D], S32[:, 0:D],
                                                      ps_po[:, :], ALU.add))
                p.wait("scalar", "st")
                p.wait("vector", "st")
                emit_ln_tile(p, S32[:, 0:D], OUT[:, 0:D], stats, sq_scr=OUT[:, 0:D])
                ve_out = p.now("ve")
                p.wait("sync", "ve", ve_out)
                p.inc_after("sync", "st", 16,
                            lambda e, nt=nt: e.dma_start(out=yo_t[nt], in_=OUT[:, 0:D]))

        p.wait("sync", "st")
        p.barrier()
    if standalone:
        p.run()
        _octx.__exit__(None, None, None)
    return nc



def build_l2_ffn(T_=None, p=None, dram=None):
    Tl = T_ or T
    NTl = Tl // 128
    H = 4096
    NHC = H // 128
    TS = 256
    NS = Tl // TS
    TPS = TS // 128
    REPS = 1e-8

    standalone = p is None
    if standalone:
        _nc = bass.Bass("TRN2", target_bir_lowering=False)
        _octx = ExitStack()
        _octx.__enter__()
        p = Prog(_nc, _octx)
        p.pfx = ""
        dram = lambda name, shape, dt, kind: _nc.dram_tensor(name, shape, dt, kind=kind).ap()
    nc = p.nc
    x = dram("x", [Tl, D], F32, "ExternalInput")
    wupT = dram("wupT", [D, H], F32, "ExternalInput")
    wdnT = dram("wdnT", [H, D], F32, "ExternalInput")
    ident = dram("c_ident", [128, 128], F32, "ExternalInput")
    yo = dram("y", [Tl, D], F32, "ExternalOutput")

    x_t = x.rearrange("(nt p) d -> nt p d", p=128)
    yo_t = yo.rearrange("(nt p) d -> nt p d", p=128)

    with ExitStack() as ctx:
        sbt = lambda n, *a: ctx.enter_context(nc.sbuf_tensor(p.pfx + n, *a))
        pst = lambda n, *a: ctx.enter_context(nc.psum_tensor(p.pfx + n, *a))

        IDN = sbt("IDN", [128, 128], BF16)
        WUQ = sbt("WUQ", [128, ND * H], BF16)
        WDQ = sbt("WDQ", [128, NHC * D], BF16)
        WSTG = sbt("WSTG2", [128, 4096], F32)
        ABSS = sbt("ABSS", [128, 2], F32)
        TH = sbt("TH2", [128, 2], F32)
        SGN = sbt("SGN2", [128, 4096], BF16)
        AB2 = sbt("AB2f", [128, 4096], F32)
        RED = sbt("RED2", [128, 1], F32)
        XNT = sbt("XNT2", [128, TPS * D], BF16)
        XS32 = sbt("XS32f2", [128, TPS * D], F32)
        XNF = [sbt(f"XNF{i}", [128, ND * TS], BF16) for i in range(2)]
        HS = sbt("HS2", [128, TS], BF16)
        SQH = sbt("SQH2", [128, TS], BF16)
        RSTDC = sbt("RSTDC", [128, TPS], F32)
        ONEC = sbt("ONECf2", [128, 1], BF16)
        ONE1 = sbt("ONE1f2", [1, 128], F32)
        OUT = sbt("OUT2f", [128, D], F32)
        RROW = sbt("RROW", [1, TS], F32)
        stats = alloc_stats(sbt)

        ps_tr = pst("ps_tr", [128, ND * 128], BF16)
        ps_u = pst("ps_u", [128, TS], F32)
        ps_sq = pst("ps_sq", [1, TS], F32)
        ps_d = [pst(f"ps_d{i}", [128, D], F32) for i in range(TPS)]
        ps_al = pst("ps_al", [128, 8], F32)

        p.inc_after("vector", "ve", 1, lambda e: e.memset(stats["eps"][:, 0:1], LN_EPS))
        p.inc_after("vector", "ve", 1, lambda e: e.memset(ONEC[:, :], 1.0))
        p.inc_after("vector", "ve", 1, lambda e: e.memset(ONE1[:, :], 1.0))
        p.inc_after("vector", "ve", 1, lambda e: e.memset(ABSS[:, :], 0.0))
        p.inc_after("gpsimd", "gld", 16, lambda e: e.dma_start(out=IDN[:, :], in_=ident))

        wup_r = wupT.rearrange("(rc q) f -> rc q f", q=128)
        wdn_r = wdnT.rearrange("(rc q) f -> rc q f", q=128)
        for wi_, (wr, nrc, ncol) in enumerate(((wup_r, ND, H), (wdn_r, NHC, D))):
            for rc in range(nrc):
                p.wait("sync", "ve")
                p.inc_after("sync", "wld", 16,
                            lambda e, wr=wr, rc=rc, ncol=ncol: e.dma_start(
                                out=WSTG[:, 0:ncol], in_=wr[rc]))
                p.wait("vector", "wld")
                p.inc_after("vector", "ve", 1,
                            lambda e, ncol=ncol: e.reduce_sum(RED[:, 0:1], WSTG[:, 0:ncol],
                                                              axis=AX.X,
                                                              apply_absolute_value=True))
                p.wait("vector", "ve")
                p.inc_after("vector", "ve", 1,
                            lambda e, wi_=wi_: e.tensor_add(ABSS[:, wi_:wi_ + 1],
                                                            ABSS[:, wi_:wi_ + 1],
                                                            RED[:, 0:1]))
        ve_abs = p.now("ve")
        ONECF = sbt("ONECF", [128, 1], F32)
        THR = sbt("THR2", [1, 2], F32)
        p.inc_after("vector", "ve", 1, lambda e: e.memset(ONECF[:, :], 1.0))
        p.wait("tensor", "ve", p.now("ve"))
        for wi_ in range(2):
            p.inc_after("tensor", "te", 1,
                        lambda e, wi_=wi_: e.matmul(ps_al[0:1, wi_:wi_ + 1], ONECF[:, 0:1],
                                                    ABSS[:, wi_:wi_ + 1],
                                                    start=True, stop=True))
        te_al = p.now("te")
        p.wait("vector", "te", te_al)
        p.inc_after("vector", "ve", 1,
                    lambda e: e.tensor_scalar_mul(THR[0:1, 0:2], ps_al[0:1, 0:2],
                                                  0.5 / (D * H)))
        p.wait("tensor", "ve")
        for wi_ in range(2):
            p.inc_after("tensor", "te", 1,
                        lambda e, wi_=wi_: e.matmul(ps_al[:, 4 + wi_:5 + wi_], ONE1[0:1, :],
                                                    THR[0:1, wi_:wi_ + 1],
                                                    start=True, stop=True))
        te_th = p.now("te")
        p.wait("vector", "te", te_th)
        p.inc_after("vector", "ve", 1,
                    lambda e: e.tensor_copy(TH[:, 0:2], ps_al[:, 4:6]))
        ve_th = p.now("ve")

        p.wait("sync", "ve", ve_th)
        for wi_, (wr, nrc, ncol, dest) in enumerate(((wup_r, ND, H, WUQ), (wdn_r, NHC, D, WDQ))):
            for rc in range(nrc):
                p.wait("sync", "se")
                p.wait("sync", "ve")
                p.inc_after("sync", "wld", 16,
                            lambda e, wr=wr, rc=rc, ncol=ncol: e.dma_start(
                                out=WSTG[:, 0:ncol], in_=wr[rc]))
                p.wait("scalar", "wld")
                p.inc_after("scalar", "se", 1,
                            lambda e, ncol=ncol: e.activation(SGN[:, 0:ncol], WSTG[:, 0:ncol],
                                                              AF.Sign))
                p.wait("vector", "wld")
                p.inc_after("vector", "ve", 1,
                            lambda e, ncol=ncol: e.scalar_tensor_tensor(
                                AB2[:, 0:ncol], WSTG[:, 0:ncol], -1.0, WSTG[:, 0:ncol],
                                ALU.mult, ALU.max))
                p.wait("vector", "ve")
                p.inc_after("vector", "ve", 1,
                            lambda e, wi_=wi_, ncol=ncol: e.tensor_scalar(
                                AB2[:, 0:ncol], AB2[:, 0:ncol], TH[:, wi_:wi_ + 1], None,
                                ALU.is_gt))
                p.wait("vector", "se")
                p.wait("vector", "ve")
                if wi_ == 0:
                    out_ap = dest[:, :].rearrange("q (dc f) -> q dc f", f=H)[:, rc, :]
                else:
                    out_ap = dest[:, rc * D:(rc + 1) * D]
                p.inc_after("vector", "ve", 1,
                            lambda e, out_ap=out_ap, ncol=ncol: e.tensor_tensor(
                                out_ap, SGN[:, 0:ncol], AB2[:, 0:ncol], ALU.mult))
        ve_w = p.now("ve")
        p.wait("tensor", "ve", ve_w)

        wuq_view = WUQ[:, :].rearrange("q (dc hc f) -> q dc hc f", hc=NHC, f=128)
        for s in range(NS):
            xnf = XNF[s % 2]
            if s > 0:
                p.wait("sync", "ve")
            for tt in range(TPS):
                p.inc_after("sync", "ld", 16,
                            lambda e, s=s, tt=tt: e.dma_start(out=XS32[:, tt * D:(tt + 1) * D],
                                                              in_=x_t[s * TPS + tt]))
            ld_s = p.now("ld")
            for tt in range(TPS):
                ssq, rstd = stats["ssq"], stats["rstd"]
                p.wait("scalar", "ld", ld_s)
                p.wait("scalar", "st")
                p.wait("scalar", "ve")
                p.inc_after("scalar", "se", 1,
                            lambda e, tt=tt: e.activation(OUT[:, 0:D],
                                                          XS32[:, tt * D:(tt + 1) * D],
                                                          AF.Square, accum_out=ssq[:, 0:1]))
                p.wait("scalar", "se")
                p.inc_after("scalar", "se", 1,
                            lambda e: e.activation(rstd[:, 0:1], ssq[:, 0:1], AF.Sqrt,
                                                   scale=1.0 / D))
                p.wait("vector", "se")
                p.inc_after("vector", "ve", 1,
                            lambda e: e.tensor_scalar_add(rstd[:, 0:1], rstd[:, 0:1], REPS))
                p.wait("vector", "ve")
                p.inc_after("vector", "ve", 1,
                            lambda e: e.reciprocal(rstd[:, 0:1], rstd[:, 0:1]))
                p.wait("vector", "ve")
                p.inc_after("vector", "ve", 1,
                            lambda e, tt=tt: e.tensor_scalar_mul(
                                XNT[:, tt * D:(tt + 1) * D], XS32[:, tt * D:(tt + 1) * D],
                                rstd[:, 0:1]))
            ve_xn = p.now("ve")
            p.wait("tensor", "ve", ve_xn)
            p.wait("tensor", "gld")
            xnf_view = xnf[:, :].rearrange("q (nd t) -> q nd t", t=TS)
            for tt in range(TPS):
                p.wait("tensor", "ve")
                fns = [
                    (lambda e, tt=tt, dc=dc: e.transpose(
                        ps_tr[:, dc * 128:(dc + 1) * 128],
                        XNT[:, tt * D + dc * 128: tt * D + (dc + 1) * 128], IDN[:, :]))
                    for dc in range(ND)]
                for i, fn in enumerate(fns):
                    if i == ND - 1:
                        p.inc_after("tensor", "te", 1, fn)
                    else:
                        p.t(fn)
                p.wait("vector", "te")
                p.inc_after("vector", "ve", 1,
                            lambda e, tt=tt, xnf_view=xnf_view: e.tensor_copy(
                                xnf_view[:, :, tt * 128:(tt + 1) * 128],
                                ps_tr[:, :].rearrange("q (nd f) -> q nd f", f=128)))
            ve_xf = p.now("ve")
            p.wait("tensor", "ve", ve_xf)

            for hc in range(NHC):
                p.wait("tensor", "se")
                fns = [
                    (lambda e, hc=hc, dc=dc, xnf=xnf: e.matmul(
                        ps_u[:, :], wuq_view[:, dc, hc, :], xnf[:, dc * TS:(dc + 1) * TS],
                        start=(dc == 0), stop=(dc == ND - 1)))
                    for dc in range(ND)]
                for i, fn in enumerate(fns):
                    if i == ND - 1:
                        p.inc_after("tensor", "te", 1, fn)
                    else:
                        p.t(fn)
                te_u = p.now("te")
                p.wait("scalar", "te", te_u)
                p.wait("scalar", "te")
                p.inc_after("scalar", "se", 1,
                            lambda e: e.activation(HS[:, :], ps_u[:, :], AF.Gelu))
                p.wait("scalar", "se")
                p.inc_after("scalar", "se", 1,
                            lambda e: e.activation(SQH[:, :], HS[:, :], AF.Square))
                se_h = p.now("se")
                p.wait("tensor", "se", se_h)
                p.inc_after("tensor", "te", 1,
                            lambda e, hc=hc: e.matmul(ps_sq[0:1, :], ONEC[:, 0:1], SQH[:, :],
                                                      start=(hc == 0), stop=(hc == NHC - 1)))
                for tt in range(TPS):
                    for half in range(2):
                        fn = lambda e, hc=hc, tt=tt, half=half: e.matmul(
                            ps_d[tt][:, half * 512:(half + 1) * 512],
                            HS[:, tt * 128:(tt + 1) * 128],
                            WDQ[:, hc * D + half * 512: hc * D + half * 512 + 512],
                            start=(hc == 0), stop=(hc == NHC - 1))
                        if tt == TPS - 1 and half == 1:
                            p.inc_after("tensor", "te", 1, fn)
                        else:
                            p.t(fn)
            te_dn = p.now("te")
            p.wait("vector", "te", te_dn)
            p.inc_after("vector", "ve", 1,
                        lambda e: e.tensor_scalar_mul(RROW[0:1, :], ps_sq[0:1, :], 1.0 / H))
            p.wait("scalar", "ve")
            p.inc_after("scalar", "se", 1,
                        lambda e: e.activation(RROW[0:1, :], RROW[0:1, :], AF.Sqrt))
            p.wait("vector", "se")
            p.inc_after("vector", "ve", 1,
                        lambda e: e.tensor_scalar_add(RROW[0:1, :], RROW[0:1, :], REPS))
            p.wait("vector", "ve")
            p.inc_after("vector", "ve", 1,
                        lambda e: e.reciprocal(RROW[0:1, :], RROW[0:1, :]))
            ve_rr = p.now("ve")
            p.wait("tensor", "ve", ve_rr)
            for tt in range(TPS):
                p.inc_after("tensor", "te", 1,
                            lambda e, tt=tt: e.matmul(
                                ps_u[:, tt:tt + 1],
                                RROW[0:1, tt * 128:(tt + 1) * 128], ONE1[0:1, 0:1],
                                start=True, stop=True))
            te_rt = p.now("te")
            p.wait("vector", "te", te_rt)
            p.inc_after("vector", "ve", 1,
                        lambda e: e.tensor_copy(RSTDC[:, 0:TPS], ps_u[:, 0:TPS]))
            ve_rt = p.now("ve")

            for tt in range(TPS):
                p.wait("vector", "te", te_dn)
                p.wait("vector", "ve", ve_rt)
                p.wait("vector", "ld", ld_s)
                p.inc_after("vector", "ve", 1,
                            lambda e, tt=tt: e.scalar_tensor_tensor(
                                XS32[:, tt * D:(tt + 1) * D], ps_d[tt][:, :],
                                RSTDC[:, tt:tt + 1], XS32[:, tt * D:(tt + 1) * D],
                                ALU.mult, ALU.add))
                p.wait("scalar", "st")
                p.wait("vector", "st")
                emit_ln_tile(p, XS32[:, tt * D:(tt + 1) * D], OUT[:, 0:D], stats,
                             sq_scr=OUT[:, 0:D])
                ve_out = p.now("ve")
                p.wait("sync", "ve", ve_out)
                p.inc_after("sync", "st", 16,
                            lambda e, s=s, tt=tt: e.dma_start(out=yo_t[s * TPS + tt],
                                                              in_=OUT[:, 0:D]))

        p.wait("sync", "st")
        p.barrier()
    if standalone:
        p.run()
        _octx.__exit__(None, None, None)
    return nc



def build_l3_ffn(T_=None, p=None, dram=None):
    Tl = T_ or T
    NTl = Tl // 128
    NQS = Tl // 512

    standalone = p is None
    if standalone:
        _nc = bass.Bass("TRN2", target_bir_lowering=False)
        _octx = ExitStack()
        _octx.__enter__()
        p = Prog(_nc, _octx)
        p.pfx = ""
        dram = lambda name, shape, dt, kind: _nc.dram_tensor(name, shape, dt, kind=kind).ap()
    nc = p.nc
    x = dram("x", [Tl, D], F32, "ExternalInput")
    wrT = dram("wrT", [D, D], F32, "ExternalInput")
    wkT = dram("wkT", [D, D], F32, "ExternalInput")
    wvT = dram("wvT", [D, D], F32, "ExternalInput")
    mur = dram("mur", [128, ND], F32, "ExternalInput")
    muk = dram("muk", [128, ND], F32, "ExternalInput")
    ident = dram("c_ident", [128, 128], F32, "ExternalInput")
    yo = dram("y", [Tl, D], F32, "ExternalOutput")

    x_t = x.rearrange("(nt p) d -> nt p d", p=128)
    yo_t = yo.rearrange("(nt p) d -> nt p d", p=128)

    with ExitStack() as ctx:
        sbt = lambda n, *a: ctx.enter_context(nc.sbuf_tensor(p.pfx + n, *a))
        pst = lambda n, *a: ctx.enter_context(nc.psum_tensor(p.pfx + n, *a))

        IDN = sbt("IDN", [128, 128], BF16)
        XF = sbt("XF3", [128, ND * Tl], BF16)
        WRT = sbt("WRT", [128, ND * D], BF16)
        WKT = sbt("WKT", [128, ND * D], BF16)
        WVT = sbt("WVT", [128, ND * D], BF16)
        MUR = sbt("MUR", [128, ND], F32)
        MUK = sbt("MUK", [128, ND], F32)
        OMR = sbt("OMR", [128, ND], F32)
        OMK = sbt("OMK", [128, ND], F32)
        XRFb = [sbt(f"XRF{i}", [128, ND * 512], BF16) for i in range(2)]
        XKFb = [sbt(f"XKF{i}", [128, ND * 512], BF16) for i in range(2)]
        KA = sbt("KA", [128, ND * 512], BF16)
        RT = sbt("RT", [128, 4 * D], BF16)
        RSC = sbt("RSC3", [128, 512], F32)
        KSC = sbt("KSC3", [128, 512], F32)
        SHARE = sbt("SHARE3", [128, 4 * D], BF16)
        XSB = SHARE[:, 0:2 * D]
        S32 = SHARE[:, :].bitcast(F32)
        OUT = sbt("OUT3", [128, D], F32)
        stats = alloc_stats(sbt)

        ps_tr = pst("ps_tr", [128, ND * 128], BF16)
        ps_a = pst("ps_a", [128, 512], F32)
        ps_po = pst("ps_po", [128, D], F32)

        p.inc_after("vector", "ve", 1, lambda e: e.memset(stats["eps"][:, 0:1], LN_EPS))

        p.inc_after("gpsimd", "gld", 16, lambda e: e.dma_start(out=IDN[:, :], in_=ident))
        p.inc_after("gpsimd", "gld", 16,
                    lambda e: e.dma_start(out=WRT[:, :].rearrange("q (dc f) -> q dc f", f=D),
                                          in_=wrT.rearrange("(dc q) f -> q dc f", q=128)))
        p.inc_after("gpsimd", "gld", 16,
                    lambda e: e.dma_start(out=WKT[:, :].rearrange("q (dc f) -> q dc f", f=D),
                                          in_=wkT.rearrange("(dc q) f -> q dc f", q=128)))
        p.inc_after("gpsimd", "gld", 16,
                    lambda e: e.dma_start(out=WVT[:, :].rearrange("q (dc f) -> q dc f", f=D),
                                          in_=wvT.rearrange("(dc q) f -> q dc f", q=128)))
        p.inc_after("sync", "cld", 16, lambda e: e.dma_start(out=MUR[:, :], in_=mur))
        p.inc_after("sync", "cld", 16, lambda e: e.dma_start(out=MUK[:, :], in_=muk))
        cld_c = p.now("cld")
        p.wait("vector", "cld", cld_c)
        p.inc_after("vector", "ve", 1, lambda e: e.tensor_scalar(OMR[:, :], MUR[:, :], -1.0, 1.0,
                                                                 ALU.mult, ALU.add))
        p.inc_after("vector", "ve", 1, lambda e: e.tensor_scalar(OMK[:, :], MUK[:, :], -1.0, 1.0,
                                                                 ALU.mult, ALU.add))

        te_nt = []
        for nt in range(NTl):
            buf = (nt % 2) * D
            par = f"gx{nt % 2}"
            if nt >= 2:
                p.wait("gpsimd", "te", te_nt[nt - 2])
            p.inc_after("gpsimd", par, 16,
                        lambda e, nt=nt, buf=buf: e.dma_start(out=XSB[:, buf:buf + D], in_=x_t[nt]))
            p.wait("tensor", par)
            if nt == 0:
                p.wait("tensor", "gld")
            p.wait("tensor", "ve")
            fns = [
                (lambda e, nt=nt, dc=dc, buf=buf: e.transpose(
                    ps_tr[:, dc * 128:(dc + 1) * 128],
                    XSB[:, buf + dc * 128: buf + (dc + 1) * 128], IDN[:, :]))
                for dc in range(ND)]
            for i, fn in enumerate(fns):
                if i == ND - 1:
                    p.inc_after("tensor", "te", 1, fn)
                else:
                    p.t(fn)
            te_nt.append(p.now("te"))
            p.wait("vector", "te")
            xf_view = XF[:, :].rearrange("q (nd t) -> q nd t", t=Tl)
            p.inc_after("vector", "ve", 1,
                        lambda e, nt=nt, xf_view=xf_view: e.tensor_copy(
                            xf_view[:, :, nt * 128:(nt + 1) * 128],
                            ps_tr[:, :].rearrange("q (nd f) -> q nd f", f=128)))
        ve_xf = p.now("ve")
        p.wait("tensor", "ve", ve_xf)

        for qs in range(NQS):
            t0 = qs * 512
            XRF = XRFb[qs % 2]
            XKF = XKFb[qs % 2]
            p.wait("vector", "ve", ve_xf)
            p.wait("vector", "te")
            for dc in range(ND):
                for (XT, MUc, OMc) in ((XRF, MUR, OMR), (XKF, MUK, OMK)):
                    p.inc_after("vector", "ve", 1,
                                lambda e, dc=dc, XT=XT, MUc=MUc, t0=t0: e.tensor_scalar(
                                    RSC[:, :], XF[:, dc * Tl + t0: dc * Tl + t0 + 512],
                                    MUc[:, dc:dc + 1], None, ALU.mult))
                    p.wait("vector", "ve")
                    if t0 == 0:
                        p.inc_after("vector", "ve", 1,
                                    lambda e, dc=dc, XT=XT, OMc=OMc: e.scalar_tensor_tensor(
                                        XT[:, dc * 512 + 1: dc * 512 + 512],
                                        XF[:, dc * Tl: dc * Tl + 511],
                                        OMc[:, dc:dc + 1], RSC[:, 1:512],
                                        ALU.mult, ALU.add))
                        p.wait("vector", "ve")
                        p.inc_after("vector", "ve", 1,
                                    lambda e, dc=dc, XT=XT: e.tensor_copy(
                                        XT[:, dc * 512: dc * 512 + 1], RSC[:, 0:1]))
                    else:
                        p.inc_after("vector", "ve", 1,
                                    lambda e, dc=dc, XT=XT, OMc=OMc, t0=t0: e.scalar_tensor_tensor(
                                        XT[:, dc * 512: dc * 512 + 512],
                                        XF[:, dc * Tl + t0 - 1: dc * Tl + t0 + 511],
                                        OMc[:, dc:dc + 1], RSC[:, :],
                                        ALU.mult, ALU.add))
                    p.wait("vector", "ve")
            ve_mix = p.now("ve")
            p.wait("tensor", "ve", ve_mix)

            for tt in range(4):
                for half in range(2):
                    if qs > 0 or tt > 0 or half > 0:
                        p.wait("tensor", "se")
                    fns = [
                        (lambda e, tt=tt, half=half, dc=dc, XRF=XRF: e.matmul(
                            ps_a[:, :],
                            XRF[:, dc * 512 + tt * 128: dc * 512 + tt * 128 + 128],
                            WRT[:, dc * D + half * 512: dc * D + half * 512 + 512],
                            start=(dc == 0), stop=(dc == ND - 1)))
                        for dc in range(ND)]
                    for i, fn in enumerate(fns):
                        if i == ND - 1:
                            p.inc_after("tensor", "te", 1, fn)
                        else:
                            p.t(fn)
                    p.wait("scalar", "te")
                    p.inc_after("scalar", "se", 1,
                                lambda e, tt=tt, half=half: e.activation(
                                    RT[:, tt * D + half * 512: tt * D + half * 512 + 512],
                                    ps_a[:, :], AF.Sigmoid))
            for fc in range(ND):
                p.wait("tensor", "ve")
                p.wait("tensor", "se")
                fns = [
                    (lambda e, fc=fc, dc=dc, XKF=XKF: e.matmul(
                        ps_a[:, :], WKT[:, dc * D + fc * 128: dc * D + fc * 128 + 128],
                        XKF[:, dc * 512:(dc + 1) * 512],
                        start=(dc == 0), stop=(dc == ND - 1)))
                    for dc in range(ND)]
                for i, fn in enumerate(fns):
                    if i == ND - 1:
                        p.inc_after("tensor", "te", 1, fn)
                    else:
                        p.t(fn)
                p.wait("vector", "te")
                p.inc_after("vector", "ve", 1,
                            lambda e: e.tensor_scalar_max(KSC[:, :], ps_a[:, :], 0.0))
                p.wait("scalar", "ve")
                p.inc_after("scalar", "se", 1,
                            lambda e, fc=fc: e.activation(KA[:, fc * 512:(fc + 1) * 512],
                                                          KSC[:, :], AF.Square))
            se_ka = p.now("se")
            p.wait("tensor", "se", se_ka)

            for tt in range(4):
                nt = qs * 4 + tt
                p.wait("sync", "ve")
                p.inc_after("sync", "ld", 16,
                            lambda e, nt=nt: e.dma_start(out=S32[:, 0:D], in_=x_t[nt]))
                ld_nt = p.now("ld")
                if nt > 0:
                    p.wait("tensor", "ve")
                for half in range(2):
                    fns = [
                        (lambda e, tt=tt, half=half, fc=fc: e.matmul(
                            ps_po[:, half * 512:(half + 1) * 512],
                            KA[:, fc * 512 + tt * 128: fc * 512 + tt * 128 + 128],
                            WVT[:, fc * D + half * 512: fc * D + half * 512 + 512],
                            start=(fc == 0), stop=(fc == ND - 1)))
                        for fc in range(ND)]
                    for i, fn in enumerate(fns):
                        if i == ND - 1:
                            p.inc_after("tensor", "te", 1, fn)
                        else:
                            p.t(fn)
                te_po = p.now("te")
                p.wait("vector", "te", te_po)
                p.wait("vector", "ld", ld_nt)
                p.wait("vector", "se")
                p.wait("vector", "st")
                p.wait("scalar", "st")
                p.inc_after("vector", "ve", 1,
                            lambda e, tt=tt: e.tensor_tensor(OUT[:, 0:D], RT[:, tt * D:(tt + 1) * D],
                                                             ps_po[:, :], ALU.mult))
                p.wait("vector", "ve")
                p.inc_after("vector", "ve", 1,
                            lambda e: e.tensor_tensor(S32[:, 0:D], S32[:, 0:D], OUT[:, 0:D],
                                                      ALU.add))
                emit_ln_tile(p, S32[:, 0:D], OUT[:, 0:D], stats, sq_scr=OUT[:, 0:D])
                ve_out = p.now("ve")
                p.wait("sync", "ve", ve_out)
                p.inc_after("sync", "st", 16,
                            lambda e, nt=nt: e.dma_start(out=yo_t[nt], in_=OUT[:, 0:D]))

        p.wait("sync", "st")
        p.barrier()
    if standalone:
        p.run()
        _octx.__exit__(None, None, None)
    return nc



CHAIN = [
    ("m_", lambda **kw: build_mamba(**kw)),
    ("f0_", lambda **kw: build_ffn_int8(**kw)),
    ("a1_", lambda **kw: build_attn("l1", **kw)),
    ("f1_", lambda **kw: build_ffn_int8(**kw)),
    ("a2_", lambda **kw: build_l2_attn(**kw)),
    ("f2_", lambda **kw: build_l2_ffn(**kw)),
    ("a3_", lambda **kw: build_attn("l3", **kw)),
    ("f3_", lambda **kw: build_l3_ffn(**kw)),
]


def build_fused(T_=None):
    Tl = T_ or T
    nc = bass.Bass("TRN2", target_bir_lowering=False)
    octx = ExitStack()
    octx.__enter__()
    p = Prog(nc, octx)
    xs_dram = [nc.dram_tensor("x", [Tl, D], F32, kind="ExternalInput").ap()]
    for i in range(len(CHAIN) - 1):
        xs_dram.append(nc.dram_tensor(f"xi{i}", [Tl, D], F32, kind="Internal").ap())
    y_final = nc.dram_tensor("y", [Tl, D], F32, kind="ExternalOutput").ap()

    for k, (pfx, fn) in enumerate(CHAIN):
        p.pfx = pfx
        xin = xs_dram[k]
        yout = xs_dram[k + 1] if k < len(CHAIN) - 1 else y_final

        def dram(name, shape, dt, kind, pfx=pfx, xin=xin, yout=yout):
            if name == "x":
                return xin
            if name == "y":
                return yout
            return nc.dram_tensor(pfx + name, shape, dt, kind="ExternalInput").ap()

        fn(T_=Tl, p=p, dram=dram)
    p.run()
    octx.__exit__(None, None, None)
    return nc



def make_consts():
    return {"c_ident": np.eye(128, dtype=np.float32)}


def _rope_consts(Tl):
    theta = 1.0 / 10000.0 ** (np.arange(0, 64, 2, dtype=np.float32) / 64)
    COSC = np.zeros((128, 8), np.float32)
    SINC = np.zeros((128, 8), np.float32)
    for col in range(8):
        for p_ in range(128):
            r = col * 128 + p_
            h = r // 64
            j = (r % 64) // 2
            ang = np.float32(h) * theta[j]
            COSC[p_, col] = np.cos(ang)
            SINC[p_, col] = (-np.sin(ang)) if (r % 2 == 0) else np.sin(ang)
    perm = np.zeros((128, 128), np.float32)
    for i in range(0, 128, 2):
        perm[i + 1, i] = 1.0
        perm[i, i + 1] = 1.0
    mask = np.triu(np.ones((128, 128), np.float32))
    return dict(c_cosc=COSC, c_sinc=SINC, c_perm=perm, c_mask=mask)


def _l2_consts():
    NH = 8
    gammas = 1.0 - np.exp(np.linspace(np.log(1 / 512), np.log(1 / 32), NH))
    GP = np.zeros((128, NH * 36), np.float32)
    for h in range(NH):
        for mm in range(-3, 33):
            e = 128.0 * mm - np.arange(128)
            GP[:, h * 36 + mm + 3] = np.exp(e * np.log(gammas[h]))
    RAMP = np.zeros((128, NH * 512), np.float32)
    for h in range(NH):
        RAMP[:, h * 512:(h + 1) * 512] = (gammas[h] ** np.arange(512))[None, :]
    return dict(c_gp=GP, c_ramp=RAMP, c_mask=np.triu(np.ones((128, 128), np.float32)))


def _ehbc():
    ehb = np.zeros((32, 16 * 128), np.float32)
    ehc = np.zeros((32, 16 * 128), np.float32)
    for n in range(16):
        ehb[n, n * 128:(n + 1) * 128] = 1.0
        ehc[16 + n, n * 128:(n + 1) * 128] = 1.0
    return dict(c_ehb=ehb, c_ehc=ehc)


def _col8(v):
    return np.ascontiguousarray(v.reshape(8, 128).T)


def _col16(v):
    return np.ascontiguousarray(v.reshape(16, 128).T)


def _ct(a):
    return np.ascontiguousarray(a)



def kernel(**inputs):
    ii = inputs
    consts = make_consts()

    w_mamba = dict(
        lnw=_ct(ii["L0_attn_ln_w"][None]), lnb=_ct(ii["L0_attn_ln_b"][None]),
        wipT=_ct(ii["L0_attn_in_proj_w"].T),
        convw=_ct(ii["L0_attn_conv_w"].reshape(2048, 4).reshape(16, 128, 4)
                  .transpose(1, 0, 2).reshape(128, 64)),
        convb=_col16(ii["L0_attn_conv_b"]),
        wxpT=_ct(ii["L0_attn_x_proj_w"].T),
        wdtT=_ct(ii["L0_attn_dt_proj_w"].T),
        dtb=_col16(ii["L0_attn_dt_proj_b"]),
        alog=_ct(ii["L0_attn_A_log"].reshape(16, 128, 16).transpose(1, 0, 2)
                 .reshape(128, 256)),
        dmat=_col16(ii["L0_attn_D"]),
        wopT=_ct(ii["L0_attn_out_proj_w"].T),
        **_ehbc(), **consts)
    w_ffn0 = dict(wgT=_ct(ii["L0_ffn_w_gate"].T), wuT=_ct(ii["L0_ffn_w_up"].T),
                  wdT=_ct(ii["L0_ffn_w_down"].T), **consts)
    w_attn1 = dict(wqkv=_ct(ii["L1_attn_W_qkv"]), bqkv=_ct(ii["L1_attn_b_qkv"][None]),
                   bqkvc=_ct(ii["L1_attn_b_qkv"].reshape(24, 128).T),
                   wproj=_ct(ii["L1_attn_W_proj"]), bproj=_ct(ii["L1_attn_b_proj"][None]),
                   aqkv=_ct(ii["L1_attn_A_qkv"]), bl_qkv=_ct(ii["L1_attn_B_qkv"]),
                   aproj=_ct(ii["L1_attn_A_proj"]), bl_proj=_ct(ii["L1_attn_B_proj"]),
                   **consts)
    w_ffn1 = dict(wgT=_ct(ii["L1_ffn_w_gate"].T), wuT=_ct(ii["L1_ffn_w_up"].T),
                  wdT=_ct(ii["L1_ffn_w_down"].T), **consts)
    w_attn2 = dict(wq=_ct(ii["L2_attn_Wq"].transpose(1, 0, 2).reshape(128, 8 * 128)),
                   wk=_ct(ii["L2_attn_Wk"].transpose(1, 0, 2).reshape(128, 8 * 128)),
                   wv=_ct(ii["L2_attn_Wv"].transpose(1, 0, 2).reshape(128, 8 * 128)),
                   w1=_ct(ii["L2_attn_W1"]), w2=_ct(ii["L2_attn_W2"]),
                   gnw=_col8(ii["L2_attn_gn_w"]), gnb=_col8(ii["L2_attn_gn_b"]),
                   **_l2_consts(), **consts)
    w_ffn2 = dict(wupT=_ct(ii["L2_ffn_w_up"].T), wdnT=_ct(ii["L2_ffn_w_down"].T), **consts)
    w_attn3 = dict(wqkv=_ct(ii["L3_attn_W_attn"]), bqkv=_ct(ii["L3_attn_b_attn"][None]),
                   bqkvc=_ct(ii["L3_attn_b_attn"].reshape(24, 128).T),
                   wproj=_ct(ii["L3_attn_W_proj"]), bproj=_ct(ii["L3_attn_b_proj"][None]),
                   **_rope_consts(T), **consts)
    w_ffn3 = dict(wrT=_ct(ii["L3_ffn_W_r"].T), wkT=_ct(ii["L3_ffn_W_k"].T),
                  wvT=_ct(ii["L3_ffn_W_v"].T),
                  mur=_col8(ii["L3_ffn_mu_r"]), muk=_col8(ii["L3_ffn_mu_k"]),
                  **consts)

    wmaps = [w_mamba, w_ffn0, w_attn1, w_ffn1, w_attn2, w_ffn2, w_attn3, w_ffn3]
    weights = {}
    for (pfx, _), wm in zip(CHAIN, wmaps):
        for k, v in wm.items():
            weights[pfx + k] = v

    nc = build_fused()
    in_maps = [dict(x=np.ascontiguousarray(ii["L0_attn_x"][b]), **weights) for b in range(8)]
    res = run_bass_kernel_spmd(nc, in_maps, core_ids=list(range(8)))
    return np.stack([np.asarray(res.results[b]["y"]) for b in range(8)], axis=0)


# version 05
# speedup vs baseline: 1.1448x; 1.1412x over previous
import numpy as np
from contextlib import ExitStack

import concourse.bass as bass
import concourse.mybir as mybir
from concourse.bass_utils import run_bass_kernel_spmd

F32 = mybir.dt.float32
BF16 = mybir.dt.bfloat16
I32 = mybir.dt.int32
AF = mybir.ActivationFunctionType
ALU = mybir.AluOpType
AX = mybir.AxisListType

T = 4096
D = 1024
NT = T // 128
ND = D // 128

LN_EPS = 1e-5


class Ctr:
    def __init__(self):
        self.v = 0

    def add(self, n):
        self.v += n
        return self.v


class Prog:

    ENGS = ("tensor", "vector", "scalar", "sync", "gpsimd")

    def __init__(self, nc, ctx):
        self.nc = nc
        self.ctx = ctx
        self.streams = {e: [] for e in self.ENGS}
        self.sems = {}
        self.ctrs = {}
        self.pfx = ""

    def sem(self, name):
        if name not in self.sems:
            self.sems[name] = self.ctx.enter_context(self.nc.semaphore(name))
            self.ctrs[name] = Ctr()
        return self.sems[name]

    def emit(self, eng, fn):
        self.streams[eng].append(fn)

    def wait(self, eng, sem_name, val=None):
        s = self.sem(sem_name)
        v = self.ctrs[sem_name].v if val is None else val
        if v > 0:
            self.emit(eng, lambda e, s=s, v=v: e.wait_ge(s, v))
        return v

    def inc_after(self, eng, sem_name, n, fn):
        s = self.sem(sem_name)
        self.ctrs[sem_name].add(n)
        self.emit(eng, lambda e, s=s, n=n, fn=fn: fn(e).then_inc(s, n))

    def now(self, sem_name):
        self.sem(sem_name)
        return self.ctrs[sem_name].v

    def barrier(self):
        s = self.sem("bar")
        self.ctrs["bar"].add(len(self.ENGS))
        target = self.ctrs["bar"].v
        for eng in self.ENGS:
            self.emit(eng, lambda e, s=s: e.sem_inc(s, 1))
        for eng in self.ENGS:
            self.emit(eng, lambda e, s=s, v=target: e.wait_ge(s, v))

    def run(self):
        block = self.ctx.enter_context(self.nc.Block())
        for eng in self.ENGS:
            fns = self.streams[eng]
            getattr(block, eng)(lambda e, fns=fns: [f(e) for f in fns])

    def t(self, fn):
        self.emit("tensor", fn)

    def v(self, fn):
        self.emit("vector", fn)

    def s(self, fn):
        self.emit("scalar", fn)


def emit_ln_tile(p, s32_ap, out_ap, stats, sq_scr=None):
    ssq, ssum, mean, var, rstd = (
        stats["ssq"], stats["ssum"], stats["mean"], stats["var"], stats["rstd"])
    if sq_scr is None:
        sq_scr = stats["sq_scr"][:, :]
    p.wait("scalar", "ve")
    p.inc_after("scalar", "se", 1,
                lambda e: e.activation(sq_scr, s32_ap, AF.Square, accum_out=ssq[:, 0:1]))
    p.wait("vector", "ve")
    p.inc_after("vector", "ve", 1,
                lambda e: e.reduce_sum(ssum[:, 0:1], s32_ap, axis=AX.X))
    p.wait("vector", "ve")
    p.wait("vector", "se")
    p.inc_after("vector", "ve", 1,
                lambda e: e.tensor_scalar_mul(mean[:, 0:1], ssum[:, 0:1], 1.0 / D))
    p.wait("vector", "ve")
    p.inc_after("vector", "ve", 1,
                lambda e: e.tensor_scalar_mul(var[:, 0:1], ssq[:, 0:1], 1.0 / D))
    p.wait("vector", "ve")
    p.inc_after("vector", "ve", 1,
                lambda e: e.tensor_mul(ssum[:, 0:1], mean[:, 0:1], mean[:, 0:1]))
    p.wait("vector", "ve")
    p.inc_after("vector", "ve", 1,
                lambda e: e.tensor_sub(var[:, 0:1], var[:, 0:1], ssum[:, 0:1]))
    p.wait("scalar", "ve")
    p.inc_after("scalar", "se", 1,
                lambda e: e.activation(rstd[:, 0:1], var[:, 0:1], AF.Ln,
                                       bias=stats["eps"][:, 0:1]))
    p.wait("scalar", "se")
    p.inc_after("scalar", "se", 1,
                lambda e: e.activation(rstd[:, 0:1], rstd[:, 0:1], AF.Exp, scale=-0.5))
    p.wait("vector", "se")
    p.inc_after("vector", "ve", 1,
                lambda e: e.tensor_scalar(out_ap, s32_ap, mean[:, 0:1], rstd[:, 0:1],
                                          ALU.subtract, ALU.mult))


def alloc_stats(sbt, tag=""):
    return dict(
        ssq=sbt("ssq" + tag, [128, 1], F32),
        ssum=sbt("ssum" + tag, [128, 1], F32),
        mean=sbt("mean" + tag, [128, 1], F32),
        var=sbt("var" + tag, [128, 1], F32),
        rstd=sbt("rstd" + tag, [128, 1], F32),
        eps=sbt("eps" + tag, [128, 1], F32),
    )



def build_ffn_int8(H=3072, T_=None, p=None, dram=None):
    T = T_ or globals()["T"]
    NH = H // 128
    TS = 256
    NS = T // TS
    TPS = TS // 128

    standalone = p is None
    if standalone:
        _nc = bass.Bass("TRN2", target_bir_lowering=False)
        _octx = ExitStack()
        _octx.__enter__()
        p = Prog(_nc, _octx)
        p.pfx = ""
        dram = lambda name, shape, dt, kind: _nc.dram_tensor(name, shape, dt, kind=kind).ap()
    nc = p.nc
    x = dram("x", [T, D], F32, "ExternalInput")
    wgT = dram("wgT", [D, H], I32, "ExternalInput")
    wuT = dram("wuT", [D, H], I32, "ExternalInput")
    wdT = dram("wdT", [H, D], I32, "ExternalInput")
    ident = dram("c_ident", [128, 128], F32, "ExternalInput")
    yo = dram("y", [T, D], F32, "ExternalOutput")

    with ExitStack() as ctx:
        sbt = lambda n, *a: ctx.enter_context(nc.sbuf_tensor(p.pfx + n, *a))
        pst = lambda n, *a: ctx.enter_context(nc.psum_tensor(p.pfx + n, *a))

        IDN = sbt("IDN", [128, 128], BF16)
        WG = sbt("WG", [128, NH * ND * 128], BF16)
        WU = sbt("WU", [128, NH * ND * 128], BF16)
        WD = sbt("WD", [128, NH * D], BF16)
        WSTG = [sbt("WSTG0", [128, 3072], I32)]
        XSB = sbt("XSB", [128, TPS * D], BF16)
        XS32 = sbt("XS32", [128, TPS * D], F32)
        XF = [sbt(f"XF{i}", [128, ND * TS], BF16) for i in range(2)]
        GS = sbt("GS", [128, TS], BF16)
        HS = sbt("HS", [128, TS], BF16)
        S32 = sbt("S32", [128, TPS * D], F32)
        OUT = sbt("OUT", [128, TPS * D], F32)
        stats = alloc_stats(sbt)

        ps_tr = pst("ps_tr", [128, ND * 128], BF16)
        ps_g = pst("ps_g", [128, TS], F32)
        ps_u = pst("ps_u", [128, TS], F32)
        ps_d = [pst(f"ps_d{i}", [128, D], F32) for i in range(TPS)]

        p.inc_after("vector", "ve", 1, lambda e: e.memset(stats["eps"][:, 0:1], LN_EPS))
        p.inc_after("gpsimd", "gld", 16, lambda e: e.dma_start(out=IDN[:, :], in_=ident))

        def load_dequant(wT, dest, nrow_chunks, ncols, kind):
            src = wT.rearrange("(rc p) f -> rc p f", p=128)
            for rc in range(nrow_chunks):
                stg = WSTG[0]
                p.wait("sync", "se")
                p.inc_after("sync", "wld", 16,
                            lambda e, rc=rc, stg=stg, ncols=ncols: e.dma_start(
                                out=stg[:, 0:ncols], in_=src[rc]))
                p.wait("scalar", "wld")
                if kind == "d":
                    out_ap = dest[:, rc * ncols:(rc + 1) * ncols]
                else:
                    out_ap = dest[:, :].rearrange(
                        "p (hc nd f) -> p hc nd f", nd=ND, f=128)[:, :, rc, :]
                p.inc_after("scalar", "se", 1,
                            lambda e, stg=stg, out_ap=out_ap, ncols=ncols:
                            e.activation(out_ap, stg[:, 0:ncols], AF.Copy, scale=1.0 / 127.0))

        load_dequant(wgT, WG, ND, H, "g")
        load_dequant(wuT, WU, ND, H, "u")
        load_dequant(wdT, WD, NH, D, "d")
        se_w = p.now("se")

        x_t = x.rearrange("(nt p) d -> nt p d", p=128)
        yo_t = yo.rearrange("(nt p) d -> nt p d", p=128)

        p.wait("tensor", "se", se_w)

        for s in range(NS):
            xf = XF[s % 2]
            if s > 0:
                p.wait("gpsimd", "te")
                p.wait("sync", "ve")
            for tt in range(TPS):
                nt = s * TPS + tt
                p.inc_after("gpsimd", "gld", 16,
                            lambda e, nt=nt, tt=tt: e.dma_start(
                                out=XSB[:, tt * D:(tt + 1) * D], in_=x_t[nt]))
                p.inc_after("sync", "ld", 16,
                            lambda e, nt=nt, tt=tt: e.dma_start(
                                out=XS32[:, tt * D:(tt + 1) * D], in_=x_t[nt]))
            ld_slab = p.now("ld")
            gld_slab = p.now("gld")

            p.wait("tensor", "ld", ld_slab)
            p.wait("tensor", "gld", gld_slab)
            xf_view = xf[:, :].rearrange("p (nd ts) -> p nd ts", ts=TS)
            for tt in range(TPS):
                p.wait("tensor", "ve")
                for dc in range(ND):
                    fn = lambda e, tt=tt, dc=dc: e.transpose(
                        ps_tr[:, dc * 128:(dc + 1) * 128],
                        XSB[:, tt * D + dc * 128: tt * D + (dc + 1) * 128], IDN[:, :])
                    if dc == ND - 1:
                        p.inc_after("tensor", "te", 1, fn)
                    else:
                        p.t(fn)
                p.wait("vector", "te")
                p.inc_after("vector", "ve", 1,
                            lambda e, xf_view=xf_view, tt=tt: e.tensor_copy(
                                xf_view[:, :, tt * 128:(tt + 1) * 128],
                                ps_tr[:, :].rearrange("p (nd f) -> p nd f", f=128)))
            ve_xf = p.now("ve")

            p.wait("tensor", "ve", ve_xf)
            for hc in range(NH):
                p.wait("tensor", "se")
                p.wait("tensor", "ve")
                for dc in range(ND):
                    fn = lambda e, hc=hc, dc=dc, xf=xf: e.matmul(
                        ps_g[:, :], WG[:, (hc * ND + dc) * 128:(hc * ND + dc + 1) * 128],
                        xf[:, dc * TS:(dc + 1) * TS],
                        start=(dc == 0), stop=(dc == ND - 1))
                    if dc == ND - 1:
                        p.inc_after("tensor", "te", 1, fn)
                    else:
                        p.t(fn)
                for dc in range(ND):
                    fn = lambda e, hc=hc, dc=dc, xf=xf: e.matmul(
                        ps_u[:, :], WU[:, (hc * ND + dc) * 128:(hc * ND + dc + 1) * 128],
                        xf[:, dc * TS:(dc + 1) * TS],
                        start=(dc == 0), stop=(dc == ND - 1))
                    if dc == ND - 1:
                        p.inc_after("tensor", "te", 1, fn)
                    else:
                        p.t(fn)
                te_gu = p.now("te")
                p.wait("scalar", "te", te_gu - 1)
                p.wait("scalar", "ve")
                p.inc_after("scalar", "se", 1,
                            lambda e: e.activation(GS[:, :], ps_g[:, :], AF.Silu))
                p.wait("vector", "te", te_gu)
                p.wait("vector", "se")
                p.inc_after("vector", "ve", 1,
                            lambda e: e.tensor_tensor(HS[:, :], GS[:, :], ps_u[:, :], ALU.mult))
                p.wait("tensor", "ve")
                for i, (tt, half) in enumerate([(a, b) for a in range(TPS) for b in range(2)]):
                    fn = lambda e, hc=hc, tt=tt, half=half: e.matmul(
                        ps_d[tt][:, half * 512:(half + 1) * 512],
                        HS[:, tt * 128:(tt + 1) * 128],
                        WD[:, hc * D + half * 512: hc * D + half * 512 + 512],
                        start=(hc == 0), stop=(hc == NH - 1))
                    if i == 2 * TPS - 1:
                        p.inc_after("tensor", "te", 1, fn)
                    else:
                        p.t(fn)
            te_down = p.now("te")

            for tt in range(TPS):
                nt = s * TPS + tt
                p.wait("vector", "te", te_down)
                p.wait("vector", "ld", ld_slab)
                p.inc_after("vector", "ve", 1,
                            lambda e, tt=tt: e.tensor_tensor(
                                S32[:, tt * D:(tt + 1) * D], XS32[:, tt * D:(tt + 1) * D],
                                ps_d[tt][:, :], ALU.add))
                p.wait("scalar", "st")
                p.wait("vector", "st")
                emit_ln_tile(p, S32[:, tt * D:(tt + 1) * D], OUT[:, tt * D:(tt + 1) * D], stats,
                             sq_scr=OUT[:, tt * D:(tt + 1) * D])
                ve_out = p.now("ve")
                p.wait("sync", "ve", ve_out)
                p.inc_after("sync", "st", 16,
                            lambda e, nt=nt, tt=tt: e.dma_start(
                                out=yo_t[nt], in_=OUT[:, tt * D:(tt + 1) * D]))

        p.wait("sync", "st")
        p.barrier()
    if standalone:
        p.run()
        _octx.__exit__(None, None, None)
    return nc




def build_attn(kind, T_=None, NHEADS=16, HG=4, p=None, dram=None):
    Tl = T_ or T
    NTl = Tl // 128
    HD = 64
    NG = NHEADS // HG
    GR = HG * HD
    GC = GR // 128
    NQS = Tl // 512
    F = 3 * D
    lora = kind == "l1"

    standalone = p is None
    if standalone:
        _nc = bass.Bass("TRN2", target_bir_lowering=False)
        _octx = ExitStack()
        _octx.__enter__()
        p = Prog(_nc, _octx)
        p.pfx = ""
        dram = lambda name, shape, dt, kind: _nc.dram_tensor(name, shape, dt, kind=kind).ap()
    nc = p.nc
    x = dram("x", [Tl, D], F32, "ExternalInput")
    wqkv = dram("wqkv", [D, F], F32, "ExternalInput")
    bqkv = dram("bqkv", [1, F], F32, "ExternalInput")
    bqkvc = dram("bqkvc", [128, F // 128], F32, "ExternalInput")
    wproj = dram("wproj", [D, D], F32, "ExternalInput")
    bproj = dram("bproj", [1, D], F32, "ExternalInput")
    if lora:
        aqkv = dram("aqkv", [D, 8], F32, "ExternalInput")
        bl_qkv = dram("bl_qkv", [8, F], F32, "ExternalInput")
        aproj = dram("aproj", [D, 8], F32, "ExternalInput")
        bl_proj = dram("bl_proj", [8, D], F32, "ExternalInput")
    else:
        ccos = dram("c_cosc", [128, 8], F32, "ExternalInput")
        csin = dram("c_sinc", [128, 8], F32, "ExternalInput")
        cperm = dram("c_perm", [128, 128], F32, "ExternalInput")
        cmask = dram("c_mask", [128, 128], F32, "ExternalInput")
    ident = dram("c_ident", [128, 128], F32, "ExternalInput")
    yo = dram("y", [Tl, D], F32, "ExternalOutput")

    x_t = x.rearrange("(nt p) d -> nt p d", p=128)
    yo_t = yo.rearrange("(nt p) d -> nt p d", p=128)

    with ExitStack() as ctx:
        sbt = lambda n, *a: ctx.enter_context(nc.sbuf_tensor(p.pfx + n, *a))
        pst = lambda n, *a: ctx.enter_context(nc.psum_tensor(p.pfx + n, *a))

        IDN = sbt("IDN", [128, 128], BF16)
        XF = sbt("XFa", [128, ND * Tl], BF16)
        OF = sbt("OFa", [128, ND * Tl], BF16)
        WQG = sbt("WQG", [128, ND * 2 * GR], BF16)
        KGW = sbt("KGW", [128, max(GC * Tl, ND * D)], BF16)
        KG = KGW
        WPROJ = KGW
        VT = sbt("VT", [128, NTl * HG * 65], BF16)
        QF = sbt("QF", [128, GC * 512], BF16)
        BQC = sbt("BQC", [128, F // 128], F32)
        BVB = sbt("BVB", [128, GR], BF16)
        PBJ = sbt("PBJ", [128, 1024], BF16)
        PB = [PBJ[:, 0:512], PBJ[:, 512:1024]]
        PBJ2 = sbt("PBJ2", [128, 1024], BF16)
        PB2 = [PBJ2[:, 0:512], PBJ2[:, 512:1024]]
        OS = sbt("OS", [128, 512], BF16)
        OUT = sbt("OUTa", [128, D], F32)
        ONE1 = sbt("ONE1", [1, 128], F32)
        SHARE = sbt("SHARE", [128, 4 * D], BF16)
        XSB = SHARE[:, 0:2 * D]
        S32 = SHARE[:, :].bitcast(F32)
        BPB = sbt("BPB", [128, D], BF16)
        BPR = sbt("BPR", [1, D], F32)
        DEN = BPR[0:1, 0:512]
        RECIP = BPR[0:1, 512:1024]
        BVR = BPR[0:1, 0:GR]
        stats = alloc_stats(sbt)
        if lora:
            AQ = sbt("AQ", [128, ND * 8], BF16)
            BLG = sbt("BLG", [8, 3 * GR], BF16)
            APJ = sbt("APJ", [128, ND * 8], BF16)
            BLP = sbt("BLP", [8, D], BF16)
            MIDB = sbt("MIDB", [8, Tl], BF16)
        else:
            COS = sbt("COSC", [128, 8], F32)
            SIN = sbt("SINC", [128, 8], F32)
            PERM = sbt("PERM", [128, 128], BF16)
            MASK = sbt("MASKT", [128, 128], BF16)
            RSC = PBJ[:, :].bitcast(F32)

        ps_a = pst("ps_a", [128, 512], F32)
        ps_rb = pst("ps_rb", [128, 512], F32)
        ps_tr_ctx = ExitStack()
        ps_tr = ps_tr_ctx.enter_context(nc.psum_tensor(p.pfx + "ps_tr", [128, ND * 128], BF16))

        def mm_chain(fns, extra=None):
            all_fns = fns + ([extra] if extra is not None else [])
            for i, fn in enumerate(all_fns):
                if i == len(all_fns) - 1:
                    p.inc_after("tensor", "te", 1, fn)
                else:
                    p.t(fn)
            return p.now("te")

        p.inc_after("vector", "ve", 1, lambda e: e.memset(stats["eps"][:, 0:1], LN_EPS))
        p.inc_after("vector", "ve", 1, lambda e: e.memset(ONE1[:, :], 1.0))
        p.inc_after("vector", "ve", 1, lambda e: e.memset(
            VT[:, :].rearrange("q (nt h c) -> q nt h c", h=HG, c=65)[:, :, :, 64:65], 1.0))

        p.inc_after("gpsimd", "gld", 16, lambda e: e.dma_start(out=IDN[:, :], in_=ident))
        p.inc_after("sync", "ld", 16, lambda e: e.dma_start(out=BQC[:, :], in_=bqkvc))
        if lora:
            p.inc_after("gpsimd", "gld", 16, lambda e: e.dma_start(
                out=AQ[:, :].rearrange("p (dc r) -> p dc r", r=8),
                in_=aqkv.rearrange("(dc p) r -> p dc r", p=128)))
            p.inc_after("gpsimd", "gld", 16, lambda e: e.dma_start(
                out=APJ[:, :].rearrange("p (dc r) -> p dc r", r=8),
                in_=aproj.rearrange("(dc p) r -> p dc r", p=128)))
            p.inc_after("gpsimd", "gld", 16, lambda e: e.dma_start(out=BLP[:, :], in_=bl_proj))
        else:
            p.inc_after("sync", "ld", 16, lambda e: e.dma_start(out=COS[:, :], in_=ccos))
            p.inc_after("sync", "ld", 16, lambda e: e.dma_start(out=SIN[:, :], in_=csin))
            p.inc_after("gpsimd", "gld", 16, lambda e: e.dma_start(out=PERM[:, :], in_=cperm))
            p.inc_after("gpsimd", "gld", 16, lambda e: e.dma_start(out=MASK[:, :], in_=cmask))

        te_nt = []
        for nt in range(NTl):
            buf = (nt % 2) * D
            par = f"gx{nt % 2}"
            if nt >= 2:
                p.wait("gpsimd", "te", te_nt[nt - 2])
            p.inc_after("gpsimd", par, 16,
                        lambda e, nt=nt, buf=buf: e.dma_start(out=XSB[:, buf:buf + D], in_=x_t[nt]))
            p.wait("tensor", par)
            if nt == 0:
                p.wait("tensor", "gld")
            p.wait("tensor", "ve")
            fns = [
                (lambda e, nt=nt, dc=dc, buf=buf: e.transpose(
                    ps_tr[:, dc * 128:(dc + 1) * 128],
                    XSB[:, buf + dc * 128: buf + (dc + 1) * 128], IDN[:, :]))
                for dc in range(ND)]
            te_nt.append(mm_chain(fns))
            p.wait("vector", "te")
            xf_view = XF[:, :].rearrange("q (nd t) -> q nd t", t=Tl)
            p.inc_after("vector", "ve", 1,
                        lambda e, nt=nt, xf_view=xf_view: e.tensor_copy(
                            xf_view[:, :, nt * 128:(nt + 1) * 128],
                            ps_tr[:, :].rearrange("q (nd f) -> q nd f", f=128)))
        ve_xf = p.now("ve")
        p.wait("tensor", "ve", ve_xf)
        ps_tr_ctx.close()
        ps_at_ctx = ExitStack()
        _pat = lambda n, s, d: ps_at_ctx.enter_context(nc.psum_tensor(p.pfx + n, s, d))
        ps_sa = [_pat(f"ps_sa{i}", [128, 512], F32) for i in range(2)]
        ps_sb = [_pat(f"ps_sb{i}", [128, 512], F32) for i in range(2)]
        ps_oa = _pat("ps_oa", [128, 512], F32)
        ps_ob = _pat("ps_ob", [128, 512], F32)

        if lora:
            for tc in range(Tl // 512):
                if tc > 0:
                    p.wait("tensor", "se")
                mm_chain([
                    (lambda e, tc=tc, dc=dc: e.matmul(
                        ps_a[0:8, :], AQ[:, dc * 8:(dc + 1) * 8],
                        XF[:, dc * Tl + tc * 512: dc * Tl + tc * 512 + 512],
                        start=(dc == 0), stop=(dc == ND - 1)))
                    for dc in range(ND)])
                p.wait("scalar", "te")
                p.inc_after("scalar", "se", 1,
                            lambda e, tc=tc: e.activation(
                                MIDB[0:8, tc * 512:(tc + 1) * 512], ps_a[0:8, :],
                                AF.Copy, scale=2.0))
        se_mid = p.now("se")
        p.wait("tensor", "se", se_mid)

        def emit_rope(dst_sl, col):
            tgt, sl = dst_sl
            p.wait("vector", "te")
            p.wait("tensor", "ve")
            p.inc_after("tensor", "te", 1,
                        lambda e, tgt=tgt, sl=sl: e.matmul(ps_rb[:, :], PERM[:, :], tgt[:, sl],
                                                           start=True, stop=True))
            p.wait("vector", "te")
            p.wait("vector", "ld")
            p.inc_after("vector", "ve", 1,
                        lambda e, tgt=tgt, sl=sl, col=col: e.tensor_scalar(
                            RSC[:, :], tgt[:, sl], COS[:, col:col + 1], None, ALU.mult))
            p.wait("vector", "ve")
            p.inc_after("vector", "ve", 1,
                        lambda e, tgt=tgt, sl=sl, col=col: e.scalar_tensor_tensor(
                            tgt[:, sl], ps_rb[:, :], SIN[:, col:col + 1], RSC[:, :],
                            ALU.mult, ALU.add))

        for g in range(NG):
            p.wait("gpsimd", "te")
            for si, base in ((0, D + g * GR), (1, 2 * D + g * GR)):
                p.inc_after("gpsimd", "gw", 16,
                            lambda e, si=si, base=base: e.dma_start(
                                out=WQG[:, :].rearrange("q (nd s c) -> q nd s c", s=2, c=GR)[:, :, si, :],
                                in_=wqkv.rearrange("(dc q) f -> q dc f", q=128)[:, :, base:base + GR]))
            if lora:
                for si, base in enumerate((g * GR, D + g * GR, 2 * D + g * GR)):
                    p.inc_after("gpsimd", "gw", 16,
                                lambda e, si=si, base=base: e.dma_start(
                                    out=BLG[0:8, si * GR:(si + 1) * GR],
                                    in_=bl_qkv[0:8, base:base + GR]))
            p.wait("sync", "ve")
            p.inc_after("sync", "ld", 16,
                        lambda e, g=g: e.dma_start(out=BVR[0:1, :],
                                                   in_=bqkv[0:1, 2 * D + g * GR: 2 * D + g * GR + GR]))
            ld_bv = p.now("ld")
            gld_w = p.now("gw")
            p.wait("tensor", "ld", ld_bv)
            p.wait("tensor", "gw", gld_w)
            p.wait("tensor", "ve")
            mm_chain([lambda e: e.matmul(ps_rb[:, 0:GR], ONE1[0:1, :], BVR[0:1, :],
                                         start=True, stop=True)])
            p.wait("vector", "te")
            p.inc_after("vector", "ve", 1,
                        lambda e: e.tensor_copy(BVB[:, :], ps_rb[:, 0:GR]))
            ve_bvb = p.now("ve")

            for cc in range(GC):
                for tc in range(Tl // 512):
                    p.wait("tensor", "ve")
                    extra = None
                    if lora:
                        extra = (lambda e, cc=cc, tc=tc: e.matmul(
                            ps_a[:, :], BLG[0:8, GR + cc * 128: GR + cc * 128 + 128],
                            MIDB[0:8, tc * 512:(tc + 1) * 512], start=False, stop=True))
                    mm_chain([
                        (lambda e, cc=cc, tc=tc, dc=dc: e.matmul(
                            ps_a[:, :], WQG[:, dc * 2 * GR + cc * 128: dc * 2 * GR + cc * 128 + 128],
                            XF[:, dc * Tl + tc * 512: dc * Tl + tc * 512 + 512],
                            start=(dc == 0), stop=(dc == ND - 1 and not lora)))
                        for dc in range(ND)], extra)
                    p.wait("vector", "te")
                    fglob = D + g * GR + cc * 128
                    p.inc_after("vector", "ve", 1,
                                lambda e, cc=cc, tc=tc, fglob=fglob: e.tensor_scalar_add(
                                    KG[:, cc * Tl + tc * 512: cc * Tl + tc * 512 + 512],
                                    ps_a[:, :], BQC[:, fglob // 128: fglob // 128 + 1]))
            se_k = p.now("ve")
            if not lora:
                p.wait("vector", "ve", se_k)
                for cc in range(GC):
                    for tc in range(Tl // 512):
                        emit_rope((KG, slice(cc * Tl + tc * 512, cc * Tl + tc * 512 + 512)),
                                  g * GC + cc)
                p.wait("tensor", "ve")

            p.wait("vector", "ve", ve_bvb)
            for kt in range(NTl):
                p.wait("tensor", "ve")
                extra = None
                if lora:
                    extra = (lambda e, kt=kt: e.matmul(
                        ps_sa[0][:, 0:GR], MIDB[0:8, kt * 128:(kt + 1) * 128],
                        BLG[0:8, 2 * GR: 3 * GR], start=False, stop=True))
                mm_chain([
                    (lambda e, kt=kt, dc=dc: e.matmul(
                        ps_sa[0][:, 0:GR], XF[:, dc * Tl + kt * 128: dc * Tl + kt * 128 + 128],
                        WQG[:, dc * 2 * GR + GR: dc * 2 * GR + 2 * GR],
                        start=(dc == 0), stop=(dc == ND - 1 and not lora)))
                    for dc in range(ND)], extra)
                p.wait("vector", "te")
                vt_view = VT[:, :].rearrange("q (nt h c) -> q nt h c", h=HG, c=65)
                p.inc_after("vector", "ve", 1,
                            lambda e, kt=kt, vt_view=vt_view: e.tensor_tensor(
                                vt_view[:, kt, :, 0:64],
                                ps_sa[0][:, 0:GR].rearrange("q (h c) -> q h c", c=64),
                                BVB[:, :].rearrange("q (h c) -> q h c", c=64), ALU.add))
            ve_v = p.now("ve")
            p.wait("tensor", "ve", ve_v)

            p.wait("gpsimd", "te")
            p.inc_after("gpsimd", "gw", 16,
                        lambda e, g=g: e.dma_start(
                            out=WQG[:, :].rearrange("q (nd s c) -> q nd s c", s=2, c=GR)[:, :, 0, :],
                            in_=wqkv.rearrange("(dc q) f -> q dc f", q=128)[:, :, g * GR: g * GR + GR]))
            gw_q = p.now("gw")
            p.wait("tensor", "gw", gw_q)

            for qs in range(NQS):
                for cc in range(GC):
                    p.wait("tensor", "ve")
                    extra = None
                    if lora:
                        extra = (lambda e, cc=cc, qs=qs: e.matmul(
                            ps_a[:, :], BLG[0:8, cc * 128: cc * 128 + 128],
                            MIDB[0:8, qs * 512:(qs + 1) * 512], start=False, stop=True))
                    mm_chain([
                        (lambda e, cc=cc, dc=dc, qs=qs: e.matmul(
                            ps_a[:, :], WQG[:, dc * 2 * GR + cc * 128: dc * 2 * GR + cc * 128 + 128],
                            XF[:, dc * Tl + qs * 512: dc * Tl + qs * 512 + 512],
                            start=(dc == 0), stop=(dc == ND - 1 and not lora)))
                        for dc in range(ND)], extra)
                    p.wait("vector", "te")
                    fglob = g * GR + cc * 128
                    p.inc_after("vector", "ve", 1,
                                lambda e, cc=cc, fglob=fglob: e.tensor_scalar_add(
                                    QF[:, cc * 512:(cc + 1) * 512], ps_a[:, :],
                                    BQC[:, fglob // 128: fglob // 128 + 1]))
                ve_q = p.now("ve")
                if not lora:
                    p.wait("vector", "ve", ve_q)
                    for cc in range(GC):
                        emit_rope((QF, slice(cc * 512, cc * 512 + 512)), g * GC + cc)
                p.wait("tensor", "ve")

                for cc in range(GC):
                    kts = list(range(NTl)) if lora else list(range(min(NTl, qs * 4 + 4)))
                    se_exp = {}
                    ve_msk = {}

                    def emit_st_exp(i, kt):
                        if i >= 2:
                            p.wait("tensor", "se", se_exp[i - 2])
                            if (not lora) and kts[i - 2] >= qs * 4:
                                p.wait("tensor", "ve", ve_msk[i - 2])
                        p.inc_after("tensor", "te", 1,
                                    lambda e, kt=kt, cc=cc, i=i: e.matmul(
                                        ps_sa[i % 2][:, :],
                                        KG[0:64, cc * Tl + kt * 128: cc * Tl + kt * 128 + 128],
                                        QF[0:64, cc * 512:(cc + 1) * 512],
                                        start=True, stop=True))
                        p.inc_after("tensor", "te", 1,
                                    lambda e, kt=kt, cc=cc, i=i: e.matmul(
                                        ps_sb[i % 2][:, :],
                                        KG[64:128, cc * Tl + kt * 128: cc * Tl + kt * 128 + 128],
                                        QF[64:128, cc * 512:(cc + 1) * 512],
                                        start=True, stop=True))
                        te_s = p.now("te")
                        p.wait("scalar", "te", te_s)
                        if i >= 2:
                            p.wait("scalar", "te")
                        p.s(lambda e, i=i: e.activation(
                            PB[i % 2][:, :], ps_sa[i % 2][:, :], AF.Exp, scale=0.125))
                        p.inc_after("scalar", "se", 1,
                                    lambda e, i=i: e.activation(
                                        PB2[i % 2][:, :], ps_sb[i % 2][:, :], AF.Exp, scale=0.125))
                        se_exp[i] = p.now("se")
                        if (not lora) and kt >= qs * 4:
                            mt = kt - qs * 4
                            p.wait("vector", "se", se_exp[i])
                            for PBx in (PB, PB2):
                                if mt > 0:
                                    p.inc_after("vector", "ve", 1,
                                                lambda e, i=i, mt=mt, PBx=PBx: e.memset(
                                                    PBx[i % 2][:, 0:128 * mt], 0.0))
                                p.inc_after("vector", "ve", 1,
                                            lambda e, i=i, mt=mt, PBx=PBx: e.tensor_tensor(
                                                PBx[i % 2][:, 128 * mt:128 * mt + 128],
                                                PBx[i % 2][:, 128 * mt:128 * mt + 128],
                                                MASK[:, :], ALU.mult))
                            ve_msk[i] = p.now("ve")

                    def emit_pv(i, kt, last):
                        if (not lora) and kt >= qs * 4:
                            p.wait("tensor", "ve", ve_msk[i])
                        else:
                            p.wait("tensor", "se", se_exp[i])
                        ha = (cc * 2) % HG
                        fn_a = (lambda e, kt=kt, i=i, ha=ha, last=last: e.matmul(
                            ps_oa[0:65, :],
                            VT[:, kt * HG * 65 + ha * 65: kt * HG * 65 + ha * 65 + 65],
                            PB[i % 2][:, :], start=(i == 0), stop=last))
                        fn_b = (lambda e, kt=kt, i=i, ha=ha, last=last: e.matmul(
                            ps_ob[0:65, :],
                            VT[:, kt * HG * 65 + (ha + 1) * 65: kt * HG * 65 + (ha + 1) * 65 + 65],
                            PB2[i % 2][:, :], start=(i == 0), stop=last))
                        p.t(fn_a)
                        if last:
                            p.inc_after("tensor", "te", 1, fn_b)
                        else:
                            p.t(fn_b)

                    for i, kt in enumerate(kts):
                        emit_st_exp(i, kt)
                        if i > 0:
                            emit_pv(i - 1, kts[i - 1], False)
                    emit_pv(len(kts) - 1, kts[-1], True)
                    te_pv = p.now("te")

                    for hb, ps_o in ((0, ps_oa), (1, ps_ob)):
                        p.wait("vector", "te", te_pv)
                        p.inc_after("vector", "ve", 1,
                                    lambda e, ps_o=ps_o: e.tensor_copy(DEN, ps_o[64:65, :]))
                        p.wait("vector", "ve")
                        p.inc_after("vector", "ve", 1,
                                    lambda e: e.reciprocal(RECIP, DEN))
                        ve_r = p.now("ve")
                        p.wait("tensor", "ve", ve_r)
                        p.inc_after("tensor", "te", 1,
                                    lambda e: e.matmul(ps_rb[0:64, :], ONE1[0:1, 0:64], RECIP,
                                                       start=True, stop=True))
                        te_rb = p.now("te")
                        p.wait("scalar", "te", te_rb)
                        p.inc_after("scalar", "se", 1,
                                    lambda e, ps_o=ps_o: e.activation(OS[0:64, :], ps_o[0:64, :],
                                                                      AF.Copy))
                        p.wait("vector", "se")
                        p.wait("vector", "te", te_rb)
                        grow = g * GR + (cc * 2 + hb) * 64
                        ocn, orb = grow // 128, grow % 128
                        p.inc_after("vector", "ve", 1,
                                    lambda e, ocn=ocn, orb=orb, qs=qs: e.tensor_tensor(
                                        OF[orb:orb + 64, ocn * Tl + qs * 512: ocn * Tl + qs * 512 + 512],
                                        OS[0:64, :], ps_rb[0:64, :], ALU.mult))
                        p.wait("tensor", "se")
                        p.wait("tensor", "ve")
        ve_of = p.now("ve")
        p.wait("tensor", "ve", ve_of)

        if lora:
            for tc in range(Tl // 512):
                p.wait("tensor", "se")
                mm_chain([
                    (lambda e, tc=tc, dc=dc: e.matmul(
                        ps_a[0:8, :], APJ[:, dc * 8:(dc + 1) * 8],
                        OF[:, dc * Tl + tc * 512: dc * Tl + tc * 512 + 512],
                        start=(dc == 0), stop=(dc == ND - 1)))
                    for dc in range(ND)])
                p.wait("scalar", "te")
                p.inc_after("scalar", "se", 1,
                            lambda e, tc=tc: e.activation(
                                MIDB[0:8, tc * 512:(tc + 1) * 512], ps_a[0:8, :],
                                AF.Copy, scale=2.0))
        se_mid2 = p.now("se")

        p.wait("sync", "ve")
        p.inc_after("sync", "ld", 16, lambda e: e.dma_start(out=BPR[0:1, :], in_=bproj))
        p.wait("gpsimd", "te")
        p.inc_after("gpsimd", "gwp", 16,
                    lambda e: e.dma_start(out=WPROJ[:, 0:ND * D].rearrange("q (dc f) -> q dc f", f=D),
                                          in_=wproj.rearrange("(dc q) f -> q dc f", q=128)))
        gld_wp = p.now("gwp")
        p.wait("tensor", "gwp", gld_wp)
        p.wait("tensor", "ld")
        for half in range(2):
            p.wait("tensor", "ve")
            mm_chain([lambda e, half=half: e.matmul(
                ps_rb[:, :], ONE1[0:1, :], BPR[0:1, half * 512:(half + 1) * 512],
                start=True, stop=True)])
            p.wait("vector", "te")
            p.inc_after("vector", "ve", 1,
                        lambda e, half=half: e.tensor_copy(BPB[:, half * 512:(half + 1) * 512],
                                                           ps_rb[:, :]))
        ve_bpb = p.now("ve")

        ps_at_ctx.close()
        ps_po = ctx.enter_context(nc.psum_tensor(p.pfx + "ps_po", [128, D], F32))
        p.wait("tensor", "se", se_mid2)
        p.wait("tensor", "ve", ve_bpb)
        for nt in range(NTl):
            buf = (nt % 2) * D
            p.wait("sync", "ve")
            p.inc_after("sync", "ld", 16,
                        lambda e, nt=nt, buf=buf: e.dma_start(out=S32[:, buf:buf + D], in_=x_t[nt]))
            ld_nt = p.now("ld")
            if nt >= 1:
                p.wait("tensor", "ve")
            for half in range(2):
                extra = None
                if lora:
                    extra = (lambda e, nt=nt, half=half: e.matmul(
                        ps_po[:, half * 512:(half + 1) * 512],
                        MIDB[0:8, nt * 128:(nt + 1) * 128],
                        BLP[0:8, half * 512: half * 512 + 512], start=False, stop=True))
                mm_chain([
                    (lambda e, nt=nt, half=half, oc=oc: e.matmul(
                        ps_po[:, half * 512:(half + 1) * 512],
                        OF[:, oc * Tl + nt * 128: oc * Tl + nt * 128 + 128],
                        WPROJ[:, oc * D + half * 512: oc * D + half * 512 + 512],
                        start=(oc == 0), stop=(oc == ND - 1 and not lora)))
                    for oc in range(ND)], extra)
            te_po = p.now("te")
            p.wait("vector", "te", te_po)
            p.wait("vector", "ld", ld_nt)
            p.inc_after("vector", "ve", 1,
                        lambda e, buf=buf: e.tensor_tensor(S32[:, buf:buf + D], S32[:, buf:buf + D],
                                                           ps_po[:, :], ALU.add))
            p.wait("vector", "ve")
            p.inc_after("vector", "ve", 1,
                        lambda e, buf=buf: e.tensor_tensor(S32[:, buf:buf + D], S32[:, buf:buf + D],
                                                           BPB[:, :], ALU.add))
            p.wait("scalar", "st")
            p.wait("vector", "st")
            emit_ln_tile(p, S32[:, buf:buf + D], OUT[:, 0:D], stats,
                         sq_scr=OUT[:, 0:D])
            ve_out = p.now("ve")
            p.wait("sync", "ve", ve_out)
            p.inc_after("sync", "st", 16,
                        lambda e, nt=nt: e.dma_start(out=yo_t[nt], in_=OUT[:, 0:D]))

        p.wait("sync", "st")
        p.barrier()
    if standalone:
        p.run()
        _octx.__exit__(None, None, None)
    return nc



def build_mamba(T_=None, p=None, dram=None):
    Tl = T_ or T
    DI = 2048
    NDI = DI // 128
    NST = 16
    TS = 256
    NS = Tl // TS
    TPS = TS // 128

    standalone = p is None
    if standalone:
        _nc = bass.Bass("TRN2", target_bir_lowering=False)
        _octx = ExitStack()
        _octx.__enter__()
        p = Prog(_nc, _octx)
        p.pfx = ""
        dram = lambda name, shape, dt, kind: _nc.dram_tensor(name, shape, dt, kind=kind).ap()
    nc = p.nc
    x = dram("x", [Tl, D], F32, "ExternalInput")
    lnw = dram("lnw", [1, D], F32, "ExternalInput")
    lnb = dram("lnb", [1, D], F32, "ExternalInput")
    wipT = dram("wipT", [D, 2 * DI], F32, "ExternalInput")
    convw = dram("convw", [128, NDI * 4], F32, "ExternalInput")
    convb = dram("convb", [128, NDI], F32, "ExternalInput")
    wxpT = dram("wxpT", [DI, 96], F32, "ExternalInput")
    wdtT = dram("wdtT", [64, DI], F32, "ExternalInput")
    dtb = dram("dtb", [128, NDI], F32, "ExternalInput")
    alog = dram("alog", [128, NDI * NST], F32, "ExternalInput")
    dmat = dram("dmat", [128, NDI], F32, "ExternalInput")
    wopT = dram("wopT", [DI, D], F32, "ExternalInput")
    ehb = dram("c_ehb", [32, NST * 128], F32, "ExternalInput")
    ehc = dram("c_ehc", [32, NST * 128], F32, "ExternalInput")
    ident = dram("c_ident", [128, 128], F32, "ExternalInput")
    yo = dram("y", [Tl, D], F32, "ExternalOutput")

    x_t = x.rearrange("(nt p) d -> nt p d", p=128)
    yo_t = yo.rearrange("(nt p) d -> nt p d", p=128)

    with ExitStack() as ctx:
        sbt = lambda n, *a: ctx.enter_context(nc.sbuf_tensor(p.pfx + n, *a))
        pst = lambda n, *a: ctx.enter_context(nc.psum_tensor(p.pfx + n, *a))

        IDN = sbt("IDN", [128, 128], BF16)
        WIP = sbt("WIP", [128, ND * 2 * DI], BF16)
        WOP = sbt("WOP", [128, NDI * D], BF16)
        WXP = sbt("WXP", [128, NDI * 96], BF16)
        WDT = sbt("WDT", [64, NDI * 128], BF16)
        CW = sbt("CW", [128, NDI * 4], F32)
        CB = sbt("CB", [128, NDI], F32)
        DTB = sbt("DTB", [128, NDI], F32)
        AC = sbt("AC", [128, NDI * NST], F32)
        DM = sbt("DM", [128, NDI], F32)
        EHB = sbt("EHB", [32, NST * 128], BF16)
        EHC = sbt("EHC", [32, NST * 128], BF16)
        XS32 = sbt("XS32m", [128, TPS * D], F32)
        XLNF = sbt("XLNF", [128, ND * TS], BF16)
        XCB = sbt("XCB", [128, NDI * (TS + 3)], BF16)
        ZS = sbt("ZSm", [128, NDI * TS], BF16)
        XCONV = sbt("XCONV", [128, NDI * TS], BF16)
        DELTA = sbt("DELTA", [128, NDI * TS], BF16)
        DU = sbt("DU", [128, NDI * TS], BF16)
        XLNT = DU[:, 0:TPS * D]
        BB = sbt("BBm", [128, NST * TS], BF16)
        CCB = sbt("CCm", [128, NST * TS], BF16)
        DT_ = sbt("DTm", [64, TS], BF16)
        BCB = sbt("BCB", [32, TS], BF16)
        DA = [sbt(f"DAm{i}", [128, TS], F32) for i in range(2)]
        DBu = [sbt(f"DBm{i}", [128, TS], BF16) for i in range(2)]
        HS = sbt("HSm", [128, TS], F32)
        YA = sbt("YAm", [128, TS], F32)
        SCR1 = sbt("SCR1m", [128, TS], F32)
        TS3 = SCR1
        CV = SCR1
        AB = SCR1
        YF = sbt("YFm", [128, NDI * TS], BF16)
        HC = sbt("HCm", [128, NDI * NST], F32)
        EL = sbt("ELm", [128, TS], F32)
        D0 = sbt("D0m", [128, TS], F32)
        WLNB = sbt("WLNB", [128, D], BF16)
        BLNB = sbt("BLNB", [128, D], BF16)
        ONE1 = sbt("ONE1m", [1, 128], F32)
        OUT = sbt("OUTm", [128, D], F32)
        LNROW = OUT[:, 0:512].bitcast(F32)
        stats = alloc_stats(sbt)

        ps_tr = pst("ps_tr", [128, ND * 128], BF16)
        ps_ipb = [pst(f"ps_ip{i}", [128, TS], F32) for i in range(2)]
        ps_xp = pst("ps_xp", [128, TS], F32)
        ps_dt = pst("ps_dt", [128, TS], F32)
        ps_bc = pst("ps_bc", [128, TS], F32)
        ps_o = pst("ps_o", [128, D], F32)

        p.inc_after("vector", "ve", 1, lambda e: e.memset(stats["eps"][:, 0:1], LN_EPS))
        p.inc_after("vector", "ve", 1, lambda e: e.memset(ONE1[:, :], 1.0))
        p.inc_after("vector", "ve", 1, lambda e: e.memset(HC[:, :], 0.0))
        p.inc_after("vector", "ve", 1, lambda e: e.memset(
            XCB[:, :].rearrange("q (dc c) -> q dc c", c=TS + 3)[:, :, 0:3], 0.0))

        p.inc_after("gpsimd", "gld", 16, lambda e: e.dma_start(out=IDN[:, :], in_=ident))
        p.inc_after("gpsimd", "gld", 16, lambda e: e.dma_start(out=EHB[:, :], in_=ehb))
        p.inc_after("gpsimd", "gld", 16, lambda e: e.dma_start(out=EHC[:, :], in_=ehc))
        for piece in range(4):
            p.inc_after("gpsimd", "gld", 16,
                        lambda e, piece=piece: e.dma_start(
                            out=WIP[:, :].rearrange("q (dc f) -> q dc f", f=2 * DI)[:, 2 * piece:2 * piece + 2, :],
                            in_=wipT.rearrange("(dc q) f -> q dc f", q=128)[:, 2 * piece:2 * piece + 2, :]))
        for piece in range(4):
            p.inc_after("gpsimd", "gld", 16,
                        lambda e, piece=piece: e.dma_start(
                            out=WOP[:, :].rearrange("q (fc f) -> q fc f", f=D)[:, 4 * piece:4 * piece + 4, :],
                            in_=wopT.rearrange("(fc q) f -> q fc f", q=128)[:, 4 * piece:4 * piece + 4, :]))
        p.inc_after("gpsimd", "gld", 16,
                    lambda e: e.dma_start(out=WXP[:, :].rearrange("q (dc f) -> q dc f", f=96),
                                          in_=wxpT.rearrange("(dc q) f -> q dc f", q=128)))
        p.inc_after("gpsimd", "gld", 16, lambda e: e.dma_start(out=WDT[0:64, :], in_=wdtT))
        p.inc_after("sync", "cld", 16, lambda e: e.dma_start(out=CW[:, :], in_=convw))
        p.inc_after("sync", "cld", 16, lambda e: e.dma_start(out=CB[:, :], in_=convb))
        p.inc_after("sync", "cld", 16, lambda e: e.dma_start(out=DTB[:, :], in_=dtb))
        p.inc_after("sync", "cld", 16, lambda e: e.dma_start(out=AC[:, :], in_=alog))
        p.inc_after("sync", "cld", 16, lambda e: e.dma_start(out=DM[:, :], in_=dmat))
        ld_c = p.now("cld")
        p.wait("scalar", "cld", ld_c)
        p.inc_after("scalar", "se", 1, lambda e: e.activation(AC[:, :], AC[:, :], AF.Exp))
        p.wait("vector", "se")
        p.inc_after("vector", "ve", 1, lambda e: e.tensor_scalar_mul(AC[:, :], AC[:, :], -1.0))
        ve_a = p.now("ve")
        LNW = XS32[:, :].bitcast(F32)[0:1, 0:D]
        LNB2 = XS32[:, :].bitcast(F32)[0:1, D:2 * D]
        p.inc_after("sync", "lnld", 16, lambda e: e.dma_start(out=LNW, in_=lnw))
        p.inc_after("sync", "lnld", 16, lambda e: e.dma_start(out=LNB2, in_=lnb))
        ld_ln = p.now("lnld")
        p.wait("tensor", "lnld", ld_ln)
        p.wait("tensor", "ve", ve_a)
        for half in range(2):
            p.inc_after("tensor", "te", 1,
                        lambda e, half=half: e.matmul(ps_o[:, 0:512], ONE1[0:1, :],
                                                      LNW[0:1, half * 512:(half + 1) * 512],
                                                      start=True, stop=True))
            p.wait("vector", "te")
            p.inc_after("vector", "ve", 1,
                        lambda e, half=half: e.tensor_copy(WLNB[:, half * 512:(half + 1) * 512],
                                                           ps_o[:, 0:512]))
            p.wait("tensor", "ve")
            p.inc_after("tensor", "te", 1,
                        lambda e, half=half: e.matmul(ps_o[:, 0:512], ONE1[0:1, :],
                                                      LNB2[0:1, half * 512:(half + 1) * 512],
                                                      start=True, stop=True))
            p.wait("vector", "te")
            p.inc_after("vector", "ve", 1,
                        lambda e, half=half: e.tensor_copy(BLNB[:, half * 512:(half + 1) * 512],
                                                           ps_o[:, 0:512]))
            p.wait("tensor", "ve")
        ve_pre = p.now("ve")
        p.wait("sync", "ve", ve_pre)

        for s in range(NS):
            p.wait("sync", "ve")
            for tt in range(TPS):
                p.inc_after("sync", "ld", 16,
                            lambda e, s=s, tt=tt: e.dma_start(out=XS32[:, tt * D:(tt + 1) * D],
                                                              in_=x_t[s * TPS + tt]))
            ld_s = p.now("ld")
            for tt in range(TPS):
                p.wait("vector", "ld", ld_s)
                p.wait("scalar", "ld", ld_s)
                p.wait("scalar", "st")
                p.wait("vector", "st")
                emit_ln_tile(p, XS32[:, tt * D:(tt + 1) * D], OUT[:, 0:D], stats,
                             sq_scr=OUT[:, 0:D])
                p.wait("vector", "ve")
                p.inc_after("vector", "ve", 1,
                            lambda e, tt=tt: e.tensor_tensor(OUT[:, 0:D], OUT[:, 0:D],
                                                             WLNB[:, :], ALU.mult))
                p.wait("vector", "ve")
                p.inc_after("vector", "ve", 1,
                            lambda e, tt=tt: e.tensor_tensor(XLNT[:, tt * D:(tt + 1) * D],
                                                             OUT[:, 0:D], BLNB[:, :], ALU.add))
            ve_ln = p.now("ve")
            p.wait("tensor", "gld")
            p.wait("tensor", "ve", ve_ln)
            xlnf_view = XLNF[:, :].rearrange("q (nd t) -> q nd t", t=TS)
            for tt in range(TPS):
                p.wait("tensor", "ve")
                fns = [
                    (lambda e, tt=tt, dc=dc: e.transpose(
                        ps_tr[:, dc * 128:(dc + 1) * 128],
                        XLNT[:, tt * D + dc * 128: tt * D + (dc + 1) * 128], IDN[:, :]))
                    for dc in range(ND)]
                for i, fn in enumerate(fns):
                    if i == ND - 1:
                        p.inc_after("tensor", "te", 1, fn)
                    else:
                        p.t(fn)
                p.wait("vector", "te")
                p.inc_after("vector", "ve", 1,
                            lambda e, tt=tt, xlnf_view=xlnf_view: e.tensor_copy(
                                xlnf_view[:, :, tt * 128:(tt + 1) * 128],
                                ps_tr[:, :].rearrange("q (nd f) -> q nd f", f=128)))
            ve_xf = p.now("ve")
            p.wait("tensor", "ve", ve_xf)
            p.wait("tensor", "gld")
            xcb_view = XCB[:, :].rearrange("q (dc c) -> q dc c", c=TS + 3)
            se_ip_marks = {}
            for fc in range(32):
                ps_ip = ps_ipb[fc % 2]
                if fc >= 2:
                    p.wait("tensor", "se", se_ip_marks[fc - 2])
                elif s > 0:
                    p.wait("tensor", "se")
                for dc in range(ND):
                    fn = lambda e, fc=fc, dc=dc, ps_ip=ps_ip: e.matmul(
                        ps_ip[:, :], WIP[:, (dc * 32 + fc) * 128:(dc * 32 + fc + 1) * 128],
                        XLNF[:, dc * TS:(dc + 1) * TS],
                        start=(dc == 0), stop=(dc == ND - 1))
                    if dc == ND - 1:
                        p.inc_after("tensor", "te", 1, fn)
                    else:
                        p.t(fn)
                p.wait("scalar", "te")
                if fc < NDI:
                    p.wait("scalar", "ve")
                    p.inc_after("scalar", "se", 1,
                                lambda e, fc=fc, xcb_view=xcb_view, ps_ip=ps_ip: e.activation(
                                    xcb_view[:, fc, 3:TS + 3], ps_ip[:, :], AF.Copy))
                else:
                    p.inc_after("scalar", "se", 1,
                                lambda e, fc=fc, ps_ip=ps_ip: e.activation(
                                    ZS[:, (fc - NDI) * TS:(fc - NDI + 1) * TS], ps_ip[:, :],
                                    AF.Silu))
                se_ip_marks[fc] = p.now("se")
            se_ip = p.now("se")
            for dc in range(NDI):
                p.wait("vector", "se")
                p.inc_after("vector", "ve", 1,
                            lambda e, dc=dc: e.tensor_scalar(
                                CV[:, :], XCB[:, dc * (TS + 3): dc * (TS + 3) + TS],
                                CW[:, dc * 4: dc * 4 + 1], None, ALU.mult))
                for k in range(1, 4):
                    p.wait("vector", "ve")
                    p.inc_after("vector", "ve", 1,
                                lambda e, dc=dc, k=k: e.scalar_tensor_tensor(
                                    CV[:, :], XCB[:, dc * (TS + 3) + k: dc * (TS + 3) + k + TS],
                                    CW[:, dc * 4 + k: dc * 4 + k + 1], CV[:, :],
                                    ALU.mult, ALU.add))
                p.wait("scalar", "ve")
                p.wait("scalar", "te")
                p.inc_after("scalar", "se", 1,
                            lambda e, dc=dc: e.activation(XCONV[:, dc * TS:(dc + 1) * TS],
                                                          CV[:, :], AF.Silu,
                                                          bias=CB[:, dc:dc + 1]))
            se_cv = p.now("se")
            p.wait("vector", "se", se_cv)
            p.inc_after("vector", "ve", 1,
                        lambda e: e.tensor_copy(xcb_view[:, :, 0:3], xcb_view[:, :, TS:TS + 3]))
            p.wait("tensor", "se", se_cv)
            for dc in range(NDI):
                fn = lambda e, dc=dc: e.matmul(
                    ps_xp[0:96, :], WXP[:, dc * 96:(dc + 1) * 96],
                    XCONV[:, dc * TS:(dc + 1) * TS],
                    start=(dc == 0), stop=(dc == NDI - 1))
                if dc == NDI - 1:
                    p.inc_after("tensor", "te", 1, fn)
                else:
                    p.t(fn)
            te_xp = p.now("te")
            p.wait("scalar", "te", te_xp)
            p.inc_after("scalar", "se", 1,
                        lambda e: e.activation(DT_[0:64, :], ps_xp[0:64, :], AF.Copy))
            p.inc_after("scalar", "se", 1,
                        lambda e: e.activation(BCB[0:32, :], ps_xp[64:96, :], AF.Copy))
            se_bc = p.now("se")
            p.wait("tensor", "se", se_bc)
            p.wait("vector", "se")
            for fc in range(NDI):
                p.inc_after("tensor", "te", 1,
                            lambda e, fc=fc: e.matmul(ps_dt[:, :],
                                                      WDT[0:64, fc * 128:(fc + 1) * 128],
                                                      DT_[0:64, :], start=True, stop=True))
                te_d = p.now("te")
                p.wait("vector", "te", te_d)
                p.inc_after("vector", "ve", 1,
                            lambda e, fc=fc: e.tensor_scalar_add(D0[:, :], ps_dt[:, :],
                                                                 DTB[:, fc:fc + 1]))
                p.wait("vector", "ve")
                p.inc_after("vector", "ve", 1,
                            lambda e: e.scalar_tensor_tensor(AB[:, :], D0[:, :], -1.0, D0[:, :],
                                                             ALU.mult, ALU.max))
                p.wait("scalar", "ve")
                p.inc_after("scalar", "se", 1,
                            lambda e: e.activation(EL[:, :], AB[:, :], AF.Exp, scale=-1.0))
                p.wait("scalar", "se")
                p.inc_after("scalar", "se", 1,
                            lambda e: e.activation(EL[:, :], EL[:, :], AF.Ln, bias=1.0))
                p.wait("vector", "ve")
                p.inc_after("vector", "ve", 1,
                            lambda e: e.tensor_scalar_max(D0[:, :], D0[:, :], 0.0))
                p.wait("vector", "se")
                p.wait("vector", "ve")
                p.inc_after("vector", "ve", 1,
                            lambda e, fc=fc: e.tensor_tensor(DELTA[:, fc * TS:(fc + 1) * TS],
                                                             EL[:, :], D0[:, :], ALU.add))
                p.wait("tensor", "ve")
            p.wait("vector", "ve")
            for dc in range(NDI):
                p.inc_after("vector", "ve", 1,
                            lambda e, dc=dc: e.tensor_tensor(DU[:, dc * TS:(dc + 1) * TS],
                                                             DELTA[:, dc * TS:(dc + 1) * TS],
                                                             XCONV[:, dc * TS:(dc + 1) * TS],
                                                             ALU.mult))
            ve_du = p.now("ve")
            p.wait("tensor", "se", se_bc)
            p.wait("scalar", "ve")
            for n in range(NST):
                p.wait("tensor", "se")
                p.inc_after("tensor", "te", 1,
                            lambda e, n=n: e.matmul(ps_bc[:, :], EHB[:, n * 128:(n + 1) * 128],
                                                    BCB[0:32, :], start=True, stop=True))
                p.wait("scalar", "te")
                p.inc_after("scalar", "se", 1,
                            lambda e, n=n: e.activation(BB[:, n * TS:(n + 1) * TS],
                                                        ps_bc[:, :], AF.Copy))
                p.wait("tensor", "se")
                p.inc_after("tensor", "te", 1,
                            lambda e, n=n: e.matmul(ps_bc[:, :], EHC[:, n * 128:(n + 1) * 128],
                                                    BCB[0:32, :], start=True, stop=True))
                p.wait("scalar", "te")
                p.inc_after("scalar", "se", 1,
                            lambda e, n=n: e.activation(CCB[:, n * TS:(n + 1) * TS],
                                                        ps_bc[:, :], AF.Copy))
            se_bb = p.now("se")
            p.wait("vector", "se", se_bb)
            p.wait("scalar", "ve", ve_du)
            scan_marks = {}
            for dc in range(NDI):
                for n in range(NST):
                    gi = dc * NST + n
                    par = gi % 2
                    if gi >= 2:
                        p.wait("scalar", "ve", scan_marks[gi - 2])
                    p.inc_after("scalar", "se", 1,
                                lambda e, dc=dc, n=n, par=par: e.activation(
                                    DA[par][:, :], DELTA[:, dc * TS:(dc + 1) * TS], AF.Exp,
                                    scale=AC[:, dc * NST + n: dc * NST + n + 1]))
                    p.inc_after("vector", "ve", 1,
                                lambda e, dc=dc, n=n, par=par: e.tensor_tensor(
                                    DBu[par][:, :], DU[:, dc * TS:(dc + 1) * TS],
                                    BB[:, n * TS:(n + 1) * TS], ALU.mult))
                    p.wait("vector", "se")
                    p.wait("vector", "ve")
                    p.inc_after("vector", "ve", 1,
                                lambda e, dc=dc, n=n, par=par: e.tensor_tensor_scan(
                                    HS[:, :], DA[par][:, :], DBu[par][:, :],
                                    HC[:, dc * NST + n: dc * NST + n + 1],
                                    ALU.mult, ALU.add))
                    scan_marks[gi] = p.now("ve")
                    p.wait("vector", "ve")
                    p.inc_after("vector", "ve", 1,
                                lambda e, dc=dc, n=n: e.tensor_copy(
                                    HC[:, dc * NST + n: dc * NST + n + 1], HS[:, TS - 1:TS]))
                    if n == 0:
                        p.inc_after("vector", "ve", 1,
                                    lambda e, n=n: e.tensor_tensor(YA[:, :], HS[:, :],
                                                                   CCB[:, n * TS:(n + 1) * TS],
                                                                   ALU.mult))
                    else:
                        p.inc_after("vector", "ve", 1,
                                    lambda e, n=n: e.tensor_tensor(TS3[:, :], HS[:, :],
                                                                   CCB[:, n * TS:(n + 1) * TS],
                                                                   ALU.mult))
                        p.wait("vector", "ve")
                        p.inc_after("vector", "ve", 1,
                                    lambda e: e.tensor_tensor(YA[:, :], YA[:, :], TS3[:, :],
                                                              ALU.add))
                p.wait("vector", "ve")
                p.inc_after("vector", "ve", 1,
                            lambda e, dc=dc: e.scalar_tensor_tensor(
                                YA[:, :], XCONV[:, dc * TS:(dc + 1) * TS], DM[:, dc:dc + 1],
                                YA[:, :], ALU.mult, ALU.add))
                p.wait("vector", "ve")
                p.inc_after("vector", "ve", 1,
                            lambda e, dc=dc: e.tensor_tensor(YF[:, dc * TS:(dc + 1) * TS],
                                                             YA[:, :],
                                                             ZS[:, dc * TS:(dc + 1) * TS],
                                                             ALU.mult))
            ve_y = p.now("ve")
            p.wait("tensor", "ve", ve_y)
            for tt in range(TPS):
                if s > 0 or tt > 0:
                    p.wait("tensor", "ve")
                for half in range(2):
                    for fc in range(NDI):
                        fn = lambda e, tt=tt, half=half, fc=fc: e.matmul(
                            ps_o[:, half * 512:(half + 1) * 512],
                            YF[:, fc * TS + tt * 128: fc * TS + tt * 128 + 128],
                            WOP[:, fc * D + half * 512: fc * D + half * 512 + 512],
                            start=(fc == 0), stop=(fc == NDI - 1))
                        if fc == NDI - 1:
                            p.inc_after("tensor", "te", 1, fn)
                        else:
                            p.t(fn)
                te_o = p.now("te")
                p.wait("vector", "te", te_o)
                p.inc_after("vector", "ve", 1,
                            lambda e, tt=tt: e.scalar_tensor_tensor(
                                XS32[:, tt * D:(tt + 1) * D], XS32[:, tt * D:(tt + 1) * D],
                                2.0, ps_o[:, :], ALU.mult, ALU.add))
                p.wait("scalar", "st")
                p.wait("vector", "st")
                emit_ln_tile(p, XS32[:, tt * D:(tt + 1) * D], OUT[:, 0:D], stats,
                             sq_scr=OUT[:, 0:D])
                ve_out = p.now("ve")
                p.wait("sync", "ve", ve_out)
                p.inc_after("sync", "st", 16,
                            lambda e, s=s, tt=tt: e.dma_start(out=yo_t[s * TPS + tt],
                                                              in_=OUT[:, 0:D]))

        p.wait("sync", "st")
        p.barrier()
    if standalone:
        p.run()
        _octx.__exit__(None, None, None)
    return nc



def build_l2_attn(T_=None, p=None, dram=None):
    Tl = T_ or T
    NTl = Tl // 128
    NH = 8
    NQS = Tl // 512
    GEPS = 1e-5

    standalone = p is None
    if standalone:
        _nc = bass.Bass("TRN2", target_bir_lowering=False)
        _octx = ExitStack()
        _octx.__enter__()
        p = Prog(_nc, _octx)
        p.pfx = ""
        dram = lambda name, shape, dt, kind: _nc.dram_tensor(name, shape, dt, kind=kind).ap()
    nc = p.nc
    x = dram("x", [Tl, D], F32, "ExternalInput")
    wq = dram("wq", [128, NH * 128], F32, "ExternalInput")
    wk = dram("wk", [128, NH * 128], F32, "ExternalInput")
    wv = dram("wv", [128, NH * 128], F32, "ExternalInput")
    w1 = dram("w1", [D, D], F32, "ExternalInput")
    w2 = dram("w2", [D, D], F32, "ExternalInput")
    gnw = dram("gnw", [128, NH], F32, "ExternalInput")
    gnb = dram("gnb", [128, NH], F32, "ExternalInput")
    gp = dram("c_gp", [128, NH * 36], F32, "ExternalInput")
    ramp = dram("c_ramp", [128, NH * 512], F32, "ExternalInput")
    cmask = dram("c_mask", [128, 128], F32, "ExternalInput")
    ident = dram("c_ident", [128, 128], F32, "ExternalInput")
    yo = dram("y", [Tl, D], F32, "ExternalOutput")

    gammas = 1.0 - np.exp(np.linspace(np.log(1 / 512), np.log(1 / 32), NH))
    mm_max = [min(NTl, int(np.ceil(37.0 / (128 * -np.log(g)))) + 1) for g in gammas]

    x_t = x.rearrange("(nt p) d -> nt p d", p=128)
    yo_t = yo.rearrange("(nt p) d -> nt p d", p=128)

    with ExitStack() as ctx:
        sbt = lambda n, *a: ctx.enter_context(nc.sbuf_tensor(p.pfx + n, *a))
        pst = lambda n, *a: ctx.enter_context(nc.psum_tensor(p.pfx + n, *a))

        IDN = sbt("IDN", [128, 128], BF16)
        XF = sbt("XF2", [128, ND * Tl], BF16)
        RETF = sbt("RETF", [128, ND * Tl], BF16)
        WQ = sbt("WQ2", [128, NH * 128], BF16)
        WK = sbt("WK2", [128, NH * 128], BF16)
        WV = sbt("WV2", [128, NH * 128], BF16)
        KVW = sbt("KVW", [128, max(2 * Tl, ND * D)], BF16)
        W2S = sbt("W2S", [128, ND * D], BF16)
        W1S = KVW[:, 0:ND * D]
        GNW = sbt("GNW", [128, NH], F32)
        GNB = sbt("GNB", [128, NH], F32)
        GP = sbt("GP2", [128, NH * 36], F32)
        RAMP = sbt("RAMP", [128, NH * 512], BF16)
        MASK = sbt("MASK2", [128, 128], BF16)
        KF = KVW[:, 0:Tl]
        VT = KVW[:, Tl:2 * Tl]
        QS_ = sbt("QS2", [128, 512], BF16)
        PBJ = sbt("PBJ2", [128, 512], BF16)
        RET_s = sbt("RET_s", [128, 512], BF16)
        SQ = sbt("SQ2", [128, 512], BF16)
        RSC = sbt("RSC2f", [128, 512], F32)
        SG = sbt("SG2", [128, 512], BF16)
        S2F = sbt("S2F", [128, ND * 512], BF16)
        MU = S2F[:, :].bitcast(F32)[0:1, 0:512]
        RSTD = S2F[:, :].bitcast(F32)[0:1, 512:1024]
        ONE1 = sbt("ONE1f", [1, 128], F32)
        ONEC = sbt("ONECb", [128, 1], BF16)
        SHARE = sbt("SHARE2", [128, 4 * D], BF16)
        XSB = SHARE[:, 0:2 * D]
        S32 = SHARE[:, :].bitcast(F32)
        OUT = sbt("OUT2", [128, D], F32)
        stats = alloc_stats(sbt)

        ps_tr = pst("ps_tr", [128, ND * 128], BF16)
        ps_a = pst("ps_a", [128, 512], F32)
        ps_sc = pst("ps_sc", [128, 512], F32)
        ps_ret = pst("ps_ret", [128, 512], F32)
        ps_rb = pst("ps_rb", [128, 512], F32)
        ps_po = pst("ps_po", [128, D], F32)

        p.inc_after("vector", "ve", 1, lambda e: e.memset(stats["eps"][:, 0:1], LN_EPS))
        p.inc_after("vector", "ve", 1, lambda e: e.memset(ONE1[:, :], 1.0))
        p.inc_after("vector", "ve", 1, lambda e: e.memset(ONEC[:, :], 1.0))

        p.inc_after("gpsimd", "gld", 16, lambda e: e.dma_start(out=IDN[:, :], in_=ident))
        p.inc_after("gpsimd", "gld", 16, lambda e: e.dma_start(out=WQ[:, :], in_=wq))
        p.inc_after("gpsimd", "gld", 16, lambda e: e.dma_start(out=WK[:, :], in_=wk))
        p.inc_after("gpsimd", "gld", 16, lambda e: e.dma_start(out=WV[:, :], in_=wv))
        p.inc_after("gpsimd", "gld", 16,
                    lambda e: e.dma_start(out=W2S[:, :].rearrange("q (dc f) -> q dc f", f=D),
                                          in_=w2.rearrange("(dc q) f -> q dc f", q=128)))
        p.inc_after("gpsimd", "gld", 16, lambda e: e.dma_start(out=RAMP[:, :], in_=ramp))
        p.inc_after("gpsimd", "gld", 16, lambda e: e.dma_start(out=MASK[:, :], in_=cmask))
        p.inc_after("sync", "cld", 16, lambda e: e.dma_start(out=GNW[:, :], in_=gnw))
        p.inc_after("sync", "cld", 16, lambda e: e.dma_start(out=GNB[:, :], in_=gnb))
        p.inc_after("sync", "cld", 16, lambda e: e.dma_start(out=GP[:, :], in_=gp))
        cld_c = p.now("cld")

        te_nt = []
        for nt in range(NTl):
            buf = (nt % 2) * D
            par = f"gx{nt % 2}"
            if nt >= 2:
                p.wait("gpsimd", "te", te_nt[nt - 2])
            p.inc_after("gpsimd", par, 16,
                        lambda e, nt=nt, buf=buf: e.dma_start(out=XSB[:, buf:buf + D], in_=x_t[nt]))
            p.wait("tensor", par)
            if nt == 0:
                p.wait("tensor", "gld")
            p.wait("tensor", "ve")
            fns = [
                (lambda e, nt=nt, dc=dc, buf=buf: e.transpose(
                    ps_tr[:, dc * 128:(dc + 1) * 128],
                    XSB[:, buf + dc * 128: buf + (dc + 1) * 128], IDN[:, :]))
                for dc in range(ND)]
            for i, fn in enumerate(fns):
                if i == ND - 1:
                    p.inc_after("tensor", "te", 1, fn)
                else:
                    p.t(fn)
            te_nt.append(p.now("te"))
            p.wait("vector", "te")
            xf_view = XF[:, :].rearrange("q (nd t) -> q nd t", t=Tl)
            p.inc_after("vector", "ve", 1,
                        lambda e, nt=nt, xf_view=xf_view: e.tensor_copy(
                            xf_view[:, :, nt * 128:(nt + 1) * 128],
                            ps_tr[:, :].rearrange("q (nd f) -> q nd f", f=128)))
        ve_xf = p.now("ve")
        p.wait("tensor", "ve", ve_xf)
        p.wait("scalar", "cld", cld_c)

        for h in range(NH):
            for tc in range(Tl // 512):
                if h > 0 or tc > 0:
                    p.wait("tensor", "se")
                p.inc_after("tensor", "te", 1,
                            lambda e, h=h, tc=tc: e.matmul(
                                ps_a[:, :], WK[:, h * 128:(h + 1) * 128],
                                XF[:, h * Tl + tc * 512: h * Tl + tc * 512 + 512],
                                start=True, stop=True))
                p.wait("scalar", "te")
                p.inc_after("scalar", "se", 1,
                            lambda e, tc=tc: e.activation(KF[:, tc * 512:(tc + 1) * 512],
                                                          ps_a[:, :], AF.Copy))
            for kt in range(NTl):
                p.wait("tensor", "se")
                p.inc_after("tensor", "te", 1,
                            lambda e, h=h, kt=kt: e.matmul(
                                ps_a[:, 0:128], XF[:, h * Tl + kt * 128: h * Tl + kt * 128 + 128],
                                WV[:, h * 128:(h + 1) * 128], start=True, stop=True))
                p.wait("scalar", "te")
                p.inc_after("scalar", "se", 1,
                            lambda e, kt=kt: e.activation(VT[:, kt * 128:(kt + 1) * 128],
                                                          ps_a[:, 0:128], AF.Copy))
            se_kv = p.now("se")
            p.wait("tensor", "se", se_kv)

            for qs in range(NQS):
                p.wait("tensor", "ve")
                p.inc_after("tensor", "te", 1,
                            lambda e, h=h, qs=qs: e.matmul(
                                ps_a[:, :], WQ[:, h * 128:(h + 1) * 128],
                                XF[:, h * Tl + qs * 512: h * Tl + qs * 512 + 512],
                                start=True, stop=True))
                p.wait("vector", "te")
                p.inc_after("vector", "ve", 1,
                            lambda e, h=h: e.tensor_tensor(QS_[:, :], ps_a[:, :],
                                                           RAMP[:, h * 512:(h + 1) * 512],
                                                           ALU.mult))
                ve_q = p.now("ve")
                p.wait("tensor", "ve", ve_q)
                kt_lo = max(0, qs * 4 - mm_max[h])
                kts = list(range(kt_lo, min(NTl, qs * 4 + 4)))
                for i, kt in enumerate(kts):
                    p.wait("tensor", "se")
                    p.inc_after("tensor", "te", 1,
                                lambda e, kt=kt: e.matmul(
                                    ps_sc[:, :], KF[:, kt * 128:(kt + 1) * 128], QS_[:, :],
                                    start=True, stop=True))
                    te_s = p.now("te")
                    p.wait("scalar", "te", te_s)
                    mm = qs * 4 - kt
                    p.wait("scalar", "ve")
                    p.inc_after("scalar", "se", 1,
                                lambda e, h=h, mm=mm: e.activation(
                                    PBJ[:, :], ps_sc[:, :], AF.Copy,
                                    scale=GP[:, h * 36 + mm + 3: h * 36 + mm + 4]))
                    se_e = p.now("se")
                    if mm <= 0:
                        mt = -mm
                        p.wait("vector", "se", se_e)
                        if mt > 0:
                            p.inc_after("vector", "ve", 1,
                                        lambda e, mt=mt: e.memset(PBJ[:, 0:128 * mt], 0.0))
                        p.inc_after("vector", "ve", 1,
                                    lambda e, mt=mt: e.tensor_tensor(
                                        PBJ[:, 128 * mt:128 * mt + 128],
                                        PBJ[:, 128 * mt:128 * mt + 128], MASK[:, :], ALU.mult))
                        p.wait("tensor", "ve")
                    else:
                        p.wait("tensor", "se", se_e)
                    fn = (lambda e, kt=kt, i=i, last=(i == len(kts) - 1): e.matmul(
                        ps_ret[:, :], VT[:, kt * 128:(kt + 1) * 128], PBJ[:, :],
                        start=(i == 0), stop=last))
                    if i == len(kts) - 1:
                        p.inc_after("tensor", "te", 1, fn)
                    else:
                        p.t(fn)
                te_pv = p.now("te")
                p.wait("scalar", "te", te_pv)
                p.inc_after("scalar", "se", 1,
                            lambda e: e.activation(RET_s[:, :], ps_ret[:, :], AF.Copy))
                p.wait("scalar", "se")
                p.inc_after("scalar", "se", 1,
                            lambda e: e.activation(SQ[:, :], RET_s[:, :], AF.Square))
                se_r = p.now("se")
                p.wait("tensor", "se", se_r)
                p.inc_after("tensor", "te", 1,
                            lambda e: e.matmul(ps_sc[0:1, :], ONEC[:, 0:1], RET_s[:, :],
                                               start=True, stop=True))
                p.inc_after("tensor", "te", 1,
                            lambda e: e.matmul(ps_rb[0:1, :], ONEC[:, 0:1], SQ[:, :],
                                               start=True, stop=True))
                te_st = p.now("te")
                p.wait("vector", "te", te_st)
                p.inc_after("vector", "ve", 1,
                            lambda e: e.tensor_scalar_mul(MU, ps_sc[0:1, :], 1.0 / 128))
                p.wait("vector", "ve")
                p.inc_after("vector", "ve", 1,
                            lambda e: e.tensor_scalar_mul(RSTD, ps_rb[0:1, :], 1.0 / 128))
                p.wait("vector", "ve")
                p.inc_after("vector", "ve", 1,
                            lambda e: e.tensor_tensor(ps_sc[0:1, :], MU[0:1, :], MU[0:1, :],
                                                      ALU.mult) if False else
                            e.tensor_mul(RSC[0:1, 0:512], MU, MU))
                p.wait("vector", "ve")
                p.inc_after("vector", "ve", 1,
                            lambda e: e.tensor_sub(RSTD, RSTD, RSC[0:1, 0:512]))
                p.wait("scalar", "ve")
                p.inc_after("scalar", "se", 1,
                            lambda e: e.activation(RSTD, RSTD, AF.Sqrt,
                                                   bias=stats["eps"][0:1, 0:1]))
                p.wait("vector", "se")
                p.inc_after("vector", "ve", 1,
                            lambda e: e.reciprocal(RSTD, RSTD))
                ve_st = p.now("ve")
                p.wait("tensor", "ve", ve_st)
                p.inc_after("tensor", "te", 1,
                            lambda e: e.matmul(ps_sc[:, :], ONE1[0:1, :], MU,
                                               start=True, stop=True))
                p.inc_after("tensor", "te", 1,
                            lambda e: e.matmul(ps_rb[:, :], ONE1[0:1, :], RSTD,
                                               start=True, stop=True))
                te_bc = p.now("te")
                p.wait("vector", "te", te_bc)
                p.inc_after("vector", "ve", 1,
                            lambda e: e.tensor_tensor(RSC[:, :], RET_s[:, :], ps_sc[:, :],
                                                      ALU.subtract))
                p.wait("vector", "ve")
                p.inc_after("vector", "ve", 1,
                            lambda e: e.tensor_tensor(RSC[:, :], RSC[:, :], ps_rb[:, :],
                                                      ALU.mult))
                p.wait("vector", "ve")
                p.inc_after("vector", "ve", 1,
                            lambda e, h=h, qs=qs: e.tensor_scalar(
                                RETF[:, h * Tl + qs * 512: h * Tl + qs * 512 + 512],
                                RSC[:, :], GNW[:, h:h + 1], GNB[:, h:h + 1],
                                ALU.mult, ALU.add))
                p.wait("tensor", "ve")
        ve_ret = p.now("ve")

        p.wait("gpsimd", "te")
        p.inc_after("gpsimd", "gw1", 16,
                    lambda e: e.dma_start(out=W1S[0:128, 0:ND * D].rearrange("q (dc f) -> q dc f", f=D),
                                          in_=w1.rearrange("(dc q) f -> q dc f", q=128)))
        gw1 = p.now("gw1")
        p.wait("tensor", "gw1", gw1)

        p.wait("tensor", "ve", ve_ret)
        for ts_i in range(NQS):
            for fc in range(ND):
                if ts_i > 0 or fc > 0:
                    p.wait("tensor", "ve")
                    p.wait("tensor", "se")
                fns = [
                    (lambda e, ts_i=ts_i, fc=fc, dc=dc: e.matmul(
                        ps_a[:, :], W1S[:, dc * D + fc * 128: dc * D + fc * 128 + 128],
                        XF[:, dc * Tl + ts_i * 512: dc * Tl + ts_i * 512 + 512],
                        start=(dc == 0), stop=(dc == ND - 1)))
                    for dc in range(ND)]
                for i, fn in enumerate(fns):
                    if i == ND - 1:
                        p.inc_after("tensor", "te", 1, fn)
                    else:
                        p.t(fn)
                p.wait("scalar", "te")
                p.wait("scalar", "ve")
                p.inc_after("scalar", "se", 1,
                            lambda e: e.activation(SG[:, :], ps_a[:, :], AF.Silu))
                p.wait("vector", "se")
                p.inc_after("vector", "ve", 1,
                            lambda e, ts_i=ts_i, fc=fc: e.tensor_tensor(
                                S2F[:, fc * 512:(fc + 1) * 512], SG[:, :],
                                RETF[:, fc * Tl + ts_i * 512: fc * Tl + ts_i * 512 + 512],
                                ALU.add))
            ve_s2 = p.now("ve")
            p.wait("tensor", "ve", ve_s2)
            for tt in range(4):
                nt = ts_i * 4 + tt
                p.wait("sync", "ve")
                p.inc_after("sync", "ld", 16,
                            lambda e, nt=nt: e.dma_start(out=S32[:, 0:D], in_=x_t[nt]))
                ld_nt = p.now("ld")
                if nt > 0:
                    p.wait("tensor", "ve")
                for half in range(2):
                    fns = [
                        (lambda e, tt=tt, half=half, fc=fc: e.matmul(
                            ps_po[:, half * 512:(half + 1) * 512],
                            S2F[:, fc * 512 + tt * 128: fc * 512 + tt * 128 + 128],
                            W2S[:, fc * D + half * 512: fc * D + half * 512 + 512],
                            start=(fc == 0), stop=(fc == ND - 1)))
                        for fc in range(ND)]
                    for i, fn in enumerate(fns):
                        if i == ND - 1:
                            p.inc_after("tensor", "te", 1, fn)
                        else:
                            p.t(fn)
                te_po = p.now("te")
                p.wait("vector", "te", te_po)
                p.wait("vector", "ld", ld_nt)
                p.inc_after("vector", "ve", 1,
                            lambda e: e.tensor_tensor(S32[:, 0:D], S32[:, 0:D],
                                                      ps_po[:, :], ALU.add))
                p.wait("scalar", "st")
                p.wait("vector", "st")
                emit_ln_tile(p, S32[:, 0:D], OUT[:, 0:D], stats, sq_scr=OUT[:, 0:D])
                ve_out = p.now("ve")
                p.wait("sync", "ve", ve_out)
                p.inc_after("sync", "st", 16,
                            lambda e, nt=nt: e.dma_start(out=yo_t[nt], in_=OUT[:, 0:D]))

        p.wait("sync", "st")
        p.barrier()
    if standalone:
        p.run()
        _octx.__exit__(None, None, None)
    return nc



def build_l2_ffn(T_=None, p=None, dram=None):
    Tl = T_ or T
    NTl = Tl // 128
    H = 4096
    NHC = H // 128
    TS = 256
    NS = Tl // TS
    TPS = TS // 128
    REPS = 1e-8

    standalone = p is None
    if standalone:
        _nc = bass.Bass("TRN2", target_bir_lowering=False)
        _octx = ExitStack()
        _octx.__enter__()
        p = Prog(_nc, _octx)
        p.pfx = ""
        dram = lambda name, shape, dt, kind: _nc.dram_tensor(name, shape, dt, kind=kind).ap()
    nc = p.nc
    x = dram("x", [Tl, D], F32, "ExternalInput")
    wupT = dram("wupT", [D, H], F32, "ExternalInput")
    wdnT = dram("wdnT", [H, D], F32, "ExternalInput")
    ident = dram("c_ident", [128, 128], F32, "ExternalInput")
    yo = dram("y", [Tl, D], F32, "ExternalOutput")

    x_t = x.rearrange("(nt p) d -> nt p d", p=128)
    yo_t = yo.rearrange("(nt p) d -> nt p d", p=128)

    with ExitStack() as ctx:
        sbt = lambda n, *a: ctx.enter_context(nc.sbuf_tensor(p.pfx + n, *a))
        pst = lambda n, *a: ctx.enter_context(nc.psum_tensor(p.pfx + n, *a))

        IDN = sbt("IDN", [128, 128], BF16)
        WUQ = sbt("WUQ", [128, ND * H], BF16)
        WDQ = sbt("WDQ", [128, NHC * D], BF16)
        WSTG = sbt("WSTG2", [128, 4096], F32)
        ABSS = sbt("ABSS", [128, 2], F32)
        TH = sbt("TH2", [128, 2], F32)
        SGN = sbt("SGN2", [128, 4096], BF16)
        AB2 = sbt("AB2f", [128, 4096], F32)
        RED = sbt("RED2", [128, 1], F32)
        XNT = sbt("XNT2", [128, TPS * D], BF16)
        XS32 = sbt("XS32f2", [128, TPS * D], F32)
        XNF = [sbt(f"XNF{i}", [128, ND * TS], BF16) for i in range(2)]
        HS = sbt("HS2", [128, TS], BF16)
        SQH = sbt("SQH2", [128, TS], BF16)
        RSTDC = sbt("RSTDC", [128, TPS], F32)
        ONEC = sbt("ONECf2", [128, 1], BF16)
        ONE1 = sbt("ONE1f2", [1, 128], F32)
        OUT = sbt("OUT2f", [128, D], F32)
        RROW = sbt("RROW", [1, TS], F32)
        stats = alloc_stats(sbt)

        ps_tr = pst("ps_tr", [128, ND * 128], BF16)
        ps_u = pst("ps_u", [128, TS], F32)
        ps_sq = pst("ps_sq", [1, TS], F32)
        ps_d = [pst(f"ps_d{i}", [128, D], F32) for i in range(TPS)]
        ps_al = pst("ps_al", [128, 8], F32)

        p.inc_after("vector", "ve", 1, lambda e: e.memset(stats["eps"][:, 0:1], LN_EPS))
        p.inc_after("vector", "ve", 1, lambda e: e.memset(ONEC[:, :], 1.0))
        p.inc_after("vector", "ve", 1, lambda e: e.memset(ONE1[:, :], 1.0))
        p.inc_after("vector", "ve", 1, lambda e: e.memset(ABSS[:, :], 0.0))
        p.inc_after("gpsimd", "gld", 16, lambda e: e.dma_start(out=IDN[:, :], in_=ident))

        wup_r = wupT.rearrange("(rc q) f -> rc q f", q=128)
        wdn_r = wdnT.rearrange("(rc q) f -> rc q f", q=128)
        for wi_, (wr, nrc, ncol) in enumerate(((wup_r, ND, H), (wdn_r, NHC, D))):
            for rc in range(nrc):
                p.wait("sync", "ve")
                p.inc_after("sync", "wld", 16,
                            lambda e, wr=wr, rc=rc, ncol=ncol: e.dma_start(
                                out=WSTG[:, 0:ncol], in_=wr[rc]))
                p.wait("vector", "wld")
                p.inc_after("vector", "ve", 1,
                            lambda e, ncol=ncol: e.reduce_sum(RED[:, 0:1], WSTG[:, 0:ncol],
                                                              axis=AX.X,
                                                              apply_absolute_value=True))
                p.wait("vector", "ve")
                p.inc_after("vector", "ve", 1,
                            lambda e, wi_=wi_: e.tensor_add(ABSS[:, wi_:wi_ + 1],
                                                            ABSS[:, wi_:wi_ + 1],
                                                            RED[:, 0:1]))
        ve_abs = p.now("ve")
        ONECF = sbt("ONECF", [128, 1], F32)
        THR = sbt("THR2", [1, 2], F32)
        p.inc_after("vector", "ve", 1, lambda e: e.memset(ONECF[:, :], 1.0))
        p.wait("tensor", "ve", p.now("ve"))
        for wi_ in range(2):
            p.inc_after("tensor", "te", 1,
                        lambda e, wi_=wi_: e.matmul(ps_al[0:1, wi_:wi_ + 1], ONECF[:, 0:1],
                                                    ABSS[:, wi_:wi_ + 1],
                                                    start=True, stop=True))
        te_al = p.now("te")
        p.wait("vector", "te", te_al)
        p.inc_after("vector", "ve", 1,
                    lambda e: e.tensor_scalar_mul(THR[0:1, 0:2], ps_al[0:1, 0:2],
                                                  0.5 / (D * H)))
        p.wait("tensor", "ve")
        for wi_ in range(2):
            p.inc_after("tensor", "te", 1,
                        lambda e, wi_=wi_: e.matmul(ps_al[:, 4 + wi_:5 + wi_], ONE1[0:1, :],
                                                    THR[0:1, wi_:wi_ + 1],
                                                    start=True, stop=True))
        te_th = p.now("te")
        p.wait("vector", "te", te_th)
        p.inc_after("vector", "ve", 1,
                    lambda e: e.tensor_copy(TH[:, 0:2], ps_al[:, 4:6]))
        ve_th = p.now("ve")

        p.wait("sync", "ve", ve_th)
        for wi_, (wr, nrc, ncol, dest) in enumerate(((wup_r, ND, H, WUQ), (wdn_r, NHC, D, WDQ))):
            for rc in range(nrc):
                p.wait("sync", "se")
                p.wait("sync", "ve")
                p.inc_after("sync", "wld", 16,
                            lambda e, wr=wr, rc=rc, ncol=ncol: e.dma_start(
                                out=WSTG[:, 0:ncol], in_=wr[rc]))
                p.wait("scalar", "wld")
                p.inc_after("scalar", "se", 1,
                            lambda e, ncol=ncol: e.activation(SGN[:, 0:ncol], WSTG[:, 0:ncol],
                                                              AF.Sign))
                p.wait("vector", "wld")
                p.inc_after("vector", "ve", 1,
                            lambda e, ncol=ncol: e.scalar_tensor_tensor(
                                AB2[:, 0:ncol], WSTG[:, 0:ncol], -1.0, WSTG[:, 0:ncol],
                                ALU.mult, ALU.max))
                p.wait("vector", "ve")
                p.inc_after("vector", "ve", 1,
                            lambda e, wi_=wi_, ncol=ncol: e.tensor_scalar(
                                AB2[:, 0:ncol], AB2[:, 0:ncol], TH[:, wi_:wi_ + 1], None,
                                ALU.is_gt))
                p.wait("vector", "se")
                p.wait("vector", "ve")
                if wi_ == 0:
                    out_ap = dest[:, :].rearrange("q (dc f) -> q dc f", f=H)[:, rc, :]
                else:
                    out_ap = dest[:, rc * D:(rc + 1) * D]
                p.inc_after("vector", "ve", 1,
                            lambda e, out_ap=out_ap, ncol=ncol: e.tensor_tensor(
                                out_ap, SGN[:, 0:ncol], AB2[:, 0:ncol], ALU.mult))
        ve_w = p.now("ve")
        p.wait("tensor", "ve", ve_w)

        wuq_view = WUQ[:, :].rearrange("q (dc hc f) -> q dc hc f", hc=NHC, f=128)
        for s in range(NS):
            xnf = XNF[s % 2]
            if s > 0:
                p.wait("sync", "ve")
            for tt in range(TPS):
                p.inc_after("sync", "ld", 16,
                            lambda e, s=s, tt=tt: e.dma_start(out=XS32[:, tt * D:(tt + 1) * D],
                                                              in_=x_t[s * TPS + tt]))
            ld_s = p.now("ld")
            for tt in range(TPS):
                ssq, rstd = stats["ssq"], stats["rstd"]
                p.wait("scalar", "ld", ld_s)
                p.wait("scalar", "st")
                p.wait("scalar", "ve")
                p.inc_after("scalar", "se", 1,
                            lambda e, tt=tt: e.activation(OUT[:, 0:D],
                                                          XS32[:, tt * D:(tt + 1) * D],
                                                          AF.Square, accum_out=ssq[:, 0:1]))
                p.wait("scalar", "se")
                p.inc_after("scalar", "se", 1,
                            lambda e: e.activation(rstd[:, 0:1], ssq[:, 0:1], AF.Sqrt,
                                                   scale=1.0 / D))
                p.wait("vector", "se")
                p.inc_after("vector", "ve", 1,
                            lambda e: e.tensor_scalar_add(rstd[:, 0:1], rstd[:, 0:1], REPS))
                p.wait("vector", "ve")
                p.inc_after("vector", "ve", 1,
                            lambda e: e.reciprocal(rstd[:, 0:1], rstd[:, 0:1]))
                p.wait("vector", "ve")
                p.inc_after("vector", "ve", 1,
                            lambda e, tt=tt: e.tensor_scalar_mul(
                                XNT[:, tt * D:(tt + 1) * D], XS32[:, tt * D:(tt + 1) * D],
                                rstd[:, 0:1]))
            ve_xn = p.now("ve")
            p.wait("tensor", "ve", ve_xn)
            p.wait("tensor", "gld")
            xnf_view = xnf[:, :].rearrange("q (nd t) -> q nd t", t=TS)
            for tt in range(TPS):
                p.wait("tensor", "ve")
                fns = [
                    (lambda e, tt=tt, dc=dc: e.transpose(
                        ps_tr[:, dc * 128:(dc + 1) * 128],
                        XNT[:, tt * D + dc * 128: tt * D + (dc + 1) * 128], IDN[:, :]))
                    for dc in range(ND)]
                for i, fn in enumerate(fns):
                    if i == ND - 1:
                        p.inc_after("tensor", "te", 1, fn)
                    else:
                        p.t(fn)
                p.wait("vector", "te")
                p.inc_after("vector", "ve", 1,
                            lambda e, tt=tt, xnf_view=xnf_view: e.tensor_copy(
                                xnf_view[:, :, tt * 128:(tt + 1) * 128],
                                ps_tr[:, :].rearrange("q (nd f) -> q nd f", f=128)))
            ve_xf = p.now("ve")
            p.wait("tensor", "ve", ve_xf)

            for hc in range(NHC):
                p.wait("tensor", "se")
                fns = [
                    (lambda e, hc=hc, dc=dc, xnf=xnf: e.matmul(
                        ps_u[:, :], wuq_view[:, dc, hc, :], xnf[:, dc * TS:(dc + 1) * TS],
                        start=(dc == 0), stop=(dc == ND - 1)))
                    for dc in range(ND)]
                for i, fn in enumerate(fns):
                    if i == ND - 1:
                        p.inc_after("tensor", "te", 1, fn)
                    else:
                        p.t(fn)
                te_u = p.now("te")
                p.wait("scalar", "te", te_u)
                p.wait("scalar", "te")
                p.inc_after("scalar", "se", 1,
                            lambda e: e.activation(HS[:, :], ps_u[:, :], AF.Gelu))
                p.wait("scalar", "se")
                p.inc_after("scalar", "se", 1,
                            lambda e: e.activation(SQH[:, :], HS[:, :], AF.Square))
                se_h = p.now("se")
                p.wait("tensor", "se", se_h)
                p.inc_after("tensor", "te", 1,
                            lambda e, hc=hc: e.matmul(ps_sq[0:1, :], ONEC[:, 0:1], SQH[:, :],
                                                      start=(hc == 0), stop=(hc == NHC - 1)))
                for tt in range(TPS):
                    for half in range(2):
                        fn = lambda e, hc=hc, tt=tt, half=half: e.matmul(
                            ps_d[tt][:, half * 512:(half + 1) * 512],
                            HS[:, tt * 128:(tt + 1) * 128],
                            WDQ[:, hc * D + half * 512: hc * D + half * 512 + 512],
                            start=(hc == 0), stop=(hc == NHC - 1))
                        if tt == TPS - 1 and half == 1:
                            p.inc_after("tensor", "te", 1, fn)
                        else:
                            p.t(fn)
            te_dn = p.now("te")
            p.wait("vector", "te", te_dn)
            p.inc_after("vector", "ve", 1,
                        lambda e: e.tensor_scalar_mul(RROW[0:1, :], ps_sq[0:1, :], 1.0 / H))
            p.wait("scalar", "ve")
            p.inc_after("scalar", "se", 1,
                        lambda e: e.activation(RROW[0:1, :], RROW[0:1, :], AF.Sqrt))
            p.wait("vector", "se")
            p.inc_after("vector", "ve", 1,
                        lambda e: e.tensor_scalar_add(RROW[0:1, :], RROW[0:1, :], REPS))
            p.wait("vector", "ve")
            p.inc_after("vector", "ve", 1,
                        lambda e: e.reciprocal(RROW[0:1, :], RROW[0:1, :]))
            ve_rr = p.now("ve")
            p.wait("tensor", "ve", ve_rr)
            for tt in range(TPS):
                p.inc_after("tensor", "te", 1,
                            lambda e, tt=tt: e.matmul(
                                ps_u[:, tt:tt + 1],
                                RROW[0:1, tt * 128:(tt + 1) * 128], ONE1[0:1, 0:1],
                                start=True, stop=True))
            te_rt = p.now("te")
            p.wait("vector", "te", te_rt)
            p.inc_after("vector", "ve", 1,
                        lambda e: e.tensor_copy(RSTDC[:, 0:TPS], ps_u[:, 0:TPS]))
            ve_rt = p.now("ve")

            for tt in range(TPS):
                p.wait("vector", "te", te_dn)
                p.wait("vector", "ve", ve_rt)
                p.wait("vector", "ld", ld_s)
                p.inc_after("vector", "ve", 1,
                            lambda e, tt=tt: e.scalar_tensor_tensor(
                                XS32[:, tt * D:(tt + 1) * D], ps_d[tt][:, :],
                                RSTDC[:, tt:tt + 1], XS32[:, tt * D:(tt + 1) * D],
                                ALU.mult, ALU.add))
                p.wait("scalar", "st")
                p.wait("vector", "st")
                emit_ln_tile(p, XS32[:, tt * D:(tt + 1) * D], OUT[:, 0:D], stats,
                             sq_scr=OUT[:, 0:D])
                ve_out = p.now("ve")
                p.wait("sync", "ve", ve_out)
                p.inc_after("sync", "st", 16,
                            lambda e, s=s, tt=tt: e.dma_start(out=yo_t[s * TPS + tt],
                                                              in_=OUT[:, 0:D]))

        p.wait("sync", "st")
        p.barrier()
    if standalone:
        p.run()
        _octx.__exit__(None, None, None)
    return nc



def build_l3_ffn(T_=None, p=None, dram=None):
    Tl = T_ or T
    NTl = Tl // 128
    NQS = Tl // 512

    standalone = p is None
    if standalone:
        _nc = bass.Bass("TRN2", target_bir_lowering=False)
        _octx = ExitStack()
        _octx.__enter__()
        p = Prog(_nc, _octx)
        p.pfx = ""
        dram = lambda name, shape, dt, kind: _nc.dram_tensor(name, shape, dt, kind=kind).ap()
    nc = p.nc
    x = dram("x", [Tl, D], F32, "ExternalInput")
    wrT = dram("wrT", [D, D], F32, "ExternalInput")
    wkT = dram("wkT", [D, D], F32, "ExternalInput")
    wvT = dram("wvT", [D, D], F32, "ExternalInput")
    mur = dram("mur", [128, ND], F32, "ExternalInput")
    muk = dram("muk", [128, ND], F32, "ExternalInput")
    ident = dram("c_ident", [128, 128], F32, "ExternalInput")
    yo = dram("y", [Tl, D], F32, "ExternalOutput")

    x_t = x.rearrange("(nt p) d -> nt p d", p=128)
    yo_t = yo.rearrange("(nt p) d -> nt p d", p=128)

    with ExitStack() as ctx:
        sbt = lambda n, *a: ctx.enter_context(nc.sbuf_tensor(p.pfx + n, *a))
        pst = lambda n, *a: ctx.enter_context(nc.psum_tensor(p.pfx + n, *a))

        IDN = sbt("IDN", [128, 128], BF16)
        XF = sbt("XF3", [128, ND * Tl], BF16)
        WRT = sbt("WRT", [128, ND * D], BF16)
        WKT = sbt("WKT", [128, ND * D], BF16)
        WVT = sbt("WVT", [128, ND * D], BF16)
        MUR = sbt("MUR", [128, ND], F32)
        MUK = sbt("MUK", [128, ND], F32)
        OMR = sbt("OMR", [128, ND], F32)
        OMK = sbt("OMK", [128, ND], F32)
        XRFb = [sbt(f"XRF{i}", [128, ND * 512], BF16) for i in range(2)]
        XKFb = [sbt(f"XKF{i}", [128, ND * 512], BF16) for i in range(2)]
        KA = sbt("KA", [128, ND * 512], BF16)
        RT = sbt("RT", [128, 4 * D], BF16)
        RSC = sbt("RSC3", [128, 512], F32)
        KSC = sbt("KSC3", [128, 512], F32)
        SHARE = sbt("SHARE3", [128, 4 * D], BF16)
        XSB = SHARE[:, 0:2 * D]
        S32 = SHARE[:, :].bitcast(F32)
        OUT = sbt("OUT3", [128, D], F32)
        stats = alloc_stats(sbt)

        ps_tr = pst("ps_tr", [128, ND * 128], BF16)
        ps_a = pst("ps_a", [128, 512], F32)
        ps_po = pst("ps_po", [128, D], F32)

        p.inc_after("vector", "ve", 1, lambda e: e.memset(stats["eps"][:, 0:1], LN_EPS))

        p.inc_after("gpsimd", "gld", 16, lambda e: e.dma_start(out=IDN[:, :], in_=ident))
        p.inc_after("gpsimd", "gld", 16,
                    lambda e: e.dma_start(out=WRT[:, :].rearrange("q (dc f) -> q dc f", f=D),
                                          in_=wrT.rearrange("(dc q) f -> q dc f", q=128)))
        p.inc_after("gpsimd", "gld", 16,
                    lambda e: e.dma_start(out=WKT[:, :].rearrange("q (dc f) -> q dc f", f=D),
                                          in_=wkT.rearrange("(dc q) f -> q dc f", q=128)))
        p.inc_after("gpsimd", "gld", 16,
                    lambda e: e.dma_start(out=WVT[:, :].rearrange("q (dc f) -> q dc f", f=D),
                                          in_=wvT.rearrange("(dc q) f -> q dc f", q=128)))
        p.inc_after("sync", "cld", 16, lambda e: e.dma_start(out=MUR[:, :], in_=mur))
        p.inc_after("sync", "cld", 16, lambda e: e.dma_start(out=MUK[:, :], in_=muk))
        cld_c = p.now("cld")
        p.wait("vector", "cld", cld_c)
        p.inc_after("vector", "ve", 1, lambda e: e.tensor_scalar(OMR[:, :], MUR[:, :], -1.0, 1.0,
                                                                 ALU.mult, ALU.add))
        p.inc_after("vector", "ve", 1, lambda e: e.tensor_scalar(OMK[:, :], MUK[:, :], -1.0, 1.0,
                                                                 ALU.mult, ALU.add))

        te_nt = []
        for nt in range(NTl):
            buf = (nt % 2) * D
            par = f"gx{nt % 2}"
            if nt >= 2:
                p.wait("gpsimd", "te", te_nt[nt - 2])
            p.inc_after("gpsimd", par, 16,
                        lambda e, nt=nt, buf=buf: e.dma_start(out=XSB[:, buf:buf + D], in_=x_t[nt]))
            p.wait("tensor", par)
            if nt == 0:
                p.wait("tensor", "gld")
            p.wait("tensor", "ve")
            fns = [
                (lambda e, nt=nt, dc=dc, buf=buf: e.transpose(
                    ps_tr[:, dc * 128:(dc + 1) * 128],
                    XSB[:, buf + dc * 128: buf + (dc + 1) * 128], IDN[:, :]))
                for dc in range(ND)]
            for i, fn in enumerate(fns):
                if i == ND - 1:
                    p.inc_after("tensor", "te", 1, fn)
                else:
                    p.t(fn)
            te_nt.append(p.now("te"))
            p.wait("vector", "te")
            xf_view = XF[:, :].rearrange("q (nd t) -> q nd t", t=Tl)
            p.inc_after("vector", "ve", 1,
                        lambda e, nt=nt, xf_view=xf_view: e.tensor_copy(
                            xf_view[:, :, nt * 128:(nt + 1) * 128],
                            ps_tr[:, :].rearrange("q (nd f) -> q nd f", f=128)))
        ve_xf = p.now("ve")
        p.wait("tensor", "ve", ve_xf)

        for qs in range(NQS):
            t0 = qs * 512
            XRF = XRFb[qs % 2]
            XKF = XKFb[qs % 2]
            p.wait("vector", "ve", ve_xf)
            p.wait("vector", "te")
            for dc in range(ND):
                for (XT, MUc, OMc) in ((XRF, MUR, OMR), (XKF, MUK, OMK)):
                    p.inc_after("vector", "ve", 1,
                                lambda e, dc=dc, XT=XT, MUc=MUc, t0=t0: e.tensor_scalar(
                                    RSC[:, :], XF[:, dc * Tl + t0: dc * Tl + t0 + 512],
                                    MUc[:, dc:dc + 1], None, ALU.mult))
                    p.wait("vector", "ve")
                    if t0 == 0:
                        p.inc_after("vector", "ve", 1,
                                    lambda e, dc=dc, XT=XT, OMc=OMc: e.scalar_tensor_tensor(
                                        XT[:, dc * 512 + 1: dc * 512 + 512],
                                        XF[:, dc * Tl: dc * Tl + 511],
                                        OMc[:, dc:dc + 1], RSC[:, 1:512],
                                        ALU.mult, ALU.add))
                        p.wait("vector", "ve")
                        p.inc_after("vector", "ve", 1,
                                    lambda e, dc=dc, XT=XT: e.tensor_copy(
                                        XT[:, dc * 512: dc * 512 + 1], RSC[:, 0:1]))
                    else:
                        p.inc_after("vector", "ve", 1,
                                    lambda e, dc=dc, XT=XT, OMc=OMc, t0=t0: e.scalar_tensor_tensor(
                                        XT[:, dc * 512: dc * 512 + 512],
                                        XF[:, dc * Tl + t0 - 1: dc * Tl + t0 + 511],
                                        OMc[:, dc:dc + 1], RSC[:, :],
                                        ALU.mult, ALU.add))
                    p.wait("vector", "ve")
            ve_mix = p.now("ve")
            p.wait("tensor", "ve", ve_mix)

            for tt in range(4):
                for half in range(2):
                    if qs > 0 or tt > 0 or half > 0:
                        p.wait("tensor", "se")
                    fns = [
                        (lambda e, tt=tt, half=half, dc=dc, XRF=XRF: e.matmul(
                            ps_a[:, :],
                            XRF[:, dc * 512 + tt * 128: dc * 512 + tt * 128 + 128],
                            WRT[:, dc * D + half * 512: dc * D + half * 512 + 512],
                            start=(dc == 0), stop=(dc == ND - 1)))
                        for dc in range(ND)]
                    for i, fn in enumerate(fns):
                        if i == ND - 1:
                            p.inc_after("tensor", "te", 1, fn)
                        else:
                            p.t(fn)
                    p.wait("scalar", "te")
                    p.inc_after("scalar", "se", 1,
                                lambda e, tt=tt, half=half: e.activation(
                                    RT[:, tt * D + half * 512: tt * D + half * 512 + 512],
                                    ps_a[:, :], AF.Sigmoid))
            for fc in range(ND):
                p.wait("tensor", "ve")
                p.wait("tensor", "se")
                fns = [
                    (lambda e, fc=fc, dc=dc, XKF=XKF: e.matmul(
                        ps_a[:, :], WKT[:, dc * D + fc * 128: dc * D + fc * 128 + 128],
                        XKF[:, dc * 512:(dc + 1) * 512],
                        start=(dc == 0), stop=(dc == ND - 1)))
                    for dc in range(ND)]
                for i, fn in enumerate(fns):
                    if i == ND - 1:
                        p.inc_after("tensor", "te", 1, fn)
                    else:
                        p.t(fn)
                p.wait("vector", "te")
                p.inc_after("vector", "ve", 1,
                            lambda e: e.tensor_scalar_max(KSC[:, :], ps_a[:, :], 0.0))
                p.wait("scalar", "ve")
                p.inc_after("scalar", "se", 1,
                            lambda e, fc=fc: e.activation(KA[:, fc * 512:(fc + 1) * 512],
                                                          KSC[:, :], AF.Square))
            se_ka = p.now("se")
            p.wait("tensor", "se", se_ka)

            for tt in range(4):
                nt = qs * 4 + tt
                p.wait("sync", "ve")
                p.inc_after("sync", "ld", 16,
                            lambda e, nt=nt: e.dma_start(out=S32[:, 0:D], in_=x_t[nt]))
                ld_nt = p.now("ld")
                if nt > 0:
                    p.wait("tensor", "ve")
                for half in range(2):
                    fns = [
                        (lambda e, tt=tt, half=half, fc=fc: e.matmul(
                            ps_po[:, half * 512:(half + 1) * 512],
                            KA[:, fc * 512 + tt * 128: fc * 512 + tt * 128 + 128],
                            WVT[:, fc * D + half * 512: fc * D + half * 512 + 512],
                            start=(fc == 0), stop=(fc == ND - 1)))
                        for fc in range(ND)]
                    for i, fn in enumerate(fns):
                        if i == ND - 1:
                            p.inc_after("tensor", "te", 1, fn)
                        else:
                            p.t(fn)
                te_po = p.now("te")
                p.wait("vector", "te", te_po)
                p.wait("vector", "ld", ld_nt)
                p.wait("vector", "se")
                p.wait("vector", "st")
                p.wait("scalar", "st")
                p.inc_after("vector", "ve", 1,
                            lambda e, tt=tt: e.tensor_tensor(OUT[:, 0:D], RT[:, tt * D:(tt + 1) * D],
                                                             ps_po[:, :], ALU.mult))
                p.wait("vector", "ve")
                p.inc_after("vector", "ve", 1,
                            lambda e: e.tensor_tensor(S32[:, 0:D], S32[:, 0:D], OUT[:, 0:D],
                                                      ALU.add))
                emit_ln_tile(p, S32[:, 0:D], OUT[:, 0:D], stats, sq_scr=OUT[:, 0:D])
                ve_out = p.now("ve")
                p.wait("sync", "ve", ve_out)
                p.inc_after("sync", "st", 16,
                            lambda e, nt=nt: e.dma_start(out=yo_t[nt], in_=OUT[:, 0:D]))

        p.wait("sync", "st")
        p.barrier()
    if standalone:
        p.run()
        _octx.__exit__(None, None, None)
    return nc



CHAIN = [
    ("m_", lambda **kw: build_mamba(**kw)),
    ("f0_", lambda **kw: build_ffn_int8(**kw)),
    ("a1_", lambda **kw: build_attn("l1", **kw)),
    ("f1_", lambda **kw: build_ffn_int8(**kw)),
    ("a2_", lambda **kw: build_l2_attn(**kw)),
    ("f2_", lambda **kw: build_l2_ffn(**kw)),
    ("a3_", lambda **kw: build_attn("l3", **kw)),
    ("f3_", lambda **kw: build_l3_ffn(**kw)),
]


def build_fused(T_=None):
    Tl = T_ or T
    nc = bass.Bass("TRN2", target_bir_lowering=False)
    octx = ExitStack()
    octx.__enter__()
    p = Prog(nc, octx)
    xs_dram = [nc.dram_tensor("x", [Tl, D], F32, kind="ExternalInput").ap()]
    for i in range(len(CHAIN) - 1):
        xs_dram.append(nc.dram_tensor(f"xi{i}", [Tl, D], F32, kind="Internal").ap())
    y_final = nc.dram_tensor("y", [Tl, D], F32, kind="ExternalOutput").ap()

    for k, (pfx, fn) in enumerate(CHAIN):
        p.pfx = pfx
        xin = xs_dram[k]
        yout = xs_dram[k + 1] if k < len(CHAIN) - 1 else y_final

        def dram(name, shape, dt, kind, pfx=pfx, xin=xin, yout=yout):
            if name == "x":
                return xin
            if name == "y":
                return yout
            return nc.dram_tensor(pfx + name, shape, dt, kind="ExternalInput").ap()

        fn(T_=Tl, p=p, dram=dram)
    p.run()
    octx.__exit__(None, None, None)
    return nc



def make_consts():
    return {"c_ident": np.eye(128, dtype=np.float32)}


def _rope_consts(Tl):
    theta = 1.0 / 10000.0 ** (np.arange(0, 64, 2, dtype=np.float32) / 64)
    COSC = np.zeros((128, 8), np.float32)
    SINC = np.zeros((128, 8), np.float32)
    for col in range(8):
        for p_ in range(128):
            r = col * 128 + p_
            h = r // 64
            j = (r % 64) // 2
            ang = np.float32(h) * theta[j]
            COSC[p_, col] = np.cos(ang)
            SINC[p_, col] = (-np.sin(ang)) if (r % 2 == 0) else np.sin(ang)
    perm = np.zeros((128, 128), np.float32)
    for i in range(0, 128, 2):
        perm[i + 1, i] = 1.0
        perm[i, i + 1] = 1.0
    mask = np.triu(np.ones((128, 128), np.float32))
    return dict(c_cosc=COSC, c_sinc=SINC, c_perm=perm, c_mask=mask)


def _l2_consts():
    NH = 8
    gammas = 1.0 - np.exp(np.linspace(np.log(1 / 512), np.log(1 / 32), NH))
    GP = np.zeros((128, NH * 36), np.float32)
    for h in range(NH):
        for mm in range(-3, 33):
            e = 128.0 * mm - np.arange(128)
            GP[:, h * 36 + mm + 3] = np.exp(e * np.log(gammas[h]))
    RAMP = np.zeros((128, NH * 512), np.float32)
    for h in range(NH):
        RAMP[:, h * 512:(h + 1) * 512] = (gammas[h] ** np.arange(512))[None, :]
    return dict(c_gp=GP, c_ramp=RAMP, c_mask=np.triu(np.ones((128, 128), np.float32)))


def _ehbc():
    ehb = np.zeros((32, 16 * 128), np.float32)
    ehc = np.zeros((32, 16 * 128), np.float32)
    for n in range(16):
        ehb[n, n * 128:(n + 1) * 128] = 1.0
        ehc[16 + n, n * 128:(n + 1) * 128] = 1.0
    return dict(c_ehb=ehb, c_ehc=ehc)


def _col8(v):
    return np.ascontiguousarray(v.reshape(8, 128).T)


def _col16(v):
    return np.ascontiguousarray(v.reshape(16, 128).T)


def _ct(a):
    return np.ascontiguousarray(a)



def kernel(**inputs):
    ii = inputs
    consts = make_consts()

    w_mamba = dict(
        lnw=_ct(ii["L0_attn_ln_w"][None]), lnb=_ct(ii["L0_attn_ln_b"][None]),
        wipT=_ct(ii["L0_attn_in_proj_w"].T),
        convw=_ct(ii["L0_attn_conv_w"].reshape(2048, 4).reshape(16, 128, 4)
                  .transpose(1, 0, 2).reshape(128, 64)),
        convb=_col16(ii["L0_attn_conv_b"]),
        wxpT=_ct(ii["L0_attn_x_proj_w"].T),
        wdtT=_ct(ii["L0_attn_dt_proj_w"].T),
        dtb=_col16(ii["L0_attn_dt_proj_b"]),
        alog=_ct(ii["L0_attn_A_log"].reshape(16, 128, 16).transpose(1, 0, 2)
                 .reshape(128, 256)),
        dmat=_col16(ii["L0_attn_D"]),
        wopT=_ct(ii["L0_attn_out_proj_w"].T),
        **_ehbc(), **consts)
    w_ffn0 = dict(wgT=_ct(ii["L0_ffn_w_gate"].T), wuT=_ct(ii["L0_ffn_w_up"].T),
                  wdT=_ct(ii["L0_ffn_w_down"].T), **consts)
    w_attn1 = dict(wqkv=_ct(ii["L1_attn_W_qkv"]), bqkv=_ct(ii["L1_attn_b_qkv"][None]),
                   bqkvc=_ct(ii["L1_attn_b_qkv"].reshape(24, 128).T),
                   wproj=_ct(ii["L1_attn_W_proj"]), bproj=_ct(ii["L1_attn_b_proj"][None]),
                   aqkv=_ct(ii["L1_attn_A_qkv"]), bl_qkv=_ct(ii["L1_attn_B_qkv"]),
                   aproj=_ct(ii["L1_attn_A_proj"]), bl_proj=_ct(ii["L1_attn_B_proj"]),
                   **consts)
    w_ffn1 = dict(wgT=_ct(ii["L1_ffn_w_gate"].T), wuT=_ct(ii["L1_ffn_w_up"].T),
                  wdT=_ct(ii["L1_ffn_w_down"].T), **consts)
    w_attn2 = dict(wq=_ct(ii["L2_attn_Wq"].transpose(1, 0, 2).reshape(128, 8 * 128)),
                   wk=_ct(ii["L2_attn_Wk"].transpose(1, 0, 2).reshape(128, 8 * 128)),
                   wv=_ct(ii["L2_attn_Wv"].transpose(1, 0, 2).reshape(128, 8 * 128)),
                   w1=_ct(ii["L2_attn_W1"]), w2=_ct(ii["L2_attn_W2"]),
                   gnw=_col8(ii["L2_attn_gn_w"]), gnb=_col8(ii["L2_attn_gn_b"]),
                   **_l2_consts(), **consts)
    w_ffn2 = dict(wupT=_ct(ii["L2_ffn_w_up"].T), wdnT=_ct(ii["L2_ffn_w_down"].T), **consts)
    w_attn3 = dict(wqkv=_ct(ii["L3_attn_W_attn"]), bqkv=_ct(ii["L3_attn_b_attn"][None]),
                   bqkvc=_ct(ii["L3_attn_b_attn"].reshape(24, 128).T),
                   wproj=_ct(ii["L3_attn_W_proj"]), bproj=_ct(ii["L3_attn_b_proj"][None]),
                   **_rope_consts(T), **consts)
    w_ffn3 = dict(wrT=_ct(ii["L3_ffn_W_r"].T), wkT=_ct(ii["L3_ffn_W_k"].T),
                  wvT=_ct(ii["L3_ffn_W_v"].T),
                  mur=_col8(ii["L3_ffn_mu_r"]), muk=_col8(ii["L3_ffn_mu_k"]),
                  **consts)

    wmaps = [w_mamba, w_ffn0, w_attn1, w_ffn1, w_attn2, w_ffn2, w_attn3, w_ffn3]
    weights = {}
    for (pfx, _), wm in zip(CHAIN, wmaps):
        for k, v in wm.items():
            weights[pfx + k] = v

    nc = build_fused()
    in_maps = [dict(x=np.ascontiguousarray(ii["L0_attn_x"][b]), **weights) for b in range(8)]
    res = run_bass_kernel_spmd(nc, in_maps, core_ids=list(range(8)))
    return np.stack([np.asarray(res.results[b]["y"]) for b in range(8)], axis=0)
